# version 01
# baseline (speedup 1.0000x reference)
import numpy as np
import ml_dtypes
from contextlib import ExitStack

import concourse.bass as bass
import concourse.mybir as mybir
from concourse.bass_utils import run_bass_kernel_spmd
from concourse.alu_op_type import AluOpType as Op

F32 = mybir.dt.float32
BF16 = mybir.dt.bfloat16
I32 = mybir.dt.int32
AF = mybir.ActivationFunctionType
AX = mybir.AxisListType.X

B, S, E, H, D, KN, INNER = 2, 2048, 1024, 16, 64, 4, 4096
NT = B * S
EPS_LN = 1e-5
SQRT_EPS_NW = 1e-4
LOG_RB = float(np.log(10000.0))
TWO_PI = float(2 * np.pi)
INV_2PI = float(1.0 / TWO_PI)
QOFF = 140.0

_DETECT_RACES = False
_STAGE = 99
_TAP_REQ = []

CONSTS = [0.0, 1.0, 0.5, 0.25, -0.5, 2.0, -1.0, float(np.pi / 2),
          127.0, 1e-5, float(1.0 / E), float(EPS_LN), float(np.pi)]
CIDX = {float(v): i for i, v in enumerate(CONSTS)}


class _Ctx:
    pass


def _mk_ctx():
    C = _Ctx()
    C.nc = bass.Bass("TRN2", target_bir_lowering=False,
                     detect_race_conditions=_DETECT_RACES)
    C.st = ExitStack()
    C.blocks = []
    C.taps = []
    nc, st = C.nc, C.st

    C.sb = lambda n, sh, dt: st.enter_context(nc.sbuf_tensor(n, sh, dt))
    C.ps = lambda n, sh: st.enter_context(nc.psum_tensor(n, sh, F32))
    C.dram = lambda n, sh, dt, k: nc.dram_tensor(n, sh, dt, kind=k).ap()

    C.sems, C.semv = {}, {}

    def mksem(n):
        C.sems[n] = st.enter_context(nc.semaphore(n))
        C.semv[n] = 0
    C.mksem = mksem
    for s_ in ["ld", "ld2", "st", "pe", "ve", "sc", "gp", "sy",
               "pe2", "ve2", "sc2", "gp2", "ve3", "sy2"]:
        mksem(s_)

    def inc(inst, name, n=1):
        inst.then_inc(C.sems[name], n)
        C.semv[name] += n
        return inst
    C.inc = inc
    C.dinc = lambda inst, name="ld": inc(inst, name, 16)

    def wait(eng, name, lag=0):
        v = C.semv[name] - lag
        if v > 0:
            eng.wait_ge(C.sems[name], v)
    C.wait = wait

    def block(**eng_fns):
        C.blocks.append(eng_fns)
    C.block = block
    return C


def build(tap_req=()):
    C = _mk_ctx()
    C.tap_req = list(tap_req)
    nc = C.nc
    dram, sb = C.dram, C.sb

    C.x_h = dram("x", [B, S, E], F32, "ExternalInput")
    C.wq_h = dram("Wq", [E, E], F32, "ExternalInput")
    C.wk_h = dram("Wk", [E, E], F32, "ExternalInput")
    C.wv_h = dram("Wv", [E, E], F32, "ExternalInput")
    C.wr_h = dram("Wr", [E, H * KN], F32, "ExternalInput")
    C.wo_h = dram("Wo", [E, E], F32, "ExternalInput")
    C.wg_h = dram("w_gate", [INNER, E], F32, "ExternalInput")
    C.wu_h = dram("w_up", [INNER, E], F32, "ExternalInput")
    C.wd_h = dram("w_down", [E, INNER], F32, "ExternalInput")
    C.out_h = dram("out", [B, S, E], F32, "ExternalOutput")
    C.wgT_h = dram("wgT", [32, E, 128], BF16, "Internal")
    C.wuT_h = dram("wuT", [32, E, 128], BF16, "Internal")
    C.wdT_h = dram("wdT", [8, INNER, 128], BF16, "Internal")
    C.pf_h = dram("pf_sc", [B, 64, S], F32, "Internal")
    C.u_h = dram("u_sc", [B, 64, S], BF16, "Internal")
    C.v_h = dram("v_sc", [B, 64, S], BF16, "Internal")
    C.vtm_h = dram("vtm_sc", [B, 16, 128, E], BF16, "Internal")
    C.attno_h = dram("attno_sc", [8, 128, NT], BF16, "Internal")
    C.xln_h = dram("xln_sc", [8, 128, NT], BF16, "Internal")
    C.xq_h = dram("xq_sc", [8, 128, NT], BF16, "Internal")

    C.consts = sb("consts", [128, len(CONSTS)], F32)
    C.disc = sb("disc", [128, 12], F32)
    C.wcol = sb("wcol", [128, 12], F32)
    C.ones_bf = sb("ones_bf", [128, 16], BF16)
    C.idn_bf = sb("idn_bf", [128, 128], BF16)
    C.idn_f = sb("idn_f", [128, 128], F32)
    C.seldup = sb("seldup", [128, 6 * 64], BF16)
    C.masks = sb("masks", [128, 4 * 512], BF16)
    C.rsel = [sb(f"rsel{i}", [8, 128], F32) for i in range(4)]
    C.onerow_f = sb("onerow_f", [1, 128], F32)
    C.wqs = sb("wqs", [128, 8], F32)
    C.rowsel2 = sb("rowsel2", [128, 2], BF16)
    C.rsg = sb("rsg", [128, 3 * 32], F32)
    C.scr = sb("scr", [128, 2048], F32)
    C.scr2 = sb("scr2", [128, 2048], F32)
    C.scri = sb("scri", [128, 2048], I32)
    C.scrb = sb("scrb", [128, 2048], BF16)
    C.selABf = sb("selABf", [2, 256], F32)
    C._xfm_cm = nc.sbuf_tensor("xfm", [128, 8 * NT], BF16)
    C.xfm = C._xfm_cm.__enter__()
    C.pb = [C.ps(f"pb{i}", [128, 512]) for i in range(8)]

    for v in CONSTS:
        nc.const_aps.register(F32, float(v), C.consts[:, CIDX[float(v)]:CIDX[float(v)] + 1])
    C.ccol = lambda v: C.consts[:, CIDX[float(v)]:CIDX[float(v)] + 1]

    def tap(name, src_fn, shape, dt=F32):
        if name not in C.tap_req:
            return
        t = dram("tap_" + name, shape, dt, "ExternalOutput")

        def _sy(sy, t=t, src_fn=src_fn):
            C.dinc(sy.dma_start(t, src_fn()), "st")
            C.wait(sy, "st")
        C.block(sync=_sy)
    C.tap = tap

    _p0(C)
    if _STAGE >= 1:
        _wprep(C)
    if _STAGE >= 2:
        _attention(C)
    if _STAGE >= 4:
        _wo_ln1_quant(C)
    if _STAGE >= 5:
        _ffn(C)

    def _fin(sy):
        C.wait(sy, "st")
    C.block(sync=_fin)

    for eng_fns in C.blocks:
        with nc.Block() as blk:
            for ename, fn in eng_fns.items():
                getattr(blk, ename)(fn)
    return C


def _p0(C):
    nc, block, inc, dinc, wait, ccol = C.nc, C.block, C.inc, C.dinc, C.wait, C.ccol
    C._p0scope = ExitStack()
    xtm = C._p0scope.enter_context(nc.sbuf_tensor("xtm", [128, 32 * E], BF16))
    pb = C.pb

    def g0(g):
        for t in range(32):
            b_, s_ = divmod(t, 16)
            dinc(g.dma_start(xtm[:, t * E:(t + 1) * E],
                             C.x_h[b_, s_ * 128:(s_ + 1) * 128, :]))
        for i, v in enumerate(CONSTS):
            g.memset(C.consts[:, i:i + 1], float(v))
        g.memset(C.ones_bf[:, :], 1.0)
        g.memset(C.onerow_f[:, :], 1.0)
        g.memset(C.disc[:, :], 0.0)
        g.memset(C.wcol[:, :], 0.0)
        g.memset(C.wqs[:, :], 0.0)
        g.memset(C.rsg[:, :], 0.0)
        g.iota(C.scr[:, 0:128], [[1, 128]], channel_multiplier=0,
               allow_small_or_imprecise_dtypes=True)
        g.iota(C.scr[:, 128:129], [[0, 1]], channel_multiplier=1,
               allow_small_or_imprecise_dtypes=True)
        for m in range(4):
            g.iota(C.scr2[:, 0:512], [[1, 512]], channel_multiplier=-1,
                   base=-128 * m, allow_small_or_imprecise_dtypes=True)
            g.drain()
            g.tensor_scalar(C.scr2[:, 512:1024], C.scr2[:, 0:512], -0.5,
                            None, Op.is_gt)
            g.drain()
            g.tensor_copy(C.masks[:, m * 512:(m + 1) * 512], C.scr2[:, 512:1024])
        for ib in range(4):
            g.memset(C.rsel[ib][:, :], 0.0)
        wait(g, "ld")
    block(gpsimd=g0)

    def v0(v):
        v.tensor_scalar(C.scr2[0:8, 0:128], C.scr[0:8, 0:128], 64.0, None,
                        Op.is_lt)
        v.tensor_scalar(C.scr2[0:8, 128:256], C.scr[0:8, 0:128], 63.5, None,
                        Op.is_gt)
        v.drain()
        for ib in range(4):
            v.tensor_scalar(C.scr2[0:8, 256:257], C.scr[0:8, 128:129],
                            float(2 * ib), None, Op.is_equal)
            v.tensor_scalar(C.scr2[0:8, 257:258], C.scr[0:8, 128:129],
                            float(2 * ib + 1), None, Op.is_equal)
            v.drain()
            v.tensor_scalar(C.scr2[0:8, 258:386], C.scr2[0:8, 0:128],
                            C.scr2[0:8, 256:257], None, Op.mult)
            v.tensor_scalar(C.scr2[0:8, 386:514], C.scr2[0:8, 128:256],
                            C.scr2[0:8, 257:258], None, Op.mult)
            v.drain()
            v.tensor_tensor(C.rsel[ib][:, :], C.scr2[0:8, 258:386],
                            C.scr2[0:8, 386:514], Op.add)
            v.drain()
        v.tensor_scalar(C.disc[:, 1:2], ccol(0.25), 0.5, None, Op.add)
        v.drain()
        v.tensor_copy(C.scri[:, 0:1], C.disc[:, 1:2])
        v.drain()
        v.tensor_copy(C.disc[:, 0:1], C.scri[:, 0:1])
        v.drain()
        v.tensor_scalar(C.disc[:, 1:2], C.disc[:, 0:1], -1.0, 1.0, Op.mult, Op.add)
        v.drain()
        v.tensor_scalar(C.disc[:, 2:3], C.disc[:, 1:2], QOFF + 0.5, None, Op.mult)
        v.tensor_scalar(C.disc[:, 3:4], C.disc[:, 0:1], -0.5, None, Op.mult)
        v.tensor_scalar(C.disc[:, 4:5], C.disc[:, 1:2], QOFF, None, Op.mult)
        v.tensor_scalar(C.disc[:, 5:6], C.disc[:, 1:2], 128.5, None, Op.mult)
        v.tensor_scalar(C.disc[:, 6:7], C.disc[:, 1:2], 128.0, 1.0, Op.mult, Op.add)
        v.tensor_scalar(C.disc[:, 7:8], C.disc[:, 1:2], 128.0, -1.0, Op.mult, Op.add)
        v.tensor_scalar(C.disc[:, 8:9], C.disc[:, 1:2], 128.0, None, Op.mult)
        v.tensor_scalar(C.disc[:, 9:10], C.disc[:, 0:1], float(np.pi),
                        float(-np.pi), Op.mult, Op.add)
        v.tensor_scalar(C.idn_f[:, :], C.scr[:, 0:128], C.scr[:, 128:129],
                        None, Op.is_equal)
        v.tensor_scalar(C.idn_bf[:, :], C.scr[:, 0:128], C.scr[:, 128:129],
                        None, Op.is_equal)
        v.tensor_scalar(C.rowsel2[:, 0:1], C.scr[:, 128:129], 64.0, None, Op.is_lt)
        v.tensor_scalar(C.rowsel2[:, 1:2], C.scr[:, 128:129], 63.5, None, Op.is_gt)
        v.tensor_copy(C.scri[:, 1:2], C.scr[:, 128:129])
        v.drain()
        v.tensor_scalar(C.scri[:, 2:3], C.scri[:, 1:2], 31, None, Op.bitwise_and)
        v.drain()
        v.tensor_copy(C.wcol[:, 0:1], C.scri[:, 2:3])
    block(vector=v0)

    def s0(s):
        s.activation(C.wcol[:, 1:2], C.wcol[:, 0:1], AF.Exp, bias=ccol(0.0),
                     scale=-LOG_RB / 32.0)
    block(scalar=s0)

    def v0b(v):
        v.tensor_scalar(C.wcol[:, 2:3], C.wcol[:, 1:2], INV_2PI, None, Op.mult)
        v.tensor_scalar(C.wcol[:, 3:4], C.disc[:, 3:4], 1.0, 0.5, Op.mult, Op.add)
        v.tensor_scalar(C.wcol[:, 4:5], C.disc[:, 3:4], 1.0, 0.75, Op.mult, Op.add)
    block(vector=v0b)

    def s0b(s):
        s.activation(C.wcol[:, 5:6], C.wcol[:, 1:2], AF.Sin, bias=ccol(0.0),
                     scale=1.0)
        s.activation(C.wcol[:, 6:7], C.wcol[:, 1:2], AF.Sin,
                     bias=ccol(np.pi / 2), scale=1.0)
    block(scalar=s0b)

    def v0c(v):
        sc = C.scr
        v.tensor_copy(C.scri[:, 0:64], sc[:, 0:64])
        v.drain()
        v.tensor_scalar(C.scri[:, 64:128], C.scri[:, 0:64], 31, None, Op.bitwise_and)
        v.drain()
        v.tensor_copy(sc[:, 256:320], C.scri[:, 64:128])
        v.tensor_scalar(sc[:, 576:640], sc[:, 0:64], 31.5, None, Op.is_gt)
        v.drain()
        v.tensor_scalar(sc[:, 320:384], sc[:, 256:320], C.wcol[:, 0:1],
                        None, Op.is_equal)
        v.tensor_scalar(sc[:, 704:768], sc[:, 576:640], 2.0, -1.0, Op.mult, Op.add)
        v.drain()
        v.tensor_copy(C.seldup[:, 0:64], sc[:, 320:384])
        v.tensor_scalar(sc[:, 384:448], sc[:, 320:384], C.wcol[:, 6:7], None, Op.mult)
        v.tensor_scalar(sc[:, 448:512], sc[:, 320:384], C.wcol[:, 5:6], None, Op.mult)
        v.tensor_mul(sc[:, 640:704], sc[:, 320:384], sc[:, 704:768])
        v.drain()
        v.tensor_copy(C.seldup[:, 64:128], sc[:, 384:448])
        v.tensor_scalar(sc[:, 512:576], sc[:, 448:512], -1.0, None, Op.mult)
        v.tensor_copy(C.seldup[:, 192:256], sc[:, 640:704])
        v.tensor_mul(sc[:, 768:832], sc[:, 384:448], sc[:, 704:768])
        v.tensor_mul(sc[:, 832:896], sc[:, 448:512], sc[:, 704:768])
        v.drain()
        v.tensor_copy(C.seldup[:, 128:192], sc[:, 512:576])
        v.tensor_copy(C.seldup[:, 256:320], sc[:, 768:832])
        v.tensor_copy(C.seldup[:, 320:384], sc[:, 832:896])
    block(vector=v0c)

    def pe0(t):
        base_ve = C.semv["ve"]
        for n in range(64):
            tt, half = divmod(n, 2)
            bank = pb[6] if n % 2 == 0 else pb[7]
            if n >= 2:
                t.wait_ge(C.sems["ve"], base_ve + n - 1)
            for c in range(4):
                cc = half * 4 + c
                t.transpose(bank.bitcast(BF16)[:, c * 128:(c + 1) * 128],
                            xtm[:, tt * E + cc * 128:tt * E + (cc + 1) * 128],
                            C.idn_bf[:, :])
            t.drain().then_inc(C.sems["pe"], 1)
        C.semv["pe"] += 64

    def ve0(v):
        base_pe = C.semv["pe"] - 64
        for n in range(64):
            tt, half = divmod(n, 2)
            b_, s_ = divmod(tt, 16)
            tok0 = b_ * S + s_ * 128
            bank = pb[6] if n % 2 == 0 else pb[7]
            v.wait_ge(C.sems["pe"], base_pe + n + 1)
            for c in range(4):
                cc = half * 4 + c
                v.tensor_copy(C.xfm[:, cc * NT + tok0:cc * NT + tok0 + 128],
                              bank.bitcast(BF16)[:, c * 128:(c + 1) * 128])
            v.drain().then_inc(C.sems["ve"], 1)
        C.semv["ve"] += 64
    block(tensor=pe0, vector=ve0)

    C.tap("disc", lambda: C.disc[:, :], [128, 12])
    C.tap("wcol", lambda: C.wcol[:, :], [128, 12])
    C.tap("xfm", lambda: C.xfm[:, :].bitcast(F32), [128, 4 * NT])
    C.tap("masks", lambda: C.masks[:, :].bitcast(F32), [128, 1024])
    C.tap("seldup", lambda: C.seldup[:, :].bitcast(F32), [128, 192])


def _wprep(C):
    nc, block, inc, dinc, wait, ccol = C.nc, C.block, C.inc, C.dinc, C.wait, C.ccol
    pb = C.pb
    C._p0scope.close()
    with ExitStack() as scope:
        e = scope.enter_context
        wfl = e(nc.sbuf_tensor("wfl", [128, 4096], F32))
        wfl2 = e(nc.sbuf_tensor("wfl2", [128, 4096], F32))
        wiq = e(nc.sbuf_tensor("wiq", [128, 4096], I32))
        wzb = e(nc.sbuf_tensor("wzb", [128, 4096], F32))
        wqb = e(nc.sbuf_tensor("wqb", [128, 4096], BF16))
        wtb = e(nc.sbuf_tensor("wtb", [128, 2 * 1024], BF16))

        WCFG = [(C.wg_h, INNER, E, C.wgT_h, 0), (C.wu_h, INNER, E, C.wuT_h, 1),
                (C.wd_h, E, INNER, C.wdT_h, 2)]

        def w1g(g):
            C._w1ve = C.semv["ve"]
            g.memset(C.scr2[:, 0:5], 0.0)
            g.drain().then_inc(C.sems["gp"], 1)
            C.semv["gp"] += 1
            n = 0
            for wh, rows, cols, _, wi in WCFG:
                for ch in range(rows // 128):
                    buf = wfl if n % 2 == 0 else wfl2
                    if n >= 2:
                        g.wait_ge(C.sems["ve"], C._w1ve + n - 1)
                    dinc(g.dma_start(buf[:, 0:cols],
                                     wh[ch * 128:(ch + 1) * 128, :]))
                    g.wait_ge(C.sems["ld"], C.semv["ld"])
                    g.drain().then_inc(C.sems["gp"], 1)
                    C.semv["gp"] += 1
                    n += 1

        def w1v(v):
            nch1 = sum(rows // 128 for _, rows, _, _, _ in WCFG)
            base_gp = C.semv["gp"] - (nch1 + 1)
            n = 0
            for wh, rows, cols, _, wi in WCFG:
                for ch in range(rows // 128):
                    buf = wfl if n % 2 == 0 else wfl2
                    v.wait_ge(C.sems["gp"], base_gp + n + 2)
                    v.tensor_reduce(C.scr2[:, 4:5], buf[:, 0:cols], AX, Op.add,
                                    apply_absolute_value=True)
                    v.drain()
                    v.tensor_add(C.scr2[:, wi:wi + 1], C.scr2[:, wi:wi + 1],
                                 C.scr2[:, 4:5])
                    v.drain().then_inc(C.sems["ve"], 1)
                    n += 1
            C.semv["ve"] += nch1
        block(gpsimd=w1g, vector=w1v)

        def w1pe(t):
            t.matmul(pb[0][0:3, 0:1], C.scr2[:, 0:3], ccol(1.0), start=True,
                     stop=True)
        block(tensor=w1pe)

        def w1v2(v):
            v.tensor_copy(C.scr2[0:3, 8:9], pb[0][0:3, 0:1])
            v.drain()
            v.tensor_scalar(C.scr2[0:3, 9:10], C.scr2[0:3, 8:9],
                            1.0 / (INNER * E), None, Op.mult)
            v.drain()
            v.tensor_scalar(C.scr2[0:3, 10:11], C.scr2[0:3, 9:10], 1e-5,
                            None, Op.max)
            v.drain()
            v.reciprocal(C.scr2[0:3, 11:12], C.scr2[0:3, 10:11])
        block(vector=w1v2)

        def w1sy(sy):
            dinc(sy.dma_start(C.scr2[0:1, 12:13], C.scr2[1:2, 11:12]), "st")
            dinc(sy.dma_start(C.scr2[0:1, 13:14], C.scr2[2:3, 11:12]), "st")
            wait(sy, "st")
        block(sync=w1sy)

        def w1pe2(t):
            for wi in range(3):
                t.matmul(pb[1][:, wi:wi + 1], C.onerow_f[0:1, :],
                         C.scr2[0:1, 11 + wi:12 + wi], start=True, stop=True,
                         skip_group_check=True)
        block(tensor=w1pe2)

        def w1v3(v):
            v.tensor_copy(C.wqs[:, 0:3], pb[1][:, 0:3])
        block(vector=w1v3)
        C.tap("wqs", lambda: C.wqs[:, :], [128, 8])

        chunks = []
        for wh, rows, cols, whT, wi in WCFG:
            for ch in range(rows // 128):
                chunks.append((wh, rows, cols, whT, wi, ch))
        NCH = len(chunks)
        nbat = [cols // 1024 for _, _, cols, _, _, _ in chunks]
        batpre = [0]
        for nb in nbat:
            batpre.append(batpre[-1] + nb)
        NBAT = batpre[-1]

        b0 = {}

        def w2g(g):
            for k_ in ("ld2", "gp", "ve", "pe", "ve3", "st"):
                b0[k_] = C.semv[k_]
            for n, (wh, rows, cols, whT, wi, ch) in enumerate(chunks):
                buf = wfl if n % 2 == 0 else wfl2
                if n >= 2:
                    g.wait_ge(C.sems["ve"], b0["ve"] + 2 * (n - 1))
                dinc(g.dma_start(buf[:, 0:cols],
                                 wh[ch * 128:(ch + 1) * 128, :]), "ld2")
                g.wait_ge(C.sems["ld2"], b0["ld2"] + 16 * (n + 1))
                g.drain().then_inc(C.sems["gp"], 1)
                g.wait_ge(C.sems["ve"], b0["ve"] + 2 * n + 1)
                g.tensor_copy(wiq[:, 0:cols], wzb[:, 0:cols])
                g.drain().then_inc(C.sems["gp"], 1)
            C.semv["gp"] += 2 * NCH

        def w2v(v):
            for n, (wh, rows, cols, whT, wi, ch) in enumerate(chunks):
                buf = wfl if n % 2 == 0 else wfl2
                v.wait_ge(C.sems["gp"], b0["gp"] + 2 * n + 1)
                v.tensor_scalar(wzb[:, 0:cols], buf[:, 0:cols],
                                C.wqs[:, wi:wi + 1], C.disc[:, 5:6],
                                Op.mult, Op.add)
                v.drain().then_inc(C.sems["ve"], 1)
                v.wait_ge(C.sems["gp"], b0["gp"] + 2 * n + 2)
                v.tensor_copy(wzb[:, 0:cols], wiq[:, 0:cols])
                v.drain()
                v.tensor_scalar(wzb[:, 0:cols], wzb[:, 0:cols], C.disc[:, 6:7],
                                C.disc[:, 7:8], Op.min, Op.max)
                v.drain()
                v.tensor_scalar(wzb[:, 0:cols], wzb[:, 0:cols], C.disc[:, 8:9],
                                None, Op.subtract)
                v.drain()
                v.tensor_copy(wqb[:, 0:cols], wzb[:, 0:cols])
                v.tensor_reduce(C.rsg[:, wi * 32 + ch:wi * 32 + ch + 1],
                                wzb[:, 0:cols], AX, Op.add)
                v.drain().then_inc(C.sems["ve"], 1)
                for bb in range(nbat[n]):
                    gb = batpre[n] + bb
                    bank = pb[6] if gb % 2 == 0 else pb[7]
                    v.wait_ge(C.sems["pe"], b0["pe"] + gb + 1)
                    if gb >= 2:
                        v.wait_ge(C.sems["st"], b0["st"] + 16 * (gb - 1))
                    v.tensor_copy(wtb[:, (gb % 2) * 1024:(gb % 2) * 1024 + 1024],
                                  bank.bitcast(BF16)[:, 0:1024])
                    v.drain().then_inc(C.sems["ve3"], 1)
            C.semv["ve"] += 2 * NCH
            C.semv["ve3"] += NBAT

        def w2pe(t):
            for n, (wh, rows, cols, whT, wi, ch) in enumerate(chunks):
                t.wait_ge(C.sems["ve"], b0["ve"] + 2 * n + 2)
                for bb in range(nbat[n]):
                    gb = batpre[n] + bb
                    bank = pb[6] if gb % 2 == 0 else pb[7]
                    if gb >= 2:
                        t.wait_ge(C.sems["ve3"], b0["ve3"] + gb - 1)
                    for c in range(8):
                        col0 = bb * 1024 + c * 128
                        t.transpose(bank.bitcast(BF16)[:, c * 128:(c + 1) * 128],
                                    wqb[:, col0:col0 + 128], C.idn_bf[:, :])
                    t.drain().then_inc(C.sems["pe"], 1)
            C.semv["pe"] += NBAT

        def w2sy(sy):
            for n, (wh, rows, cols, whT, wi, ch) in enumerate(chunks):
                for bb in range(nbat[n]):
                    gb = batpre[n] + bb
                    sy.wait_ge(C.sems["ve3"], b0["ve3"] + gb + 1)
                    if gb >= 1:
                        sy.wait_ge(C.sems["st"], b0["st"] + 16 * gb)
                    src = wtb[:, (gb % 2) * 1024:(gb % 2) * 1024 + 1024]
                    if wi < 2:
                        dst = whT[ch, :, :].rearrange("(ic r) c -> r ic c", r=128)
                    else:
                        icn0 = bb * 8
                        dst = whT[ch, icn0 * 128:(icn0 + 8) * 128, :].rearrange(
                            "(ic r) c -> r ic c", r=128)
                    dinc(sy.dma_start(dst, src.rearrange("r (ic c) -> r ic c",
                                                         c=128)), "st")
        block(gpsimd=w2g, vector=w2v, tensor=w2pe, sync=w2sy)
    C.tap("rsg", lambda: C.rsg[:, :], [128, 96])
    C.tap("wgT0", lambda: C.wgT_h[0, :, :], [E, 128], BF16)
    C.tap("wdT0", lambda: C.wdT_h[0, :, :], [INNER, 128], BF16)


def _attention(C):
    nc, block, inc, dinc, wait, ccol = C.nc, C.block, C.inc, C.dinc, C.wait, C.ccol
    pb = C.pb
    selABf = C.selABf

    def selABf_init(v):
        pc = C.scr[0:2, 128:129].broadcast_to((2, 128))
        v.tensor_scalar(selABf[0:2, 0:128], pc, 0.5, None, Op.is_lt)
        v.tensor_scalar(selABf[0:2, 128:256], pc, 0.5, None, Op.is_gt)
    block(vector=selABf_init)

    for b in range(B):
        bscope = ExitStack()
        be = bscope.enter_context
        pbf = be(nc.sbuf_tensor(f"pbf{b}", [64, S], BF16))
        strip = be(nc.sbuf_tensor(f"strip{b}", [64, S], BF16))
        _p1_seq(C, b, pbf=pbf, strip=strip)
        ps_ = ExitStack()
        e = ps_.enter_context
        A = _Ctx()
        A.pbf, A.strip, A.selABf = pbf, strip, selABf
        A.wqp = e(nc.sbuf_tensor(f"wqp{b}", [128, 8 * 128], BF16))
        A.wkp = e(nc.sbuf_tensor(f"wkp{b}", [128, 8 * 128], BF16))
        A.vp = e(nc.sbuf_tensor(f"vp{b}", [128, 16 * 128], BF16))
        A.qbf = e(nc.sbuf_tensor(f"qbf{b}", [128, S], BF16))
        A.kbf = e(nc.sbuf_tensor(f"kbf{b}", [128, S], BF16))
        A.qrot = e(nc.sbuf_tensor(f"qrot{b}", [128, S], BF16))
        A.krot = e(nc.sbuf_tensor(f"krot{b}", [128, S], BF16))
        A.qr = e(nc.sbuf_tensor(f"qr{b}", [128, S], BF16))
        A.kr = e(nc.sbuf_tensor(f"kr{b}", [128, S], BF16))
        A.pfrep = e(nc.sbuf_tensor(f"pfrep{b}", [128, S], F32))
        A.urep = e(nc.sbuf_tensor(f"urep{b}", [128, S], BF16))
        A.vrep = e(nc.sbuf_tensor(f"vrep{b}", [128, S], BF16))
        A.trigb = e(nc.sbuf_tensor(f"trigb{b}", [128, 2 * 512], BF16))
        A.prods = e(nc.sbuf_tensor(f"prods{b}", [128, 4 * 512], BF16))
        A.tmpq = e(nc.sbuf_tensor(f"tmpq{b}", [128, 512], BF16))
        A.tmpq2 = e(nc.sbuf_tensor(f"tmpq2{b}", [128, 512], BF16))
        A.negC = e(nc.sbuf_tensor(f"negC{b}", [128, 2], F32))
        A.qn = e(nc.sbuf_tensor(f"qn{b}", [2, 16], F32))
        A.eA = [e(nc.sbuf_tensor(f"eA{b}_{i}", [128, 512], BF16)) for i in range(2)]
        A.eB = [e(nc.sbuf_tensor(f"eB{b}_{i}", [128, 512], BF16)) for i in range(2)]
        A.ptA = [e(nc.sbuf_tensor(f"ptA{b}_{i}", [128, 512], BF16)) for i in range(2)]
        A.ptB = [e(nc.sbuf_tensor(f"ptB{b}_{i}", [128, 512], BF16)) for i in range(2)]
        A.outev = e(nc.sbuf_tensor(f"outev{b}", [128, S], F32))
        A.attp = e(nc.sbuf_tensor(f"attp{b}", [128, S], BF16))
        A.cscr = C.scr2
        A.sstg = C.scri.bitcast(F32)[0:8, 0:512]
        A.sstg2 = C.scri.bitcast(F32)[0:8, 512:1024]
        for g in range(8):
            _pair_prep(C, A, b, g)
            _pair_scores(C, A, b, g)
        ps_.close()
        bscope.close()


def _p1_seq(C, b, pbf, strip):
    nc, block, inc, dinc, wait, ccol = C.nc, C.block, C.inc, C.dinc, C.wait, C.ccol
    pb = C.pb
    scope = ExitStack()
    e = scope.enter_context
    wvs = e(nc.sbuf_tensor(f"wvs{b}", [128, 8 * E], BF16))
    wrs = e(nc.sbuf_tensor(f"wrs{b}", [128, 8 * 64], BF16))
    vtm = e(nc.sbuf_tensor(f"vtm{b}", [128, 2 * E], BF16))
    ltm = e(nc.sbuf_tensor(f"ltm{b}", [128, 16 * 64], F32))
    ptm = e(nc.sbuf_tensor(f"ptm{b}", [128, 16 * 64], F32))
    onesf = e(nc.sbuf_tensor(f"onesf{b}", [64, S], F32))
    pcm = e(nc.sbuf_tensor(f"pcm{b}", [64, S], F32))
    ubf = e(nc.sbuf_tensor(f"ubf{b}", [64, S], BF16))

    def l0(g):
        for c in range(8):
            dinc(g.dma_start(wvs[:, c * E:(c + 1) * E],
                             C.wv_h[c * 128:(c + 1) * 128, :]))
            dinc(g.dma_start(wrs[:, c * 64:(c + 1) * 64],
                             C.wr_h[c * 128:(c + 1) * 128, :]))
        g.memset(onesf[:, :], 1.0)
        g.memset(strip[0:64, :], SQRT_EPS_NW)
        wait(g, "ld")
    block(gpsimd=l0)

    def vp_pe(t):
        C._p1st = C.semv["st"]
        base_ve = C.semv["ve"]
        C._p1ve = base_ve
        for tt in range(16):
            bs = [pb[0], pb[1], pb[2]] if tt % 2 == 0 else [pb[3], pb[4], pb[5]]
            if tt >= 2:
                t.wait_ge(C.sems["ve"], base_ve + tt - 1)
            for ic in range(8):
                lhs = C.xfm[:, ic * NT + b * S + tt * 128:
                            ic * NT + b * S + tt * 128 + 128]
                t.matmul(bs[0][:, :], lhs, wvs[:, ic * E:ic * E + 512],
                         start=(ic == 0), stop=(ic == 7))
                t.matmul(bs[1][:, :], lhs, wvs[:, ic * E + 512:ic * E + 1024],
                         start=(ic == 0), stop=(ic == 7))
                t.matmul(bs[2][:, 0:64], lhs, wrs[:, ic * 64:(ic + 1) * 64],
                         start=(ic == 0), stop=(ic == 7))
            t.drain().then_inc(C.sems["pe"], 1)
        C.semv["pe"] += 16

    def vp_ve(v):
        base_pe = C.semv["pe"] - 16
        for tt in range(16):
            bs = [pb[0], pb[1], pb[2]] if tt % 2 == 0 else [pb[3], pb[4], pb[5]]
            v.wait_ge(C.sems["pe"], base_pe + tt + 1)
            if tt >= 2:
                v.wait_ge(C.sems["st"], C._p1st + 16 * (tt - 1))
            v.tensor_copy(vtm[:, (tt % 2) * E:(tt % 2) * E + 512], bs[0][:, :])
            v.tensor_copy(vtm[:, (tt % 2) * E + 512:(tt % 2) * E + 1024],
                          bs[1][:, :])
            v.tensor_copy(ltm[:, tt * 64:(tt + 1) * 64], bs[2][:, 0:64])
            v.drain().then_inc(C.sems["ve"], 1)
        C.semv["ve"] += 16

    def vp_sy(sy):
        base_ve = C.semv["ve"] - 16
        for tt in range(16):
            sy.wait_ge(C.sems["ve"], base_ve + tt + 1)
            if tt >= 1:
                sy.wait_ge(C.sems["st"], C._p1st + 16 * tt)
            dinc(sy.dma_start(C.vtm_h[b, tt, :, :],
                              vtm[:, (tt % 2) * E:(tt % 2 + 1) * E]), "st")
    block(tensor=vp_pe, vector=vp_ve, sync=vp_sy)
    C.tap(f"ltm{b}", lambda: ltm[:, :], [128, 1024])

    def _kv(t_, k):
        return t_[:, 0:1024].rearrange("p (n k) -> p n k", k=4)[:, :, k]

    def _bc4(row):
        return row.rearrange("p n -> p n ()").broadcast_to((128, 256, 4))

    def sm_v(v):
        v.tensor_tensor(C.scr[:, 0:256], _kv(ltm, 0), _kv(ltm, 1), Op.max)
        v.tensor_tensor(C.scr[:, 256:512], _kv(ltm, 2), _kv(ltm, 3), Op.max)
        v.drain()
        v.tensor_tensor(C.scr[:, 512:768], C.scr[:, 0:256], C.scr[:, 256:512], Op.max)
        v.drain()
        v.tensor_tensor(C.scr2[:, 0:1024].rearrange("p (n k) -> p n k", k=4),
                        ltm[:, 0:1024].rearrange("p (n k) -> p n k", k=4),
                        _bc4(C.scr[:, 512:768]), Op.subtract)
        v.drain().then_inc(C.sems["ve"], 1)
        C.semv["ve"] += 1

    def sm_s(s):
        wait(s, "ve")
        s.activation(ptm[:, 0:1024], C.scr2[:, 0:1024], AF.Exp, bias=ccol(0.0),
                     scale=1.0)
        s.drain().then_inc(C.sems["sc"], 1)
        C.semv["sc"] += 1

    def sm_v2(v):
        wait(v, "sc")
        v.tensor_tensor(C.scr[:, 0:256], _kv(ptm, 0), _kv(ptm, 1), Op.add)
        v.tensor_tensor(C.scr[:, 256:512], _kv(ptm, 2), _kv(ptm, 3), Op.add)
        v.drain()
        v.tensor_tensor(C.scr[:, 512:768], C.scr[:, 0:256], C.scr[:, 256:512], Op.add)
        v.drain()
        v.reciprocal(C.scr[:, 768:1024], C.scr[:, 512:768])
        v.drain()
        v.tensor_tensor(C.scr2[:, 0:1024].rearrange("p (n k) -> p n k", k=4),
                        ptm[:, 0:1024].rearrange("p (n k) -> p n k", k=4),
                        _bc4(C.scr[:, 768:1024]), Op.mult)
    block(vector=sm_v)
    block(scalar=sm_s)
    block(vector=sm_v2)
    C.tap(f"ptm{b}", lambda: C.scr2[:, 0:1024], [128, 1024])

    def pt_pe(t):
        base_ve = C.semv["ve"]
        C._ptve = base_ve
        for bt in range(4):
            bank = pb[6] if bt % 2 == 0 else pb[7]
            if bt >= 2:
                t.wait_ge(C.sems["ve"], base_ve + bt - 1)
            for c in range(4):
                tt = bt * 4 + c
                t.transpose(bank[0:64, c * 128:(c + 1) * 128],
                            C.scr2[:, tt * 64:(tt + 1) * 64], C.idn_f[:, :])
            t.drain().then_inc(C.sems["pe"], 1)
        C.semv["pe"] += 4

    def pt_ve(v):
        base_pe = C.semv["pe"] - 4
        for bt in range(4):
            bank = pb[6] if bt % 2 == 0 else pb[7]
            v.wait_ge(C.sems["pe"], base_pe + bt + 1)
            v.tensor_copy(pcm[0:64, bt * 512:(bt + 1) * 512], bank[0:64, 0:512])
            v.drain().then_inc(C.sems["ve"], 1)
        C.semv["ve"] += 4
    block(tensor=pt_pe, vector=pt_ve)
    C.tap(f"pcm{b}", lambda: pcm[0:64, :], [64, S])

    def cs_v(v):
        v.tensor_copy(pbf[0:64, :], pcm[0:64, :])
        v.tensor_tensor_scan(C.scr2[0:64, 0:S], onesf[0:64, :], pcm[0:64, :],
                             0.0, Op.mult, Op.add)
        v.drain()
        v.tensor_sub(C.scr2[0:64, 0:S], C.scr2[0:64, 0:S], pcm[0:64, :])
        v.drain()
        v.tensor_scalar(C.scr[0:64, 0:S], C.scr2[0:64, 0:S], C.disc[0:64, 3:4],
                        None, Op.add)
        v.drain()
        v.tensor_copy(C.scri[0:64, 0:S], C.scr[0:64, 0:S])
        v.drain()
        v.tensor_copy(C.scr[0:64, 0:S], C.scri[0:64, 0:S])
        v.drain()
        v.tensor_sub(C.scr2[0:64, 0:S], C.scr2[0:64, 0:S], C.scr[0:64, 0:S])
        v.drain()
        v.tensor_tensor(C.scrb[0:64, 0:S], pcm[0:64, :], C.scr2[0:64, 0:S],
                        Op.mult)
        v.drain()
        v.tensor_tensor(ubf[0:64, :], pcm[0:64, :], C.scrb[0:64, 0:S],
                        Op.subtract)
        v.drain().then_inc(C.sems["ve"], 1)
        C.semv["ve"] += 1

    def cs_sy(sy):
        wait(sy, "ve")
        dinc(sy.dma_start(C.pf_h[b, :, :], C.scr[0:64, 0:S]), "st")
        dinc(sy.dma_start(C.u_h[b, :, :], ubf[0:64, :]), "st")
        dinc(sy.dma_start(C.v_h[b, :, :], C.scrb[0:64, 0:S]), "st")
        wait(sy, "st")
    block(vector=cs_v)
    block(sync=cs_sy)
    C.tap(f"pf{b}", lambda: C.scr[0:64, 0:S], [64, S])
    C.tap(f"ubf{b}", lambda: ubf[0:64, :].bitcast(F32), [64, S // 2])
    scope.close()


PAIR_ITERS = [(ib, jc) for ib in range(4) for jc in range(4 * ib + 4)]
PAIR_DIAG = [jc >= 4 * ib for (ib, jc) in PAIR_ITERS]
PAIR_DC = []
_c = 0
for _d in PAIR_DIAG:
    _c += 1 if _d else 0
    PAIR_DC.append(_c)
NIT = len(PAIR_ITERS)


def _pair_prep(C, A, b, g):
    nc, block, inc, dinc, wait, ccol = C.nc, C.block, C.inc, C.dinc, C.wait, C.ccol
    pb = C.pb

    def pp_g(g_):
        dinc(g_.dma_start(
            A.wqp[:, :].rearrange("p (ic c) -> p ic c", c=128),
            C.wq_h[:, g * 128:(g + 1) * 128].rearrange("(ic p) c -> p ic c",
                                                       p=128)))
        dinc(g_.dma_start(
            A.wkp[:, :].rearrange("p (ic c) -> p ic c", c=128),
            C.wk_h[:, g * 128:(g + 1) * 128].rearrange("(ic p) c -> p ic c",
                                                       p=128)))
        dinc(g_.dma_start(
            A.vp[:, :].rearrange("p (tt c) -> p tt c", c=128),
            C.vtm_h[b, :, :, g * 128:(g + 1) * 128].rearrange(
                "tt p c -> p tt c")))
        wait(g_, "ld")
        g_.drain().then_inc(C.sems["gp"], 1)
        C.semv["gp"] += 1

    def pp_sy(sy):
        C._ppsy_ve = C.semv["ve"]
        dinc(sy.dma_start(A.strip[0:4, :], A.pbf[8 * g:8 * g + 4, :]), "st")
        dinc(sy.dma_start(A.strip[32:36, :], A.pbf[8 * g + 4:8 * g + 8, :]), "st")
        wait(sy, "ve")
        for (dst, src) in ((A.qrot, A.qbf), (A.krot, A.kbf)):
            dinc(sy.dma_start(dst[0:32, :], src[32:64, :]), "st")
            dinc(sy.dma_start(dst[32:64, :], src[0:32, :]), "st")
            dinc(sy.dma_start(dst[64:96, :], src[96:128, :]), "st")
            dinc(sy.dma_start(dst[96:128, :], src[64:96, :]), "st")
        wait(sy, "st")
        sy.drain().then_inc(C.sems["sy"], 1)
        C.semv["sy"] += 1

    def pp_pe(t):
        C._pp_pe_ve = C.semv["ve"]
        wait(t, "gp")
        for sblk in range(4):
            if sblk >= 1:
                t.wait_ge(C.sems["ve"], C._pp_pe_ve + sblk)
            for ic in range(8):
                rhs = C.xfm[:, ic * NT + b * S + sblk * 512:
                            ic * NT + b * S + sblk * 512 + 512]
                t.matmul(pb[0][:, :], A.wqp[:, ic * 128:(ic + 1) * 128], rhs,
                         start=(ic == 0), stop=(ic == 7))
                t.matmul(pb[1][:, :], A.wkp[:, ic * 128:(ic + 1) * 128], rhs,
                         start=(ic == 0), stop=(ic == 7))
            t.drain().then_inc(C.sems["pe"], 1)
        C.semv["pe"] += 4

    def pp_ve(v):
        base_pe = C.semv["pe"] - 4
        for sblk in range(4):
            v.wait_ge(C.sems["pe"], base_pe + sblk + 1)
            v.tensor_copy(A.qbf[:, sblk * 512:(sblk + 1) * 512], pb[0][:, :])
            v.tensor_copy(A.kbf[:, sblk * 512:(sblk + 1) * 512], pb[1][:, :])
            v.drain().then_inc(C.sems["ve"], 1)
            C.semv["ve"] += 1
        v.drain().then_inc(C.sems["ve"], 1)
        C.semv["ve"] += 1
    block(gpsimd=pp_g, tensor=pp_pe, vector=pp_ve, sync=pp_sy)

    _rope_emit(C, A, b, g)


def _rope_emit(C, A, b, g):
    nc, block, inc, dinc, wait, ccol = C.nc, C.block, C.inc, C.dinc, C.wait, C.ccol
    pb = C.pb

    def vidx(hh, sblk, j):
        return 12 * hh + 3 * sblk + j

    def rp_g(g_):
        C._rp = {k_: C.semv[k_] for k_ in ("ve", "gp", "gp2", "sc", "pe", "ld")}
        base = C._rp
        for hh in range(2):
            if hh == 1:
                g_.wait_ge(C.sems["ve"], base["ve"] + vidx(0, 3, 3))
            row0 = 8 * g + hh * 4
            for (dst, hsrc) in ((A.pfrep, C.pf_h), (A.urep, C.u_h),
                                (A.vrep, C.v_h)):
                src = bass.AP(hsrc.tensor, b * 64 * S + row0 * S,
                              [[S, 4], [0, 32], [1, S]])
                dinc(g_.dma_start(dst[:, :], src))
            g_.wait_ge(C.sems["ld"], C.semv["ld"])
            g_.drain().then_inc(C.sems["gp"], 1)
            C.semv["gp"] += 1
            for sblk in range(4):
                g_.wait_ge(C.sems["ve"], base["ve"] + vidx(hh, sblk, 1))
                g_.tensor_copy(C.scri[:, 0:1024], C.scr[:, 0:1024])
                g_.drain().then_inc(C.sems["gp"], 1)
                C.semv["gp"] += 1
        for sblk in range(4):
            c0 = sblk * 512
            g_.wait_ge(C.sems["ve"], base["ve"] + 24 + 2 * sblk + 1)
            g_.tensor_tensor(A.qr[:, c0:c0 + 512], A.tmpq[:, :], A.tmpq2[:, :],
                             Op.add)
            g_.drain().then_inc(C.sems["gp2"], 1)
            C.semv["gp2"] += 1
            g_.wait_ge(C.sems["ve"], base["ve"] + 24 + 2 * sblk + 2)
            g_.tensor_tensor(A.kr[:, c0:c0 + 512], A.tmpq[:, :], A.tmpq2[:, :],
                             Op.add)
            g_.drain().then_inc(C.sems["gp2"], 1)
            C.semv["gp2"] += 1

    def rp_v(v):
        base = C._rp
        for hh in range(2):
            for sblk in range(4):
                n_ = 4 * hh + sblk
                col = sblk * 512
                v.wait_ge(C.sems["gp"], base["gp"] + hh * 5 + 1)
                if n_ >= 1:
                    ph, psb = divmod(n_ - 1, 4)
                    v.wait_ge(C.sems["gp"], base["gp"] + ph * 5 + psb + 2)
                v.tensor_scalar(C.scr[:, 0:512], A.pfrep[:, col:col + 512],
                                C.wcol[:, 2:3], C.wcol[:, 3:4], Op.mult, Op.add)
                v.tensor_scalar(C.scr[:, 512:1024], A.pfrep[:, col:col + 512],
                                C.wcol[:, 2:3], C.wcol[:, 4:5], Op.mult, Op.add)
                v.drain().then_inc(C.sems["ve"], 1)
                C.semv["ve"] += 1
                v.wait_ge(C.sems["gp"], base["gp"] + hh * 5 + sblk + 2)
                v.tensor_copy(C.scr2[:, 0:1024], C.scri[:, 0:1024])
                v.drain()
                v.tensor_sub(C.scr[:, 0:1024], C.scr[:, 0:1024],
                             C.scr2[:, 0:1024])
                v.drain().then_inc(C.sems["ve"], 1)
                C.semv["ve"] += 1
                v.wait_ge(C.sems["sc"], base["sc"] + n_ + 1)
                if n_ >= 1:
                    v.wait_ge(C.sems["pe"], base["pe"] + n_)
                v.tensor_tensor(A.prods[:, 0:512], A.urep[:, col:col + 512],
                                A.trigb[:, 512:1024], Op.mult)
                v.tensor_tensor(A.prods[:, 512:1024], A.vrep[:, col:col + 512],
                                A.trigb[:, 512:1024], Op.mult)
                v.tensor_tensor(A.prods[:, 1024:1536], A.urep[:, col:col + 512],
                                A.trigb[:, 0:512], Op.mult)
                v.tensor_tensor(A.prods[:, 1536:2048], A.vrep[:, col:col + 512],
                                A.trigb[:, 0:512], Op.mult)
                v.drain().then_inc(C.sems["ve"], 1)
                C.semv["ve"] += 1
        for sblk in range(4):
            c0 = sblk * 512
            v.wait_ge(C.sems["pe"], base["pe"] + 4 + sblk + 1)
            if sblk >= 1:
                v.wait_ge(C.sems["gp2"], base["gp2"] + 2 * sblk)
            v.tensor_tensor(A.tmpq[:, :], A.qbf[:, c0:c0 + 512],
                            pb[sblk][:, :], Op.mult)
            v.tensor_tensor(A.tmpq2[:, :], A.qrot[:, c0:c0 + 512],
                            pb[4 + sblk][:, :], Op.mult)
            v.drain().then_inc(C.sems["ve"], 1)
            C.semv["ve"] += 1
            v.wait_ge(C.sems["gp2"], base["gp2"] + 2 * sblk + 1)
            v.tensor_tensor(A.tmpq[:, :], A.kbf[:, c0:c0 + 512],
                            pb[sblk][:, :], Op.mult)
            v.tensor_tensor(A.tmpq2[:, :], A.krot[:, c0:c0 + 512],
                            pb[4 + sblk][:, :], Op.mult)
            v.drain().then_inc(C.sems["ve"], 1)
            C.semv["ve"] += 1

    def rp_s(s):
        base = C._rp
        for hh in range(2):
            for sblk in range(4):
                n_ = 4 * hh + sblk
                s.wait_ge(C.sems["ve"], base["ve"] + vidx(hh, sblk, 2))
                if n_ >= 1:
                    ph, psb = divmod(n_ - 1, 4)
                    s.wait_ge(C.sems["ve"], base["ve"] + vidx(ph, psb, 3))
                s.activation(A.trigb[:, 0:512], C.scr[:, 0:512], AF.Sin,
                             bias=C.disc[:, 9:10], scale=TWO_PI)
                s.activation(A.trigb[:, 512:1024], C.scr[:, 512:1024], AF.Sin,
                             bias=C.disc[:, 9:10], scale=TWO_PI)
                s.drain().then_inc(C.sems["sc"], 1)
                C.semv["sc"] += 1

    def rp_pe(t):
        base = C._rp
        sel = C.seldup
        for hh in range(2):
            r0 = hh * 64
            for sblk in range(4):
                n_ = 4 * hh + sblk
                t.wait_ge(C.sems["ve"], base["ve"] + vidx(hh, sblk, 3))
                P1 = A.prods[:, 0:512]
                P2 = A.prods[:, 512:1024]
                P3 = A.prods[:, 1024:1536]
                P4 = A.prods[:, 1536:2048]
                cw, sw = pb[sblk], pb[4 + sblk]
                t.matmul(cw[r0:r0 + 64, :], sel[:, 0:64], P1, start=True,
                         stop=False, skip_group_check=True)
                t.matmul(cw[r0:r0 + 64, :], sel[:, 64:128], P2, start=False,
                         stop=False, skip_group_check=True)
                t.matmul(cw[r0:r0 + 64, :], sel[:, 128:192], P4, start=False,
                         stop=True, skip_group_check=True)
                t.matmul(sw[r0:r0 + 64, :], sel[:, 192:256], P3, start=True,
                         stop=False, skip_group_check=True)
                t.matmul(sw[r0:r0 + 64, :], sel[:, 256:320], P4, start=False,
                         stop=False, skip_group_check=True)
                t.matmul(sw[r0:r0 + 64, :], sel[:, 320:384], P2, start=False,
                         stop=True, skip_group_check=True)
                t.drain().then_inc(C.sems["pe"], 1)
                C.semv["pe"] += 1
    block(gpsimd=rp_g, vector=rp_v, scalar=rp_s, tensor=rp_pe)

    def nb_s2(s):
        C._nb = {k_: C.semv[k_] for k_ in ("ve", "pe", "sc")}
        s.activation(C.scrb[:, 0:2048], A.qr[:, :], AF.Square, bias=ccol(0.0),
                     scale=1.0)
        s.drain().then_inc(C.sems["sc"], 1)
        C.semv["sc"] += 1
        s.wait_ge(C.sems["pe"], C._nb["pe"] + 4)
        s.activation(C.scrb[:, 0:2048], A.kr[:, :], AF.Square, bias=ccol(0.0),
                     scale=1.0)
        s.drain().then_inc(C.sems["sc"], 1)
        C.semv["sc"] += 1

    def nb_pe2(t):
        base_sc = C.semv["sc"] - 2
        t.wait_ge(C.sems["sc"], base_sc + 1)
        for i_, (qk, bias) in enumerate((("q", 0), ("k", 1))):
            if i_ == 1:
                t.wait_ge(C.sems["sc"], base_sc + 2)
            for sblk in range(4):
                bank = pb[4] if sblk % 2 == 0 else pb[5]
                lin = i_ * 4 + sblk
                if lin >= 2:
                    t.wait_ge(C.sems["ve"], C._nb["ve"] + lin - 1)
                t.matmul(bank[0:2, :], C.rowsel2[:, :],
                         C.scrb[:, sblk * 512:(sblk + 1) * 512],
                         start=True, stop=True, skip_group_check=True)
                t.drain().then_inc(C.sems["pe"], 1)
                C.semv["pe"] += 1

    def nb_ve(v):
        base_pe = C.semv["pe"] - 8
        for i_ in range(2):
            for sblk in range(4):
                bank = pb[4] if sblk % 2 == 0 else pb[5]
                v.wait_ge(C.sems["pe"], base_pe + i_ * 4 + sblk + 1)
                v.tensor_reduce(A.qn[0:2, i_ * 8 + sblk:i_ * 8 + sblk + 1],
                                bank[0:2, :], AX, Op.max)
                v.drain().then_inc(C.sems["ve"], 1)
                C.semv["ve"] += 1
        v.tensor_tensor(A.qn[0:2, 4:5], A.qn[0:2, 0:1], A.qn[0:2, 1:2], Op.max)
        v.tensor_tensor(A.qn[0:2, 5:6], A.qn[0:2, 2:3], A.qn[0:2, 3:4], Op.max)
        v.tensor_tensor(A.qn[0:2, 12:13], A.qn[0:2, 8:9], A.qn[0:2, 9:10], Op.max)
        v.tensor_tensor(A.qn[0:2, 13:14], A.qn[0:2, 10:11], A.qn[0:2, 11:12], Op.max)
        v.drain()
        v.tensor_tensor(A.qn[0:2, 14:15], A.qn[0:2, 4:5], A.qn[0:2, 5:6], Op.max)
        v.tensor_tensor(A.qn[0:2, 15:16], A.qn[0:2, 12:13], A.qn[0:2, 13:14], Op.max)
        v.drain()
        v.tensor_add(A.qn[0:2, 6:7], A.qn[0:2, 14:15], A.qn[0:2, 15:16])
        v.drain()
        v.tensor_scalar(A.qn[0:2, 7:8], A.qn[0:2, 6:7], -1.01 / 16.0, None,
                        Op.mult)
        v.drain().then_inc(C.sems["ve"], 1)
        C.semv["ve"] += 1

    def nb_pe3(t):
        wait(t, "ve")
        t.matmul(pb[6][:, 0:1], A.selABf[0:2, 0:128], A.qn[0:2, 7:8],
                 start=True, stop=True, skip_group_check=True)
        t.matmul(pb[6][:, 1:2], A.selABf[0:2, 128:256], A.qn[0:2, 7:8],
                 start=True, stop=True, skip_group_check=True)
        t.drain().then_inc(C.sems["pe"], 1)
        C.semv["pe"] += 1

    def nb_ve2(v):
        wait(v, "pe")
        v.tensor_copy(A.negC[:, 0:2], pb[6][:, 0:2])
    block(scalar=nb_s2, tensor=nb_pe2, vector=nb_ve)
    block(tensor=nb_pe3)
    block(vector=nb_ve2)
    C.tap(f"qr{b}_{g}", lambda: A.qr[:, :].bitcast(F32), [128, S // 2])
    C.tap(f"kr{b}_{g}", lambda: A.kr[:, :].bitcast(F32), [128, S // 2])
    C.tap(f"negC{b}_{g}", lambda: A.negC[:, 0:2], [128, 2])



PAIR_PRE = [0]
for _ib in range(4):
    PAIR_PRE.append(PAIR_PRE[-1] + 4 * _ib + 4)


def _pair_scores(C, A, b, g):
    nc, block, inc, dinc, wait, ccol = C.nc, C.block, C.inc, C.dinc, C.wait, C.ccol
    pb = C.pb

    def cover_pv(m):
        return 2 * (m + 1) + 2 if (m + 1) < NIT else 2 * NIT + 1

    def sl_pe(t):
        C._sl = {k_: C.semv[k_] for k_ in
                 ("pe2", "ve2", "sc2", "gp2", "ve3", "sy2")}
        b0 = C._sl

        def emit_pv(m):
            ib_, jc_ = PAIR_ITERS[m]
            last = (jc_ == 4 * ib_ + 3)
            if PAIR_DIAG[m]:
                t.wait_ge(C.sems["gp2"], b0["gp2"] + PAIR_DC[m])
            else:
                t.wait_ge(C.sems["ve2"], b0["ve2"] + m + 1)
            if jc_ == 0:
                t.wait_ge(C.sems["ve3"], b0["ve3"] + ib_)
            pA, pB = A.ptA[m % 2], A.ptB[m % 2]
            j0 = jc_ * 128
            t.matmul(pb[6][0:64, :], A.vp[:, j0:j0 + 64], pA[:, :],
                     start=(jc_ == 0), stop=last, skip_group_check=True)
            t.matmul(pb[6][64:128, :], A.vp[:, j0 + 64:j0 + 128], pB[:, :],
                     start=(jc_ == 0), stop=last, skip_group_check=True)
            t.matmul(pb[7][0:1, :], C.ones_bf[:, 0:1], pA[:, :],
                     start=(jc_ == 0), stop=last, skip_group_check=True)
            t.matmul(pb[7][64:65, :], C.ones_bf[:, 0:1], pB[:, :],
                     start=(jc_ == 0), stop=last, skip_group_check=True)

        for m, (ib_, jc_) in enumerate(PAIR_ITERS):
            iA, iB = pb[0 + m % 2], pb[2 + m % 2]
            j0, i0 = jc_ * 128, ib_ * 512
            if m >= 2:
                t.wait_ge(C.sems["sc2"], b0["sc2"] + m - 1)
            t.matmul(iA[:, :], A.kr[0:64, j0:j0 + 128],
                     A.qr[0:64, i0:i0 + 512], start=True, stop=True,
                     skip_group_check=True)
            t.matmul(iB[:, :], A.kr[64:128, j0:j0 + 128],
                     A.qr[64:128, i0:i0 + 512], start=True, stop=True,
                     skip_group_check=True)
            t.drain().then_inc(C.sems["pe2"], 1)
            if m >= 1:
                t.wait_ge(C.sems["ve2"], b0["ve2"] + m)
            t.matmul(pb[4][:, :], A.strip[0:5, j0:j0 + 128],
                     A.strip[0:5, i0:i0 + 512], start=True, stop=True,
                     skip_group_check=True)
            t.matmul(pb[5][:, :], A.strip[32:37, j0:j0 + 128],
                     A.strip[32:37, i0:i0 + 512], start=True, stop=True,
                     skip_group_check=True)
            if m >= 1:
                emit_pv(m - 1)
            t.drain().then_inc(C.sems["pe2"], 1)
        emit_pv(NIT - 1)
        t.drain().then_inc(C.sems["pe2"], 1)
        for ib_ in range(4):
            t.wait_ge(C.sems["ve3"], b0["ve3"] + 5 + ib_)
            t.matmul(pb[7][:, :], C.rsel[ib_][:, :], A.sstg2[0:8, :],
                     start=True, stop=True, skip_group_check=True)
            t.drain().then_inc(C.sems["pe2"], 1)
        C.semv["pe2"] += 2 * NIT + 1 + 4

    def sl_sc(s):
        b0 = C._sl
        for m in range(NIT):
            s.wait_ge(C.sems["pe2"], b0["pe2"] + 2 * m + 1)
            if m >= 2:
                s.wait_ge(C.sems["ve2"], b0["ve2"] + m - 1)
            s.activation(A.eA[m % 2][:, :], pb[0 + m % 2][:, :], AF.Exp,
                         bias=A.negC[:, 0:1], scale=0.125)
            s.activation(A.eB[m % 2][:, :], pb[2 + m % 2][:, :], AF.Exp,
                         bias=A.negC[:, 1:2], scale=0.125)
            s.drain().then_inc(C.sems["sc2"], 1)
        C.semv["sc2"] += NIT

    def sl_ve(v):
        b0 = C._sl

        def epilogue(ib_):
            m_end = PAIR_PRE[ib_ + 1] - 1
            v.wait_ge(C.sems["pe2"], b0["pe2"] + cover_pv(m_end))
            v.tensor_copy(A.outev[:, ib_ * 512:(ib_ + 1) * 512], pb[6][:, :])
            v.tensor_copy(A.cscr[:, ib_ * 512:(ib_ + 1) * 512], pb[7][:, :])
            v.drain().then_inc(C.sems["ve3"], 1)

        epi_at = {PAIR_PRE[i_ + 1]: i_ for i_ in range(3)}
        for m in range(NIT):
            v.wait_ge(C.sems["sc2"], b0["sc2"] + m + 1)
            v.wait_ge(C.sems["pe2"], b0["pe2"] + 2 * m + 2)
            v.tensor_tensor(A.ptA[m % 2][:, :], A.eA[m % 2][:, :], pb[4][:, :],
                            Op.mult)
            v.tensor_tensor(A.ptB[m % 2][:, :], A.eB[m % 2][:, :], pb[5][:, :],
                            Op.mult)
            v.drain().then_inc(C.sems["ve2"], 1)
            if m in epi_at:
                epilogue(epi_at[m])
        C.semv["ve2"] += NIT
        epilogue(3)
        v.wait_ge(C.sems["sy2"], b0["sy2"] + 1)
        v.reciprocal(A.sstg2[0:8, :], A.sstg[0:8, :])
        v.drain().then_inc(C.sems["ve3"], 1)
        for ib_ in range(4):
            v.wait_ge(C.sems["pe2"], b0["pe2"] + 2 * NIT + 1 + ib_ + 1)
            v.tensor_tensor(A.attp[:, ib_ * 512:(ib_ + 1) * 512],
                            A.outev[:, ib_ * 512:(ib_ + 1) * 512], pb[7][:, :],
                            Op.mult)
            v.drain().then_inc(C.sems["ve3"], 1)
        C.semv["ve3"] += 9

    def sl_gp(g_):
        b0 = C._sl
        for m in range(NIT):
            if not PAIR_DIAG[m]:
                continue
            ib_, jc_ = PAIR_ITERS[m]
            mi = jc_ - 4 * ib_
            g_.wait_ge(C.sems["ve2"], b0["ve2"] + m + 1)
            g_.tensor_tensor(A.ptA[m % 2][:, :], A.ptA[m % 2][:, :],
                             C.masks[:, mi * 512:(mi + 1) * 512], Op.mult)
            g_.tensor_tensor(A.ptB[m % 2][:, :], A.ptB[m % 2][:, :],
                             C.masks[:, mi * 512:(mi + 1) * 512], Op.mult)
            g_.drain().then_inc(C.sems["gp2"], 1)
        C.semv["gp2"] += PAIR_DC[-1]

    def sl_sy(sy):
        b0 = C._sl
        st0 = C.semv["st"]
        sy.wait_ge(C.sems["ve3"], b0["ve3"] + 4)
        nst = 0
        for h_ in range(2):
            for ib_ in range(4):
                dinc(sy.dma_start(A.sstg[ib_ * 2 + h_:ib_ * 2 + h_ + 1, :],
                                  A.cscr[h_ * 64:h_ * 64 + 1,
                                         ib_ * 512:(ib_ + 1) * 512]), "st")
                nst += 1
        sy.wait_ge(C.sems["st"], st0 + 16 * nst)
        sy.drain().then_inc(C.sems["sy2"], 1)
        C.semv["sy2"] += 1
        sy.wait_ge(C.sems["ve3"], b0["ve3"] + 9)
        dinc(sy.dma_start(C.attno_h[g, :, b * S:(b + 1) * S], A.attp[:, :]),
             "st")
    block(tensor=sl_pe, scalar=sl_sc, vector=sl_ve, gpsimd=sl_gp, sync=sl_sy)
    C.tap(f"attp{b}_{g}", lambda: A.attp[:, :].bitcast(F32), [128, S // 2])


def _wo_ln1_quant(C):
    nc, block, inc, dinc, wait, ccol = C.nc, C.block, C.inc, C.dinc, C.wait, C.ccol
    pb = C.pb
    scope = ExitStack()
    e = scope.enter_context
    wosb = e(nc.sbuf_tensor("wosb", [128, 8 * E], BF16))
    attld = e(nc.sbuf_tensor("attld", [128, 8 * 512], BF16))
    h1f = e(nc.sbuf_tensor("h1f", [128, 8 * 512], F32))
    h1b = e(nc.sbuf_tensor("h1b", [128, 8 * 512], BF16))
    sqb = e(nc.sbuf_tensor("sqb", [128, 8 * 512], BF16))
    xlnb = e(nc.sbuf_tensor("xlnb", [128, 8 * 512], BF16))
    xqb = e(nc.sbuf_tensor("xqb", [128, 8 * 512], BF16))
    amcol = e(nc.sbuf_tensor("amcol", [128, 16], F32))
    rows = e(nc.sbuf_tensor("rows", [1, 8 * 512], F32))
    rowe = e(nc.sbuf_tensor("rowe", [8, 128], F32))
    C.invsx_h = C.dram("invsx_sc", [8, 512], F32, "Internal")

    def wo_ld(g):
        for c in range(8):
            dinc(g.dma_start(wosb[:, c * E:(c + 1) * E],
                             C.wo_h[c * 128:(c + 1) * 128, :]))
        wait(g, "ld")
    block(gpsimd=wo_ld)

    for tb in range(8):
        t0 = tb * 512
        def a_g(g, t0=t0):
            dinc(g.dma_start(attld[:, :].rearrange("p (ic t) -> p ic t", t=512),
                             C.attno_h[:, :, t0:t0 + 512].rearrange(
                                 "ic p t -> p ic t")))
            wait(g, "ld")
            g.drain().then_inc(C.sems["gp"], 1)
            C.semv["gp"] += 1

        def a_pe(t, t0=t0):
            wait(t, "gp")
            for oc in range(8):
                for ic in range(8):
                    t.matmul(pb[oc][:, :],
                             wosb[:, ic * E + oc * 128:ic * E + oc * 128 + 128],
                             attld[:, ic * 512:(ic + 1) * 512],
                             start=(ic == 0), stop=(ic == 7))
        block(gpsimd=a_g, tensor=a_pe)

        def b_v(v, t0=t0):
            C._p4b = C.semv["ve"]
            for oc in range(8):
                v.tensor_tensor(h1f[:, oc * 512:(oc + 1) * 512],
                                C.xfm[:, oc * NT + t0:oc * NT + t0 + 512],
                                pb[oc][:, :], Op.add)
                v.drain()
                v.tensor_copy(h1b[:, oc * 512:(oc + 1) * 512],
                              h1f[:, oc * 512:(oc + 1) * 512])
                v.drain().then_inc(C.sems["ve"], 1)
                C.semv["ve"] += 1

        def b_s(s, t0=t0):
            for oc in range(8):
                s.wait_ge(C.sems["ve"], C._p4b + oc + 1)
                s.activation(sqb[:, oc * 512:(oc + 1) * 512],
                             h1f[:, oc * 512:(oc + 1) * 512], AF.Square,
                             bias=ccol(0.0), scale=1.0)
                s.drain().then_inc(C.sems["sc"], 1)
                C.semv["sc"] += 1
        block(vector=b_v, scalar=b_s)

        def c_pe(t, t0=t0):
            for oc in range(8):
                t.matmul(pb[0][0:1, :], C.ones_bf[:, 0:1],
                         h1b[:, oc * 512:(oc + 1) * 512],
                         start=(oc == 0), stop=(oc == 7), skip_group_check=True)
                t.matmul(pb[1][0:1, :], C.ones_bf[:, 0:1],
                         sqb[:, oc * 512:(oc + 1) * 512],
                         start=(oc == 0), stop=(oc == 7), skip_group_check=True)
            t.drain().then_inc(C.sems["pe"], 1)
            C.semv["pe"] += 1

        def c_v(v, t0=t0):
            wait(v, "pe")
            v.tensor_scalar(rows[0:1, 0:512], pb[0][0:1, :], 1.0 / E, None,
                            Op.mult)
            v.tensor_scalar(rows[0:1, 512:1024], pb[1][0:1, :], 1.0 / E, None,
                            Op.mult)
            for q in range(4):
                v.tensor_tensor(C.scr[:, q * 512:(q + 1) * 512],
                                h1f[:, 2 * q * 512:(2 * q + 1) * 512],
                                h1f[:, (2 * q + 1) * 512:(2 * q + 2) * 512],
                                Op.max)
                v.tensor_tensor(C.scr2[:, q * 512:(q + 1) * 512],
                                h1f[:, 2 * q * 512:(2 * q + 1) * 512],
                                h1f[:, (2 * q + 1) * 512:(2 * q + 2) * 512],
                                Op.min)
            v.drain()
            v.tensor_tensor(rows[0:1, 1024:1536], rows[0:1, 0:512],
                            rows[0:1, 0:512], Op.mult)
            v.tensor_tensor(C.scr[:, 0:512], C.scr[:, 0:512],
                            C.scr[:, 512:1024], Op.max)
            v.tensor_tensor(C.scr[:, 1024:1536], C.scr[:, 1024:1536],
                            C.scr[:, 1536:2048], Op.max)
            v.tensor_tensor(C.scr2[:, 0:512], C.scr2[:, 0:512],
                            C.scr2[:, 512:1024], Op.min)
            v.tensor_tensor(C.scr2[:, 1024:1536], C.scr2[:, 1024:1536],
                            C.scr2[:, 1536:2048], Op.min)
            v.drain()
            v.tensor_sub(rows[0:1, 1024:1536], rows[0:1, 512:1024],
                         rows[0:1, 1024:1536])
            v.tensor_tensor(C.scr[:, 0:512], C.scr[:, 0:512],
                            C.scr[:, 1024:1536], Op.max)
            v.tensor_tensor(C.scr2[:, 0:512], C.scr2[:, 0:512],
                            C.scr2[:, 1024:1536], Op.min)
            v.drain().then_inc(C.sems["ve"], 1)
            C.semv["ve"] += 1

        def c_s(s, t0=t0):
            wait(s, "ve")
            s.activation(rows[0:1, 1536:2048], rows[0:1, 1024:1536], AF.Sqrt,
                         bias=ccol(EPS_LN)[0:1, :], scale=1.0)
            s.drain().then_inc(C.sems["sc"], 1)
            C.semv["sc"] += 1

        def c_pe2(t, t0=t0):
            wait(t, "ve")
            for q in range(4):
                t.transpose(pb[2][:, q * 128:(q + 1) * 128],
                            C.scr[:, q * 128:(q + 1) * 128], C.idn_f[:, :])
                t.transpose(pb[3][:, q * 128:(q + 1) * 128],
                            C.scr2[:, q * 128:(q + 1) * 128], C.idn_f[:, :])
            t.drain().then_inc(C.sems["pe"], 1)
            C.semv["pe"] += 1

        def c_v2(v, t0=t0):
            wait(v, "pe")
            wait(v, "sc")
            for q in range(4):
                v.tensor_reduce(amcol[:, q:q + 1],
                                pb[2][:, q * 128:(q + 1) * 128], AX, Op.max)
                v.tensor_reduce(amcol[:, 4 + q:5 + q],
                                pb[3][:, q * 128:(q + 1) * 128], AX, Op.min)
            v.reciprocal(rows[0:1, 1024:1536], rows[0:1, 1536:2048])
            v.drain()
            v.tensor_tensor(rows[0:1, 1536:2048], rows[0:1, 0:512],
                            rows[0:1, 1024:1536], Op.mult)
            v.drain().then_inc(C.sems["ve"], 1)
            C.semv["ve"] += 1

        def c_pe3(t, t0=t0):
            wait(t, "ve")
            t.transpose(pb[4][0:8, 0:128], amcol[:, 0:8], C.idn_f[:, :])
            t.drain().then_inc(C.sems["pe"], 1)
            C.semv["pe"] += 1

        def c_v3(v, t0=t0):
            wait(v, "pe")
            v.tensor_copy(rowe[0:8, :], pb[4][0:8, 0:128])
            v.drain().then_inc(C.sems["ve"], 1)
            C.semv["ve"] += 1

        def c_sy(sy, tb=tb):
            wait(sy, "ve")
            for q in range(4):
                dinc(sy.dma_start(C.scr[0:1, 1024 + q * 128:1152 + q * 128],
                                  rowe[q:q + 1, :]), "st")
                dinc(sy.dma_start(C.scr[0:1, 1536 + q * 128:1664 + q * 128],
                                  rowe[4 + q:5 + q, :]), "st")
            wait(sy, "st")
            sy.drain().then_inc(C.sems["sy"], 1)
            C.semv["sy"] += 1

        def c_v4(v, tb=tb):
            wait(v, "sy")
            v.tensor_sub(C.scr2[0:1, 0:512], C.scr[0:1, 1024:1536],
                         rows[0:1, 0:512])
            v.tensor_sub(C.scr2[0:1, 512:1024], rows[0:1, 0:512],
                         C.scr[0:1, 1536:2048])
            v.drain()
            v.tensor_tensor(C.scr2[0:1, 0:512], C.scr2[0:1, 0:512],
                            C.scr2[0:1, 512:1024], Op.max)
            v.drain()
            v.tensor_tensor(C.scr2[0:1, 0:512], C.scr2[0:1, 0:512],
                            rows[0:1, 1024:1536], Op.mult)
            v.drain()
            v.tensor_scalar(C.scr2[0:1, 0:512], C.scr2[0:1, 0:512], 1e-5,
                            None, Op.max)
            v.drain()
            v.reciprocal(C.scr2[0:1, 512:1024], C.scr2[0:1, 0:512])
            v.drain()
            v.tensor_scalar(C.scr[0:1, 1024:1536], C.scr2[0:1, 512:1024],
                            127.0, None, Op.mult)
            v.tensor_scalar(C.scr[0:1, 1536:2048], C.scr2[0:1, 0:512],
                            1.0 / 127.0, None, Op.mult)
            v.drain().then_inc(C.sems["ve"], 1)
            C.semv["ve"] += 1

        def c_sy2(sy, tb=tb):
            wait(sy, "ve")
            dinc(sy.dma_start(C.invsx_h[tb, :], C.scr[0:1, 1536:2048]), "st")
            wait(sy, "st")
        block(tensor=c_pe, vector=c_v)
        block(scalar=c_s, tensor=c_pe2)
        block(vector=c_v2)
        block(tensor=c_pe3)
        block(vector=c_v3)
        block(sync=c_sy)
        block(vector=c_v4)
        block(sync=c_sy2)

        def d_pe(t, t0=t0):
            wait(t, "sy")
            t.matmul(pb[2][:, :], C.onerow_f[0:1, :], rows[0:1, 1024:1536],
                     start=True, stop=True, skip_group_check=True)
            t.matmul(pb[3][:, :], C.onerow_f[0:1, :], rows[0:1, 1536:2048],
                     start=True, stop=True, skip_group_check=True)
            t.matmul(pb[4][:, :], C.onerow_f[0:1, :], C.scr[0:1, 1024:1536],
                     start=True, stop=True, skip_group_check=True)
            t.drain().then_inc(C.sems["pe"], 1)
            C.semv["pe"] += 1

        def d_v(v, t0=t0):
            wait(v, "pe")
            for oc in range(8):
                sl = slice(oc * 512, (oc + 1) * 512)
                v.tensor_tensor(C.scr[:, 0:512], h1f[:, sl], pb[2][:, :],
                                Op.mult)
                v.drain()
                v.tensor_sub(C.scr[:, 0:512], C.scr[:, 0:512], pb[3][:, :])
                v.drain()
                v.tensor_copy(xlnb[:, sl], C.scr[:, 0:512])
                v.tensor_tensor(C.scr2[:, 0:512], C.scr[:, 0:512], pb[4][:, :],
                                Op.mult)
                v.drain()
                v.tensor_scalar(C.scr2[:, 0:512], C.scr2[:, 0:512],
                                C.disc[:, 2:3], None, Op.add)
                v.drain().then_inc(C.sems["ve"], 1)
                C.semv["ve"] += 1
                v.wait_ge(C.sems["gp"], C._p4g + oc + 1)
                v.tensor_copy(xqb[:, sl], C.scri[:, 0:512])
                v.drain().then_inc(C.sems["ve"], 1)
                C.semv["ve"] += 1

        def d_g(g, t0=t0):
            C._p4g = C.semv["gp"]
            b_ve = C.semv["ve"]
            for oc in range(8):
                g.wait_ge(C.sems["ve"], b_ve + 2 * oc + 1)
                g.tensor_copy(C.scri[:, 0:512], C.scr2[:, 0:512])
                g.drain().then_inc(C.sems["gp"], 1)
                C.semv["gp"] += 1

        def d_sy(sy, tb=tb, t0=t0):
            wait(sy, "ve")
            dinc(sy.dma_start(C.xln_h[:, :, t0:t0 + 512].rearrange(
                "ic p t -> p ic t"), xlnb[:, :].rearrange(
                "p (ic t) -> p ic t", t=512)), "st")
            dinc(sy.dma_start(C.xq_h[:, :, t0:t0 + 512].rearrange(
                "ic p t -> p ic t"), xqb[:, :].rearrange(
                "p (ic t) -> p ic t", t=512)), "st")
            wait(sy, "st")
        block(tensor=d_pe, gpsimd=d_g, vector=d_v)
        block(sync=d_sy)
        C.tap(f"xln{tb}", lambda t0=t0: xlnb[:, :].bitcast(F32), [128, 4 * 512])
        C.tap(f"xq{tb}", lambda t0=t0: xqb[:, :].bitcast(F32), [128, 4 * 512])
        C.tap(f"h1f{tb}", lambda t0=t0: h1f[:, :], [128, 8 * 512])
    scope.close()


def _ffn(C):
    nc, block, inc, dinc, wait, ccol = C.nc, C.block, C.inc, C.dinc, C.wait, C.ccol
    pb = C.pb
    C._xfm_cm.__exit__(None, None, None)
    scope = ExitStack()
    e = scope.enter_context
    hblk = e(nc.sbuf_tensor("hblk", [128, 32 * 512], F32))
    hqb = e(nc.sbuf_tensor("hqb", [128, 32 * 512], BF16))
    xqld = e(nc.sbuf_tensor("xqld", [128, 8 * 512], BF16))
    sq2 = xqld
    xlnld = e(nc.sbuf_tensor("xlnld", [128, 8 * 512], BF16))
    wgst = [e(nc.sbuf_tensor(f"wgst{i}", [128, 1024], BF16)) for i in range(2)]
    wust = [e(nc.sbuf_tensor(f"wust{i}", [128, 1024], BF16)) for i in range(2)]
    wdst = [e(nc.sbuf_tensor(f"wdst{i}", [128, 4096], BF16)) for i in range(2)]
    sgf = e(nc.sbuf_tensor("sgf", [128, 512], F32))
    t2f = e(nc.sbuf_tensor("t2f", [128, 512], F32))
    ibc = e(nc.sbuf_tensor("ibc", [128, 512], F32))
    sbc = e(nc.sbuf_tensor("sbc", [128, 512], F32))
    dbc = e(nc.sbuf_tensor("dbc", [128, 512], F32))
    frow = e(nc.sbuf_tensor("frow", [1, 2 * 512], F32))
    amcol = e(nc.sbuf_tensor("amcol2", [128, 16], F32))
    rowe = e(nc.sbuf_tensor("rowe2", [8, 128], F32))
    h2f = hblk[:, 4096:8192]
    h2b = hblk.bitcast(BF16)[:, 16384:20480]
    outf = hblk[:, 0:4096]
    otm = e(nc.sbuf_tensor("otm", [128, 2 * E], F32))

    def f0_v(v):
        v.tensor_scalar(C.rsg[:, :], C.rsg[:, :], C.disc[:, 4:5], -1.0,
                        Op.mult, Op.mult)
        v.reciprocal(C.wqs[:, 3:6], C.wqs[:, 0:3])
        v.drain().then_inc(C.sems["ve"], 1)
        C.semv["ve"] += 1
    block(vector=f0_v)

    for tb in range(8):
        t0 = tb * 512
        def a_g(g, tb=tb, t0=t0):
            C._f5 = {k_: C.semv[k_] for k_ in ("ld2", "gp", "ve", "sc", "pe")}
            dinc(g.dma_start(xqld[:, :].rearrange("p (ic t) -> p ic t", t=512),
                             C.xq_h[:, :, t0:t0 + 512].rearrange(
                                 "ic p t -> p ic t")), "ld2")
            dinc(g.dma_start(C.scr[0:1, 0:512], C.invsx_h[tb, :]), "ld2")
            g.wait_ge(C.sems["ld2"], C._f5["ld2"] + 32)
            for icn in range(32):
                if icn >= 2:
                    g.wait_ge(C.sems["pe"], C._f5["pe"] + icn)
                if icn >= 1:
                    g.wait_ge(C.sems["ld2"], C._f5["ld2"] + 32 + 32 * icn)
                dinc(g.dma_start(wgst[icn % 2][:, :].rearrange(
                    "p (ic c) -> p ic c", c=128),
                    C.wgT_h[icn, :, :].rearrange("(ic p) c -> p ic c", p=128)),
                    "ld2")
                dinc(g.dma_start(wust[icn % 2][:, :].rearrange(
                    "p (ic c) -> p ic c", c=128),
                    C.wuT_h[icn, :, :].rearrange("(ic p) c -> p ic c", p=128)),
                    "ld2")
                g.drain().then_inc(C.sems["gp"], 1)
                C.semv["gp"] += 1

        def a_pe(t, tb=tb, t0=t0):
            b5 = C._f5
            t.wait_ge(C.sems["ld2"], b5["ld2"] + 32)
            t.matmul(pb[4][:, :], C.onerow_f[0:1, :], C.scr[0:1, 0:512],
                     start=True, stop=True, skip_group_check=True)
            t.drain().then_inc(C.sems["pe"], 1)
            C.semv["pe"] += 1
            for icn in range(32):
                t.wait_ge(C.sems["ld2"], b5["ld2"] + 32 + 32 * (icn + 1))
                if icn >= 2:
                    t.wait_ge(C.sems["ve"], b5["ve"] + 1 + 2 * (icn - 1))
                bg, bu = pb[icn % 2], pb[2 + icn % 2]
                for ic in range(8):
                    t.matmul(bg[:, :], wgst[icn % 2][:, ic * 128:(ic + 1) * 128],
                             xqld[:, ic * 512:(ic + 1) * 512],
                             start=(ic == 0), stop=(ic == 7))
                    t.matmul(bu[:, :], wust[icn % 2][:, ic * 128:(ic + 1) * 128],
                             xqld[:, ic * 512:(ic + 1) * 512],
                             start=(ic == 0), stop=(ic == 7))
                t.drain().then_inc(C.sems["pe"], 1)
                C.semv["pe"] += 1

        def a_v(v, tb=tb, t0=t0):
            b5 = C._f5
            v.wait_ge(C.sems["pe"], b5["pe"] + 1)
            v.tensor_copy(ibc[:, :], pb[4][:, :])
            v.drain().then_inc(C.sems["ve"], 1)
            C.semv["ve"] += 1
            for icn in range(32):
                sl = slice(icn * 512, (icn + 1) * 512)
                v.wait_ge(C.sems["pe"], b5["pe"] + icn + 2)
                v.scalar_tensor_tensor(C.scr2[:, 0:512], pb[icn % 2][:, :],
                                       C.rsg[:, icn:icn + 1], ibc[:, :],
                                       Op.add, Op.mult)
                v.drain().then_inc(C.sems["ve"], 1)
                C.semv["ve"] += 1
                v.wait_ge(C.sems["sc"], b5["sc"] + icn + 1)
                v.scalar_tensor_tensor(t2f[:, :], pb[2 + icn % 2][:, :],
                                       C.rsg[:, 32 + icn:33 + icn], ibc[:, :],
                                       Op.add, Op.mult)
                v.drain()
                v.tensor_tensor(hblk[:, sl], sgf[:, :], t2f[:, :], Op.mult)
                v.drain()
                if icn == 0:
                    v.tensor_copy(C.scr[:, 512:1024], hblk[:, sl])
                    v.tensor_copy(C.scr[:, 1024:1536], hblk[:, sl])
                else:
                    v.tensor_tensor(C.scr[:, 512:1024], C.scr[:, 512:1024],
                                    hblk[:, sl], Op.max)
                    v.tensor_tensor(C.scr[:, 1024:1536], C.scr[:, 1024:1536],
                                    hblk[:, sl], Op.min)
                v.drain().then_inc(C.sems["ve"], 1)
                C.semv["ve"] += 1

        def a_s(s, tb=tb, t0=t0):
            b5 = C._f5
            for icn in range(32):
                s.wait_ge(C.sems["ve"], b5["ve"] + 2 + 2 * icn)
                s.activation(sgf[:, :], C.scr2[:, 0:512], AF.Silu,
                             bias=ccol(0.0), scale=C.wqs[:, 3:4])
                s.drain().then_inc(C.sems["sc"], 1)
                C.semv["sc"] += 1
        block(gpsimd=a_g, tensor=a_pe, vector=a_v, scalar=a_s)

        def bb_pe(t, tb=tb):
            for q in range(4):
                t.transpose(pb[2][:, q * 128:(q + 1) * 128],
                            C.scr[:, 512 + q * 128:640 + q * 128], C.idn_f[:, :])
                t.transpose(pb[3][:, q * 128:(q + 1) * 128],
                            C.scr[:, 1024 + q * 128:1152 + q * 128],
                            C.idn_f[:, :])
            t.drain().then_inc(C.sems["pe"], 1)
            C.semv["pe"] += 1

        def bb_v(v, tb=tb):
            wait(v, "pe")
            for q in range(4):
                v.tensor_reduce(amcol[:, q:q + 1],
                                pb[2][:, q * 128:(q + 1) * 128], AX, Op.max)
                v.tensor_reduce(amcol[:, 4 + q:5 + q],
                                pb[3][:, q * 128:(q + 1) * 128], AX, Op.min)
            v.drain()
            v.tensor_scalar(amcol[:, 4:8], amcol[:, 4:8], -1.0, None, Op.mult)
            v.drain()
            v.tensor_tensor(amcol[:, 0:4], amcol[:, 0:4], amcol[:, 4:8], Op.max)
            v.drain()
            v.tensor_scalar(amcol[:, 8:12], amcol[:, 0:4], C.wqs[:, 4:5],
                            None, Op.mult)
            v.drain()
            v.tensor_scalar(amcol[:, 8:12], amcol[:, 8:12], 1e-5, None, Op.max)
            v.drain()
            v.reciprocal(amcol[:, 12:16], amcol[:, 8:12])
            v.drain()
            v.tensor_scalar(amcol[:, 0:4], amcol[:, 12:16], 127.0,
                            C.wqs[:, 4:5], Op.mult, Op.mult)
            v.tensor_scalar(amcol[:, 4:8], amcol[:, 8:12], C.wqs[:, 5:6],
                            1.0 / 127.0, Op.mult, Op.mult)
            v.drain().then_inc(C.sems["ve"], 1)
            C.semv["ve"] += 1

        def bb_pe2(t, tb=tb):
            wait(t, "ve")
            t.transpose(pb[4][0:8, 0:128], amcol[:, 0:8], C.idn_f[:, :])
            t.drain().then_inc(C.sems["pe"], 1)
            C.semv["pe"] += 1

        def bb_v2(v, tb=tb):
            wait(v, "pe")
            v.tensor_copy(rowe[0:8, :], pb[4][0:8, 0:128])
            v.drain().then_inc(C.sems["ve"], 1)
            C.semv["ve"] += 1

        def bb_sy(sy, tb=tb):
            wait(sy, "ve")
            st0 = C.semv["st"]
            for q in range(4):
                dinc(sy.dma_start(frow[0:1, q * 128:(q + 1) * 128],
                                  rowe[q:q + 1, :]), "st")
                dinc(sy.dma_start(frow[0:1, 512 + q * 128:640 + q * 128],
                                  rowe[4 + q:5 + q, :]), "st")
            wait(sy, "st")
            sy.drain().then_inc(C.sems["sy"], 1)
            C.semv["sy"] += 1
        block(tensor=bb_pe)
        block(vector=bb_v)
        block(tensor=bb_pe2)
        block(vector=bb_v2)
        block(sync=bb_sy)

        def cc_pe(t, tb=tb):
            wait(t, "sy")
            t.matmul(pb[4][:, :], C.onerow_f[0:1, :], frow[0:1, 0:512],
                     start=True, stop=True, skip_group_check=True)
            t.drain().then_inc(C.sems["pe"], 1)
            C.semv["pe"] += 1

        def cc_v(v, tb=tb):
            wait(v, "pe")
            v.tensor_copy(sbc[:, :], pb[4][:, :])
            v.drain()
            for icn in range(32):
                sl = slice(icn * 512, (icn + 1) * 512)
                v.tensor_tensor(C.scr2[:, 0:512], hblk[:, sl], sbc[:, :],
                                Op.mult)
                v.drain()
                v.tensor_scalar(C.scr2[:, 0:512], C.scr2[:, 0:512],
                                C.disc[:, 2:3], None, Op.add)
                v.drain().then_inc(C.sems["ve"], 1)
                C.semv["ve"] += 1
                v.wait_ge(C.sems["gp"], C._f5cg + icn + 1)
                v.tensor_copy(hqb[:, sl], C.scri[:, 512:1024])
                v.drain().then_inc(C.sems["ve"], 1)
                C.semv["ve"] += 1

        def cc_g(g, tb=tb):
            C._f5cg = C.semv["gp"]
            b_ve = C.semv["ve"]
            for icn in range(32):
                g.wait_ge(C.sems["ve"], b_ve + 2 * icn + 1)
                g.tensor_copy(C.scri[:, 512:1024], C.scr2[:, 0:512])
                g.drain().then_inc(C.sems["gp"], 1)
                C.semv["gp"] += 1
        block(tensor=cc_pe, gpsimd=cc_g, vector=cc_v)

        def dd_g(g, tb=tb, t0=t0):
            C._f5d = {k_: C.semv[k_] for k_ in ("ld2", "pe", "ve", "sc")}
            dinc(g.dma_start(xlnld[:, :].rearrange("p (ic t) -> p ic t", t=512),
                             C.xln_h[:, :, t0:t0 + 512].rearrange(
                                 "ic p t -> p ic t")), "ld2")
            for oc in range(8):
                if oc >= 2:
                    g.wait_ge(C.sems["pe"], C._f5d["pe"] + oc)
                if oc >= 1:
                    g.wait_ge(C.sems["ld2"], C._f5d["ld2"] + 16 + 16 * oc)
                dinc(g.dma_start(wdst[oc % 2][:, :].rearrange(
                    "p (ic c) -> p ic c", c=128),
                    C.wdT_h[oc, :, :].rearrange("(ic p) c -> p ic c", p=128)),
                    "ld2")
                g.drain().then_inc(C.sems["gp"], 1)
                C.semv["gp"] += 1

        def dd_pe(t, tb=tb, t0=t0):
            b5 = C._f5d
            t.matmul(pb[4][:, :], C.onerow_f[0:1, :], frow[0:1, 512:1024],
                     start=True, stop=True, skip_group_check=True)
            t.drain().then_inc(C.sems["pe"], 1)
            C.semv["pe"] += 1
            for oc in range(8):
                t.wait_ge(C.sems["ld2"], b5["ld2"] + 16 + 16 * (oc + 1))
                if oc >= 2:
                    t.wait_ge(C.sems["ve"], b5["ve"] + 1 + oc - 1)
                bd = pb[oc % 2]
                for icn in range(32):
                    t.matmul(bd[:, :], wdst[oc % 2][:, icn * 128:(icn + 1) * 128],
                             hqb[:, icn * 512:(icn + 1) * 512],
                             start=(icn == 0), stop=(icn == 31))
                t.drain().then_inc(C.sems["pe"], 1)
                C.semv["pe"] += 1

        def dd_v(v, tb=tb, t0=t0):
            b5 = C._f5d
            v.wait_ge(C.sems["pe"], b5["pe"] + 1)
            v.tensor_copy(dbc[:, :], pb[4][:, :])
            v.drain().then_inc(C.sems["ve"], 1)
            C.semv["ve"] += 1
            for oc in range(8):
                sl = slice(oc * 512, (oc + 1) * 512)
                v.wait_ge(C.sems["pe"], b5["pe"] + oc + 2)
                v.scalar_tensor_tensor(C.scr2[:, 0:512], pb[oc % 2][:, :],
                                       C.rsg[:, 64 + oc:65 + oc], dbc[:, :],
                                       Op.add, Op.mult)
                v.drain()
                v.tensor_tensor(h2f[:, sl], xlnld[:, oc * 512:(oc + 1) * 512],
                                C.scr2[:, 0:512], Op.add)
                v.drain()
                v.tensor_copy(h2b[:, sl], h2f[:, sl])
                v.drain().then_inc(C.sems["ve"], 1)
                C.semv["ve"] += 1

        def dd_s(s, tb=tb):
            b5 = C._f5d
            for oc in range(8):
                s.wait_ge(C.sems["ve"], b5["ve"] + oc + 2)
                s.activation(sq2[:, oc * 512:(oc + 1) * 512],
                             h2f[:, oc * 512:(oc + 1) * 512], AF.Square,
                             bias=ccol(0.0), scale=1.0)
                s.drain().then_inc(C.sems["sc"], 1)
                C.semv["sc"] += 1
        block(gpsimd=dd_g, tensor=dd_pe, vector=dd_v, scalar=dd_s)

        def ee_pe(t, tb=tb):
            for oc in range(8):
                t.matmul(pb[2][0:1, :], C.ones_bf[:, 0:1],
                         h2b[:, oc * 512:(oc + 1) * 512],
                         start=(oc == 0), stop=(oc == 7), skip_group_check=True)
                t.matmul(pb[3][0:1, :], C.ones_bf[:, 0:1],
                         sq2[:, oc * 512:(oc + 1) * 512],
                         start=(oc == 0), stop=(oc == 7), skip_group_check=True)
            t.drain().then_inc(C.sems["pe"], 1)
            C.semv["pe"] += 1

        def ee_v(v, tb=tb):
            wait(v, "pe")
            v.tensor_scalar(C.scr[0:1, 0:512], pb[2][0:1, :], 1.0 / E, None,
                            Op.mult)
            v.tensor_scalar(C.scr[0:1, 512:1024], pb[3][0:1, :], 1.0 / E, None,
                            Op.mult)
            v.drain()
            v.tensor_tensor(C.scr[0:1, 1024:1536], C.scr[0:1, 0:512],
                            C.scr[0:1, 0:512], Op.mult)
            v.drain()
            v.tensor_sub(C.scr[0:1, 1024:1536], C.scr[0:1, 512:1024],
                         C.scr[0:1, 1024:1536])
            v.drain().then_inc(C.sems["ve"], 1)
            C.semv["ve"] += 1

        def ee_s(s, tb=tb):
            wait(s, "ve")
            s.activation(C.scr[0:1, 1536:2048], C.scr[0:1, 1024:1536], AF.Sqrt,
                         bias=ccol(EPS_LN)[0:1, :], scale=1.0)
            s.drain().then_inc(C.sems["sc"], 1)
            C.semv["sc"] += 1

        def ee_v2(v, tb=tb):
            wait(v, "sc")
            v.reciprocal(C.scr[0:1, 1024:1536], C.scr[0:1, 1536:2048])
            v.drain()
            v.tensor_tensor(C.scr[0:1, 1536:2048], C.scr[0:1, 0:512],
                            C.scr[0:1, 1024:1536], Op.mult)
            v.drain().then_inc(C.sems["ve"], 1)
            C.semv["ve"] += 1

        def ee_pe2(t, tb=tb):
            wait(t, "ve")
            t.matmul(pb[2][:, :], C.onerow_f[0:1, :], C.scr[0:1, 1024:1536],
                     start=True, stop=True, skip_group_check=True)
            t.matmul(pb[3][:, :], C.onerow_f[0:1, :], C.scr[0:1, 1536:2048],
                     start=True, stop=True, skip_group_check=True)
            t.drain().then_inc(C.sems["pe"], 1)
            C.semv["pe"] += 1

        def ee_v3(v, tb=tb):
            wait(v, "pe")
            for oc in range(8):
                sl = slice(oc * 512, (oc + 1) * 512)
                v.tensor_tensor(outf[:, sl], h2f[:, sl], pb[2][:, :], Op.mult)
                v.drain()
                v.tensor_sub(outf[:, sl], outf[:, sl], pb[3][:, :])
                v.drain()
            v.drain().then_inc(C.sems["ve"], 1)
            C.semv["ve"] += 1
        block(tensor=ee_pe)
        block(vector=ee_v)
        block(scalar=ee_s)
        block(vector=ee_v2)
        block(tensor=ee_pe2)
        block(vector=ee_v3)

        def ff_pe(t, tb=tb):
            wait(t, "ve")
            C._f5f = C.semv["ve"]
            C._f5st = C.semv["st"]
            for q in range(4):
                bank = pb[4] if q % 2 == 0 else pb[5]
                if q >= 2:
                    t.wait_ge(C.sems["ve"], C._f5f + q - 1)
                for oc in range(4):
                    t.transpose(bank[:, oc * 128:(oc + 1) * 128],
                                outf[:, oc * 512 + q * 128:
                                     oc * 512 + q * 128 + 128], C.idn_f[:, :])
                t.drain().then_inc(C.sems["pe"], 1)
                C.semv["pe"] += 1
                bank2 = pb[6] if q % 2 == 0 else pb[7]
                for oc in range(4):
                    t.transpose(bank2[:, oc * 128:(oc + 1) * 128],
                                outf[:, (4 + oc) * 512 + q * 128:
                                     (4 + oc) * 512 + q * 128 + 128],
                                C.idn_f[:, :])
                t.drain().then_inc(C.sems["pe"], 1)
                C.semv["pe"] += 1

        def ff_v(v, tb=tb):
            base_pe = C.semv["pe"] - 8
            for q in range(4):
                bank = pb[4] if q % 2 == 0 else pb[5]
                bank2 = pb[6] if q % 2 == 0 else pb[7]
                v.wait_ge(C.sems["pe"], base_pe + 2 * q + 2)
                if q >= 2:
                    v.wait_ge(C.sems["st"], C._f5st + 16 * (q - 1))
                qe = (q % 2) * E
                for oc in range(4):
                    v.tensor_copy(otm[:, qe + oc * 128:qe + (oc + 1) * 128],
                                  bank[:, oc * 128:(oc + 1) * 128])
                    v.tensor_copy(otm[:, qe + (4 + oc) * 128:
                                      qe + (5 + oc) * 128],
                                  bank2[:, oc * 128:(oc + 1) * 128])
                v.drain().then_inc(C.sems["ve"], 1)
                C.semv["ve"] += 1

        def ff_sy(sy, tb=tb, t0=t0):
            base_ve = C.semv["ve"] - 4
            for q in range(4):
                sy.wait_ge(C.sems["ve"], base_ve + q + 1)
                if q >= 1:
                    sy.wait_ge(C.sems["st"], C._f5st + 16 * q)
                tok = t0 + q * 128
                b_ = tok // S
                s_ = tok % S
                dinc(sy.dma_start(C.out_h[b_, s_:s_ + 128, :],
                                  otm[:, (q % 2) * E:(q % 2 + 1) * E]), "st")
        block(tensor=ff_pe, vector=ff_v, sync=ff_sy)
        C.tap(f"outt{tb}", lambda: otm[:, :], [128, 2 * E])
    scope.close()


_BUILT = {}


def _get_built():
    if "C" not in _BUILT:
        _BUILT["C"] = build(tap_req=_TAP_REQ)
    return _BUILT["C"]


def kernel(L0_attn_x, L0_attn_Wq, L0_attn_Wk, L0_attn_Wv, L0_attn_Wr,
           L0_attn_Wo, L0_ffn_w_gate, L0_ffn_w_up, L0_ffn_w_down):
    C = _get_built()
    x = np.ascontiguousarray(np.asarray(L0_attn_x, dtype=np.float32))
    shards = [x[2 * i:2 * i + 2] for i in range(8)]
    base = {
        "Wq": np.ascontiguousarray(np.asarray(L0_attn_Wq, np.float32)),
        "Wk": np.ascontiguousarray(np.asarray(L0_attn_Wk, np.float32)),
        "Wv": np.ascontiguousarray(np.asarray(L0_attn_Wv, np.float32)),
        "Wr": np.ascontiguousarray(np.asarray(L0_attn_Wr, np.float32)),
        "Wo": np.ascontiguousarray(np.asarray(L0_attn_Wo, np.float32)),
        "w_gate": np.ascontiguousarray(np.asarray(L0_ffn_w_gate, np.float32)),
        "w_up": np.ascontiguousarray(np.asarray(L0_ffn_w_up, np.float32)),
        "w_down": np.ascontiguousarray(np.asarray(L0_ffn_w_down, np.float32)),
    }
    in_maps = [dict(base, x=s_) for s_ in shards]
    res = run_bass_kernel_spmd(C.nc, in_maps, core_ids=list(range(8)))
    out = np.concatenate([np.asarray(r["out"], np.float32)
                          for r in res.results], axis=0)
    return out


# version 02
# speedup vs baseline: 1.0865x; 1.0865x over previous
import numpy as np
import ml_dtypes
from contextlib import ExitStack

import concourse.bass as bass
import concourse.mybir as mybir
from concourse.bass_utils import run_bass_kernel_spmd
from concourse.alu_op_type import AluOpType as Op

F32 = mybir.dt.float32
BF16 = mybir.dt.bfloat16
I32 = mybir.dt.int32
AF = mybir.ActivationFunctionType
AX = mybir.AxisListType.X

B, S, E, H, D, KN, INNER = 2, 2048, 1024, 16, 64, 4, 4096
NT = B * S
EPS_LN = 1e-5
SQRT_EPS_NW = 1e-4
LOG_RB = float(np.log(10000.0))
TWO_PI = float(2 * np.pi)
INV_2PI = float(1.0 / TWO_PI)
QOFF = 140.0

_DETECT_RACES = False
_STAGE = 99
_TAP_REQ = []

CONSTS = [0.0, 1.0, 0.5, 0.25, -0.5, 2.0, -1.0, float(np.pi / 2),
          127.0, 1e-5, float(1.0 / E), float(EPS_LN), float(np.pi)]
CIDX = {float(v): i for i, v in enumerate(CONSTS)}


class _Ctx:
    pass


def _mk_ctx():
    C = _Ctx()
    C.nc = bass.Bass("TRN2", target_bir_lowering=False,
                     detect_race_conditions=_DETECT_RACES)
    C.st = ExitStack()
    C.blocks = []
    C.taps = []
    nc, st = C.nc, C.st

    C.sb = lambda n, sh, dt: st.enter_context(nc.sbuf_tensor(n, sh, dt))
    C.ps = lambda n, sh: st.enter_context(nc.psum_tensor(n, sh, F32))
    C.dram = lambda n, sh, dt, k: nc.dram_tensor(n, sh, dt, kind=k).ap()

    C.sems, C.semv = {}, {}

    def mksem(n):
        C.sems[n] = st.enter_context(nc.semaphore(n))
        C.semv[n] = 0
    C.mksem = mksem
    for s_ in ["ld", "ld2", "st", "pe", "ve", "sc", "gp", "sy",
               "pe2", "ve2", "sc2", "gp2", "ve3", "sy2", "lds", "lda"]:
        mksem(s_)

    def inc(inst, name, n=1):
        inst.then_inc(C.sems[name], n)
        C.semv[name] += n
        return inst
    C.inc = inc
    C.dinc = lambda inst, name="ld": inc(inst, name, 16)

    def wait(eng, name, lag=0):
        v = C.semv[name] - lag
        if v > 0:
            eng.wait_ge(C.sems[name], v)
    C.wait = wait

    def block(**eng_fns):
        C.blocks.append(eng_fns)
    C.block = block
    return C


def build(tap_req=()):
    C = _mk_ctx()
    C.tap_req = list(tap_req)
    nc = C.nc
    dram, sb = C.dram, C.sb

    C.x_h = dram("x", [B, S, E], F32, "ExternalInput")
    C.wq_h = dram("Wq", [E, E], F32, "ExternalInput")
    C.wk_h = dram("Wk", [E, E], F32, "ExternalInput")
    C.wv_h = dram("Wv", [E, E], F32, "ExternalInput")
    C.wr_h = dram("Wr", [E, H * KN], F32, "ExternalInput")
    C.wo_h = dram("Wo", [E, E], F32, "ExternalInput")
    C.wg_h = dram("w_gate", [INNER, E], F32, "ExternalInput")
    C.wu_h = dram("w_up", [INNER, E], F32, "ExternalInput")
    C.wd_h = dram("w_down", [E, INNER], F32, "ExternalInput")
    C.out_h = dram("out", [B, S, E], F32, "ExternalOutput")
    C.wgT_h = dram("wgT", [32, E, 128], BF16, "Internal")
    C.wuT_h = dram("wuT", [32, E, 128], BF16, "Internal")
    C.wdT_h = dram("wdT", [8, INNER, 128], BF16, "Internal")
    C.pf_h = dram("pf_sc", [B, 64, S], F32, "Internal")
    C.u_h = dram("u_sc", [B, 64, S], BF16, "Internal")
    C.v_h = dram("v_sc", [B, 64, S], BF16, "Internal")
    C.vtm_h = dram("vtm_sc", [B, 16, 128, E], BF16, "Internal")
    C.attno_h = dram("attno_sc", [8, 128, NT], BF16, "Internal")
    C.xln_h = dram("xln_sc", [8, 128, NT], BF16, "Internal")
    C.xq_h = dram("xq_sc", [8, 128, NT], BF16, "Internal")

    C.consts = sb("consts", [128, len(CONSTS)], F32)
    C.disc = sb("disc", [128, 12], F32)
    C.wcol = sb("wcol", [128, 12], F32)
    C.ones_bf = sb("ones_bf", [128, 16], BF16)
    C.idn_bf = sb("idn_bf", [128, 128], BF16)
    C.idn_f = sb("idn_f", [128, 128], F32)
    C.seldup = sb("seldup", [128, 6 * 64], BF16)
    C.masks = sb("masks", [128, 4 * 512], BF16)
    C.rsel = [sb(f"rsel{i}", [8, 128], F32) for i in range(4)]
    C.onerow_f = sb("onerow_f", [1, 128], F32)
    C.wqs = sb("wqs", [128, 8], F32)
    C.rowsel2 = sb("rowsel2", [128, 2], BF16)
    C.rsg = sb("rsg", [128, 3 * 32], F32)
    C.scr = sb("scr", [128, 2048], F32)
    C.scr2 = sb("scr2", [128, 2048], F32)
    C.scri = sb("scri", [128, 2048], I32)
    C.scrb = sb("scrb", [128, 2048], BF16)
    C.selABf = sb("selABf", [2, 256], F32)
    C._xfm_cm = nc.sbuf_tensor("xfm", [128, 8 * NT], BF16)
    C.xfm = C._xfm_cm.__enter__()
    C.pb = [C.ps(f"pb{i}", [128, 512]) for i in range(8)]

    for v in CONSTS:
        nc.const_aps.register(F32, float(v), C.consts[:, CIDX[float(v)]:CIDX[float(v)] + 1])
    C.ccol = lambda v: C.consts[:, CIDX[float(v)]:CIDX[float(v)] + 1]

    def tap(name, src_fn, shape, dt=F32):
        if name not in C.tap_req:
            return
        t = dram("tap_" + name, shape, dt, "ExternalOutput")

        def _sy(sy, t=t, src_fn=src_fn):
            C.dinc(sy.dma_start(t, src_fn()), "st")
            C.wait(sy, "st")
        C.block(sync=_sy)
    C.tap = tap

    _p0(C)
    if _STAGE >= 1:
        _wprep(C)
    if _STAGE >= 2:
        _attention(C)
    if _STAGE >= 4:
        _wo_ln1_quant(C)
    if _STAGE >= 5:
        _ffn(C)

    def _fin(sy):
        C.wait(sy, "st")
    C.block(sync=_fin)

    for eng_fns in C.blocks:
        with nc.Block() as blk:
            for ename, fn in eng_fns.items():
                getattr(blk, ename)(fn)
    return C


def _p0(C):
    nc, block, inc, dinc, wait, ccol = C.nc, C.block, C.inc, C.dinc, C.wait, C.ccol
    C._p0scope = ExitStack()
    xtm = C._p0scope.enter_context(nc.sbuf_tensor("xtm", [128, 32 * E], BF16))
    pb = C.pb

    def g0(g):
        for t in range(32):
            b_, s_ = divmod(t, 16)
            dinc(g.dma_start(xtm[:, t * E:(t + 1) * E],
                             C.x_h[b_, s_ * 128:(s_ + 1) * 128, :]))
        for i, v in enumerate(CONSTS):
            g.memset(C.consts[:, i:i + 1], float(v))
        g.memset(C.ones_bf[:, :], 1.0)
        g.memset(C.onerow_f[:, :], 1.0)
        g.memset(C.disc[:, :], 0.0)
        g.memset(C.wcol[:, :], 0.0)
        g.memset(C.wqs[:, :], 0.0)
        g.memset(C.rsg[:, :], 0.0)
        g.iota(C.scr[:, 0:128], [[1, 128]], channel_multiplier=0,
               allow_small_or_imprecise_dtypes=True)
        g.iota(C.scr[:, 128:129], [[0, 1]], channel_multiplier=1,
               allow_small_or_imprecise_dtypes=True)
        for m in range(4):
            g.iota(C.scr2[:, 0:512], [[1, 512]], channel_multiplier=-1,
                   base=-128 * m, allow_small_or_imprecise_dtypes=True)
            g.drain()
            g.tensor_scalar(C.scr2[:, 512:1024], C.scr2[:, 0:512], -0.5,
                            None, Op.is_gt)
            g.drain()
            g.tensor_copy(C.masks[:, m * 512:(m + 1) * 512], C.scr2[:, 512:1024])
        for ib in range(4):
            g.memset(C.rsel[ib][:, :], 0.0)
        wait(g, "ld")
    block(gpsimd=g0)

    def v0(v):
        v.tensor_scalar(C.scr2[0:8, 0:128], C.scr[0:8, 0:128], 64.0, None,
                        Op.is_lt)
        v.tensor_scalar(C.scr2[0:8, 128:256], C.scr[0:8, 0:128], 63.5, None,
                        Op.is_gt)
        v.drain()
        for ib in range(4):
            v.tensor_scalar(C.scr2[0:8, 256:257], C.scr[0:8, 128:129],
                            float(2 * ib), None, Op.is_equal)
            v.tensor_scalar(C.scr2[0:8, 257:258], C.scr[0:8, 128:129],
                            float(2 * ib + 1), None, Op.is_equal)
            v.drain()
            v.tensor_scalar(C.scr2[0:8, 258:386], C.scr2[0:8, 0:128],
                            C.scr2[0:8, 256:257], None, Op.mult)
            v.tensor_scalar(C.scr2[0:8, 386:514], C.scr2[0:8, 128:256],
                            C.scr2[0:8, 257:258], None, Op.mult)
            v.drain()
            v.tensor_tensor(C.rsel[ib][:, :], C.scr2[0:8, 258:386],
                            C.scr2[0:8, 386:514], Op.add)
            v.drain()
        v.tensor_scalar(C.disc[:, 1:2], ccol(0.25), 0.5, None, Op.add)
        v.drain()
        v.tensor_copy(C.scri[:, 0:1], C.disc[:, 1:2])
        v.drain()
        v.tensor_copy(C.disc[:, 0:1], C.scri[:, 0:1])
        v.drain()
        v.tensor_scalar(C.disc[:, 1:2], C.disc[:, 0:1], -1.0, 1.0, Op.mult, Op.add)
        v.drain()
        v.tensor_scalar(C.disc[:, 2:3], C.disc[:, 1:2], QOFF + 0.5, None, Op.mult)
        v.tensor_scalar(C.disc[:, 3:4], C.disc[:, 0:1], -0.5, None, Op.mult)
        v.tensor_scalar(C.disc[:, 4:5], C.disc[:, 1:2], QOFF, None, Op.mult)
        v.tensor_scalar(C.disc[:, 5:6], C.disc[:, 1:2], 128.5, None, Op.mult)
        v.tensor_scalar(C.disc[:, 6:7], C.disc[:, 1:2], 128.0, 1.0, Op.mult, Op.add)
        v.tensor_scalar(C.disc[:, 7:8], C.disc[:, 1:2], 128.0, -1.0, Op.mult, Op.add)
        v.tensor_scalar(C.disc[:, 8:9], C.disc[:, 1:2], 128.0, None, Op.mult)
        v.tensor_scalar(C.disc[:, 9:10], C.disc[:, 0:1], float(np.pi),
                        float(-np.pi), Op.mult, Op.add)
        v.tensor_scalar(C.idn_f[:, :], C.scr[:, 0:128], C.scr[:, 128:129],
                        None, Op.is_equal)
        v.tensor_scalar(C.idn_bf[:, :], C.scr[:, 0:128], C.scr[:, 128:129],
                        None, Op.is_equal)
        v.tensor_scalar(C.rowsel2[:, 0:1], C.scr[:, 128:129], 64.0, None, Op.is_lt)
        v.tensor_scalar(C.rowsel2[:, 1:2], C.scr[:, 128:129], 63.5, None, Op.is_gt)
        v.tensor_copy(C.scri[:, 1:2], C.scr[:, 128:129])
        v.drain()
        v.tensor_scalar(C.scri[:, 2:3], C.scri[:, 1:2], 31, None, Op.bitwise_and)
        v.drain()
        v.tensor_copy(C.wcol[:, 0:1], C.scri[:, 2:3])
    block(vector=v0)

    def s0(s):
        s.activation(C.wcol[:, 1:2], C.wcol[:, 0:1], AF.Exp, bias=ccol(0.0),
                     scale=-LOG_RB / 32.0)
    block(scalar=s0)

    def v0b(v):
        v.tensor_scalar(C.wcol[:, 2:3], C.wcol[:, 1:2], INV_2PI, None, Op.mult)
        v.tensor_scalar(C.wcol[:, 3:4], C.disc[:, 3:4], 1.0, 0.5, Op.mult, Op.add)
        v.tensor_scalar(C.wcol[:, 4:5], C.disc[:, 3:4], 1.0, 0.75, Op.mult, Op.add)
    block(vector=v0b)

    def s0b(s):
        s.activation(C.wcol[:, 5:6], C.wcol[:, 1:2], AF.Sin, bias=ccol(0.0),
                     scale=1.0)
        s.activation(C.wcol[:, 6:7], C.wcol[:, 1:2], AF.Sin,
                     bias=ccol(np.pi / 2), scale=1.0)
    block(scalar=s0b)

    def v0c(v):
        sc = C.scr
        v.tensor_copy(C.scri[:, 0:64], sc[:, 0:64])
        v.drain()
        v.tensor_scalar(C.scri[:, 64:128], C.scri[:, 0:64], 31, None, Op.bitwise_and)
        v.drain()
        v.tensor_copy(sc[:, 256:320], C.scri[:, 64:128])
        v.tensor_scalar(sc[:, 576:640], sc[:, 0:64], 31.5, None, Op.is_gt)
        v.drain()
        v.tensor_scalar(sc[:, 320:384], sc[:, 256:320], C.wcol[:, 0:1],
                        None, Op.is_equal)
        v.tensor_scalar(sc[:, 704:768], sc[:, 576:640], 2.0, -1.0, Op.mult, Op.add)
        v.drain()
        v.tensor_copy(C.seldup[:, 0:64], sc[:, 320:384])
        v.tensor_scalar(sc[:, 384:448], sc[:, 320:384], C.wcol[:, 6:7], None, Op.mult)
        v.tensor_scalar(sc[:, 448:512], sc[:, 320:384], C.wcol[:, 5:6], None, Op.mult)
        v.tensor_mul(sc[:, 640:704], sc[:, 320:384], sc[:, 704:768])
        v.drain()
        v.tensor_copy(C.seldup[:, 64:128], sc[:, 384:448])
        v.tensor_scalar(sc[:, 512:576], sc[:, 448:512], -1.0, None, Op.mult)
        v.tensor_copy(C.seldup[:, 192:256], sc[:, 640:704])
        v.tensor_mul(sc[:, 768:832], sc[:, 384:448], sc[:, 704:768])
        v.tensor_mul(sc[:, 832:896], sc[:, 448:512], sc[:, 704:768])
        v.drain()
        v.tensor_copy(C.seldup[:, 128:192], sc[:, 512:576])
        v.tensor_copy(C.seldup[:, 256:320], sc[:, 768:832])
        v.tensor_copy(C.seldup[:, 320:384], sc[:, 832:896])
    block(vector=v0c)

    def pe0(t):
        base_ve = C.semv["ve"]
        for n in range(64):
            tt, half = divmod(n, 2)
            bank = pb[6] if n % 2 == 0 else pb[7]
            if n >= 2:
                t.wait_ge(C.sems["ve"], base_ve + n - 1)
            for c in range(4):
                cc = half * 4 + c
                t.transpose(bank.bitcast(BF16)[:, c * 128:(c + 1) * 128],
                            xtm[:, tt * E + cc * 128:tt * E + (cc + 1) * 128],
                            C.idn_bf[:, :])
            t.drain().then_inc(C.sems["pe"], 1)
        C.semv["pe"] += 64

    def ve0(v):
        base_pe = C.semv["pe"] - 64
        for n in range(64):
            tt, half = divmod(n, 2)
            b_, s_ = divmod(tt, 16)
            tok0 = b_ * S + s_ * 128
            bank = pb[6] if n % 2 == 0 else pb[7]
            v.wait_ge(C.sems["pe"], base_pe + n + 1)
            for c in range(4):
                cc = half * 4 + c
                v.tensor_copy(C.xfm[:, cc * NT + tok0:cc * NT + tok0 + 128],
                              bank.bitcast(BF16)[:, c * 128:(c + 1) * 128])
            v.drain().then_inc(C.sems["ve"], 1)
        C.semv["ve"] += 64
    block(tensor=pe0, vector=ve0)

    C.tap("disc", lambda: C.disc[:, :], [128, 12])
    C.tap("wcol", lambda: C.wcol[:, :], [128, 12])
    C.tap("xfm", lambda: C.xfm[:, :].bitcast(F32), [128, 4 * NT])
    C.tap("masks", lambda: C.masks[:, :].bitcast(F32), [128, 1024])
    C.tap("seldup", lambda: C.seldup[:, :].bitcast(F32), [128, 192])


def _wprep(C):
    nc, block, inc, dinc, wait, ccol = C.nc, C.block, C.inc, C.dinc, C.wait, C.ccol
    pb = C.pb
    C._p0scope.close()
    with ExitStack() as scope:
        e = scope.enter_context
        wfl = e(nc.sbuf_tensor("wfl", [128, 4096], F32))
        wfl2 = e(nc.sbuf_tensor("wfl2", [128, 4096], F32))
        wiq = e(nc.sbuf_tensor("wiq", [128, 4096], I32))
        wzb = e(nc.sbuf_tensor("wzb", [128, 4096], F32))
        wqb = e(nc.sbuf_tensor("wqb", [128, 4096], BF16))
        wtb = e(nc.sbuf_tensor("wtb", [128, 2 * 1024], BF16))

        WCFG = [(C.wg_h, INNER, E, C.wgT_h, 0), (C.wu_h, INNER, E, C.wuT_h, 1),
                (C.wd_h, E, INNER, C.wdT_h, 2)]

        def w1g(g):
            C._w1ve = C.semv["ve"]
            g.memset(C.scr2[:, 0:5], 0.0)
            g.drain().then_inc(C.sems["gp"], 1)
            C.semv["gp"] += 1
            n = 0
            for wh, rows, cols, _, wi in WCFG:
                for ch in range(rows // 128):
                    buf = wfl if n % 2 == 0 else wfl2
                    if n >= 2:
                        g.wait_ge(C.sems["ve"], C._w1ve + n - 1)
                    dinc(g.dma_start(buf[:, 0:cols],
                                     wh[ch * 128:(ch + 1) * 128, :]))
                    g.wait_ge(C.sems["ld"], C.semv["ld"])
                    g.drain().then_inc(C.sems["gp"], 1)
                    C.semv["gp"] += 1
                    n += 1

        def w1v(v):
            nch1 = sum(rows // 128 for _, rows, _, _, _ in WCFG)
            base_gp = C.semv["gp"] - (nch1 + 1)
            n = 0
            for wh, rows, cols, _, wi in WCFG:
                for ch in range(rows // 128):
                    buf = wfl if n % 2 == 0 else wfl2
                    v.wait_ge(C.sems["gp"], base_gp + n + 2)
                    v.tensor_reduce(C.scr2[:, 4:5], buf[:, 0:cols], AX, Op.add,
                                    apply_absolute_value=True)
                    v.drain()
                    v.tensor_add(C.scr2[:, wi:wi + 1], C.scr2[:, wi:wi + 1],
                                 C.scr2[:, 4:5])
                    v.drain().then_inc(C.sems["ve"], 1)
                    n += 1
            C.semv["ve"] += nch1
        block(gpsimd=w1g, vector=w1v)

        def w1pe(t):
            t.matmul(pb[0][0:3, 0:1], C.scr2[:, 0:3], ccol(1.0), start=True,
                     stop=True)
        block(tensor=w1pe)

        def w1v2(v):
            v.tensor_copy(C.scr2[0:3, 8:9], pb[0][0:3, 0:1])
            v.drain()
            v.tensor_scalar(C.scr2[0:3, 9:10], C.scr2[0:3, 8:9],
                            1.0 / (INNER * E), None, Op.mult)
            v.drain()
            v.tensor_scalar(C.scr2[0:3, 10:11], C.scr2[0:3, 9:10], 1e-5,
                            None, Op.max)
            v.drain()
            v.reciprocal(C.scr2[0:3, 11:12], C.scr2[0:3, 10:11])
        block(vector=w1v2)

        def w1sy(sy):
            dinc(sy.dma_start(C.scr2[0:1, 12:13], C.scr2[1:2, 11:12]), "st")
            dinc(sy.dma_start(C.scr2[0:1, 13:14], C.scr2[2:3, 11:12]), "st")
            wait(sy, "st")
        block(sync=w1sy)

        def w1pe2(t):
            for wi in range(3):
                t.matmul(pb[1][:, wi:wi + 1], C.onerow_f[0:1, :],
                         C.scr2[0:1, 11 + wi:12 + wi], start=True, stop=True,
                         skip_group_check=True)
        block(tensor=w1pe2)

        def w1v3(v):
            v.tensor_copy(C.wqs[:, 0:3], pb[1][:, 0:3])
        block(vector=w1v3)
        C.tap("wqs", lambda: C.wqs[:, :], [128, 8])

        chunks = []
        for wh, rows, cols, whT, wi in WCFG:
            for ch in range(rows // 128):
                chunks.append((wh, rows, cols, whT, wi, ch))
        NCH = len(chunks)
        nbat = [cols // 1024 for _, _, cols, _, _, _ in chunks]
        batpre = [0]
        for nb in nbat:
            batpre.append(batpre[-1] + nb)
        NBAT = batpre[-1]

        b0 = {}

        def w2g(g):
            for k_ in ("ld2", "gp", "ve", "pe", "ve3", "st"):
                b0[k_] = C.semv[k_]
            for n, (wh, rows, cols, whT, wi, ch) in enumerate(chunks):
                buf = wfl if n % 2 == 0 else wfl2
                if n >= 2:
                    g.wait_ge(C.sems["ve"], b0["ve"] + 2 * (n - 1))
                dinc(g.dma_start(buf[:, 0:cols],
                                 wh[ch * 128:(ch + 1) * 128, :]), "ld2")
                g.wait_ge(C.sems["ld2"], b0["ld2"] + 16 * (n + 1))
                g.drain().then_inc(C.sems["gp"], 1)
                g.wait_ge(C.sems["ve"], b0["ve"] + 2 * n + 1)
                g.tensor_copy(wiq[:, 0:cols], wzb[:, 0:cols])
                g.drain().then_inc(C.sems["gp"], 1)
            C.semv["gp"] += 2 * NCH

        def w2v(v):
            for n, (wh, rows, cols, whT, wi, ch) in enumerate(chunks):
                buf = wfl if n % 2 == 0 else wfl2
                v.wait_ge(C.sems["gp"], b0["gp"] + 2 * n + 1)
                v.tensor_scalar(wzb[:, 0:cols], buf[:, 0:cols],
                                C.wqs[:, wi:wi + 1], C.disc[:, 5:6],
                                Op.mult, Op.add)
                v.drain().then_inc(C.sems["ve"], 1)
                v.wait_ge(C.sems["gp"], b0["gp"] + 2 * n + 2)
                v.tensor_copy(wzb[:, 0:cols], wiq[:, 0:cols])
                v.drain()
                v.tensor_scalar(wzb[:, 0:cols], wzb[:, 0:cols], C.disc[:, 6:7],
                                C.disc[:, 7:8], Op.min, Op.max)
                v.drain()
                v.tensor_scalar(wzb[:, 0:cols], wzb[:, 0:cols], C.disc[:, 8:9],
                                None, Op.subtract)
                v.drain()
                v.tensor_copy(wqb[:, 0:cols], wzb[:, 0:cols])
                v.tensor_reduce(C.rsg[:, wi * 32 + ch:wi * 32 + ch + 1],
                                wzb[:, 0:cols], AX, Op.add)
                v.drain().then_inc(C.sems["ve"], 1)
                for bb in range(nbat[n]):
                    gb = batpre[n] + bb
                    bank = pb[6] if gb % 2 == 0 else pb[7]
                    v.wait_ge(C.sems["pe"], b0["pe"] + gb + 1)
                    if gb >= 2:
                        v.wait_ge(C.sems["st"], b0["st"] + 16 * (gb - 1))
                    v.tensor_copy(wtb[:, (gb % 2) * 1024:(gb % 2) * 1024 + 1024],
                                  bank.bitcast(BF16)[:, 0:1024])
                    v.drain().then_inc(C.sems["ve3"], 1)
            C.semv["ve"] += 2 * NCH
            C.semv["ve3"] += NBAT

        def w2pe(t):
            for n, (wh, rows, cols, whT, wi, ch) in enumerate(chunks):
                t.wait_ge(C.sems["ve"], b0["ve"] + 2 * n + 2)
                for bb in range(nbat[n]):
                    gb = batpre[n] + bb
                    bank = pb[6] if gb % 2 == 0 else pb[7]
                    if gb >= 2:
                        t.wait_ge(C.sems["ve3"], b0["ve3"] + gb - 1)
                    for c in range(8):
                        col0 = bb * 1024 + c * 128
                        t.transpose(bank.bitcast(BF16)[:, c * 128:(c + 1) * 128],
                                    wqb[:, col0:col0 + 128], C.idn_bf[:, :])
                    t.drain().then_inc(C.sems["pe"], 1)
            C.semv["pe"] += NBAT

        def w2sy(sy):
            for n, (wh, rows, cols, whT, wi, ch) in enumerate(chunks):
                for bb in range(nbat[n]):
                    gb = batpre[n] + bb
                    sy.wait_ge(C.sems["ve3"], b0["ve3"] + gb + 1)
                    if gb >= 1:
                        sy.wait_ge(C.sems["st"], b0["st"] + 16 * gb)
                    src = wtb[:, (gb % 2) * 1024:(gb % 2) * 1024 + 1024]
                    if wi < 2:
                        dst = whT[ch, :, :].rearrange("(ic r) c -> r ic c", r=128)
                    else:
                        icn0 = bb * 8
                        dst = whT[ch, icn0 * 128:(icn0 + 8) * 128, :].rearrange(
                            "(ic r) c -> r ic c", r=128)
                    dinc(sy.dma_start(dst, src.rearrange("r (ic c) -> r ic c",
                                                         c=128)), "st")
        block(gpsimd=w2g, vector=w2v, tensor=w2pe, sync=w2sy)
    C.tap("rsg", lambda: C.rsg[:, :], [128, 96])
    C.tap("wgT0", lambda: C.wgT_h[0, :, :], [E, 128], BF16)
    C.tap("wdT0", lambda: C.wdT_h[0, :, :], [INNER, 128], BF16)


def _attention(C):
    nc, block, inc, dinc, wait, ccol = C.nc, C.block, C.inc, C.dinc, C.wait, C.ccol
    pb = C.pb
    selABf = C.selABf

    def selABf_init(v):
        pc = C.scr[0:2, 128:129].broadcast_to((2, 128))
        v.tensor_scalar(selABf[0:2, 0:128], pc, 0.5, None, Op.is_lt)
        v.tensor_scalar(selABf[0:2, 128:256], pc, 0.5, None, Op.is_gt)
    block(vector=selABf_init)

    for b in range(B):
        bscope = ExitStack()
        be = bscope.enter_context
        pbf = be(nc.sbuf_tensor(f"pbf{b}", [64, S], BF16))
        strip = be(nc.sbuf_tensor(f"strip{b}", [64, S], BF16))
        _p1_seq(C, b, pbf=pbf, strip=strip)
        ps_ = ExitStack()
        e = ps_.enter_context
        A = _Ctx()
        A.pbf, A.strip, A.selABf = pbf, strip, selABf
        A.wqp = e(nc.sbuf_tensor(f"wqp{b}", [128, 8 * 128], BF16))
        A.wkp = e(nc.sbuf_tensor(f"wkp{b}", [128, 8 * 128], BF16))
        A.vp = e(nc.sbuf_tensor(f"vp{b}", [128, 16 * 130], BF16))
        A.qbf = e(nc.sbuf_tensor(f"qbf{b}", [128, S], BF16))
        A.kbf = e(nc.sbuf_tensor(f"kbf{b}", [128, S], BF16))
        A.qrot = e(nc.sbuf_tensor(f"qrot{b}", [128, S], BF16))
        A.krot = e(nc.sbuf_tensor(f"krot{b}", [128, S], BF16))
        A.qr = e(nc.sbuf_tensor(f"qr{b}", [128, S], BF16))
        A.kr = e(nc.sbuf_tensor(f"kr{b}", [128, S], BF16))
        A.pfrep = e(nc.sbuf_tensor(f"pfrep{b}", [128, S], F32))
        A.urep = e(nc.sbuf_tensor(f"urep{b}", [128, S], BF16))
        A.vrep = e(nc.sbuf_tensor(f"vrep{b}", [128, S], BF16))
        A.trigb = e(nc.sbuf_tensor(f"trigb{b}", [128, 2 * 512], BF16))
        A.prods = e(nc.sbuf_tensor(f"prods{b}", [128, 4 * 512], BF16))
        A.tmpq = e(nc.sbuf_tensor(f"tmpq{b}", [128, 512], BF16))
        A.tmpq2 = e(nc.sbuf_tensor(f"tmpq2{b}", [128, 512], BF16))
        A.negC = e(nc.sbuf_tensor(f"negC{b}", [128, 2], F32))
        A.qn = e(nc.sbuf_tensor(f"qn{b}", [2, 16], F32))
        A.eA = [e(nc.sbuf_tensor(f"eA{b}_{i}", [128, 512], BF16)) for i in range(2)]
        A.eB = [e(nc.sbuf_tensor(f"eB{b}_{i}", [128, 512], BF16)) for i in range(2)]
        A.ptA = [e(nc.sbuf_tensor(f"ptA{b}_{i}", [128, 512], BF16)) for i in range(2)]
        A.ptB = [e(nc.sbuf_tensor(f"ptB{b}_{i}", [128, 512], BF16)) for i in range(2)]
        A.outev = e(nc.sbuf_tensor(f"outev{b}", [128, S], F32))
        A.attp = e(nc.sbuf_tensor(f"attp{b}", [128, S], BF16))
        A.cscr = C.scr2
        A.sstg = C.scri.bitcast(F32)[0:8, 0:512]
        A.sstg2 = C.scri.bitcast(F32)[0:8, 512:1024]
        for g in range(8):
            _pair_prep(C, A, b, g)
            _pair_scores(C, A, b, g)
        ps_.close()
        bscope.close()


def _p1_seq(C, b, pbf, strip):
    nc, block, inc, dinc, wait, ccol = C.nc, C.block, C.inc, C.dinc, C.wait, C.ccol
    pb = C.pb
    scope = ExitStack()
    e = scope.enter_context
    wvs = e(nc.sbuf_tensor(f"wvs{b}", [128, 8 * E], BF16))
    wrs = e(nc.sbuf_tensor(f"wrs{b}", [128, 8 * 64], BF16))
    vtm = e(nc.sbuf_tensor(f"vtm{b}", [128, 2 * E], BF16))
    ltm = e(nc.sbuf_tensor(f"ltm{b}", [128, 16 * 64], F32))
    ptm = e(nc.sbuf_tensor(f"ptm{b}", [128, 16 * 64], F32))
    onesf = e(nc.sbuf_tensor(f"onesf{b}", [64, S], F32))
    pcm = e(nc.sbuf_tensor(f"pcm{b}", [64, S], F32))
    ubf = e(nc.sbuf_tensor(f"ubf{b}", [64, S], BF16))

    def l0(g):
        for c in range(8):
            dinc(g.dma_start(wvs[:, c * E:(c + 1) * E],
                             C.wv_h[c * 128:(c + 1) * 128, :]))
            dinc(g.dma_start(wrs[:, c * 64:(c + 1) * 64],
                             C.wr_h[c * 128:(c + 1) * 128, :]))
        g.memset(onesf[:, :], 1.0)
        g.memset(strip[0:64, :], SQRT_EPS_NW)
        wait(g, "ld")
    block(gpsimd=l0)

    def vp_pe(t):
        C._p1st = C.semv["st"]
        base_ve = C.semv["ve"]
        C._p1ve = base_ve
        for tt in range(16):
            bs = [pb[0], pb[1], pb[2]] if tt % 2 == 0 else [pb[3], pb[4], pb[5]]
            if tt >= 2:
                t.wait_ge(C.sems["ve"], base_ve + tt - 1)
            for ic in range(8):
                lhs = C.xfm[:, ic * NT + b * S + tt * 128:
                            ic * NT + b * S + tt * 128 + 128]
                m0 = t.matmul(bs[0][:, :], lhs, wvs[:, ic * E:ic * E + 512],
                              start=(ic == 0), stop=(ic == 7))
                m1 = t.matmul(bs[1][:, :], lhs,
                              wvs[:, ic * E + 512:ic * E + 1024],
                              start=(ic == 0), stop=(ic == 7))
                m2 = t.matmul(bs[2][:, 0:64], lhs, wrs[:, ic * 64:(ic + 1) * 64],
                              start=(ic == 0), stop=(ic == 7))
            for m_ in (m0, m1, m2):
                m_.then_inc(C.sems["pe"], 1)
        C.semv["pe"] += 48

    def vp_ve(v):
        base_pe = C.semv["pe"] - 48
        for tt in range(16):
            bs = [pb[0], pb[1], pb[2]] if tt % 2 == 0 else [pb[3], pb[4], pb[5]]
            v.wait_ge(C.sems["pe"], base_pe + 3 * tt + 3)
            if tt >= 2:
                v.wait_ge(C.sems["st"], C._p1st + 16 * (tt - 1))
            v.tensor_copy(vtm[:, (tt % 2) * E:(tt % 2) * E + 512], bs[0][:, :])
            v.tensor_copy(vtm[:, (tt % 2) * E + 512:(tt % 2) * E + 1024],
                          bs[1][:, :])
            v.tensor_copy(ltm[:, tt * 64:(tt + 1) * 64], bs[2][:, 0:64])
            v.drain().then_inc(C.sems["ve"], 1)
        C.semv["ve"] += 16

    def vp_sy(sy):
        base_ve = C.semv["ve"] - 16
        for tt in range(16):
            sy.wait_ge(C.sems["ve"], base_ve + tt + 1)
            if tt >= 1:
                sy.wait_ge(C.sems["st"], C._p1st + 16 * tt)
            dinc(sy.dma_start(C.vtm_h[b, tt, :, :],
                              vtm[:, (tt % 2) * E:(tt % 2 + 1) * E]), "st")
    block(tensor=vp_pe, vector=vp_ve, sync=vp_sy)
    C.tap(f"ltm{b}", lambda: ltm[:, :], [128, 1024])

    def _kv(t_, k):
        return t_[:, 0:1024].rearrange("p (n k) -> p n k", k=4)[:, :, k]

    def _bc4(row):
        return row.rearrange("p n -> p n ()").broadcast_to((128, 256, 4))

    def sm_v(v):
        v.tensor_tensor(C.scr[:, 0:256], _kv(ltm, 0), _kv(ltm, 1), Op.max)
        v.tensor_tensor(C.scr[:, 256:512], _kv(ltm, 2), _kv(ltm, 3), Op.max)
        v.drain()
        v.tensor_tensor(C.scr[:, 512:768], C.scr[:, 0:256], C.scr[:, 256:512], Op.max)
        v.drain()
        v.tensor_tensor(C.scr2[:, 0:1024].rearrange("p (n k) -> p n k", k=4),
                        ltm[:, 0:1024].rearrange("p (n k) -> p n k", k=4),
                        _bc4(C.scr[:, 512:768]), Op.subtract)
        v.drain().then_inc(C.sems["ve"], 1)
        C.semv["ve"] += 1

    def sm_s(s):
        wait(s, "ve")
        s.activation(ptm[:, 0:1024], C.scr2[:, 0:1024], AF.Exp, bias=ccol(0.0),
                     scale=1.0)
        s.drain().then_inc(C.sems["sc"], 1)
        C.semv["sc"] += 1

    def sm_v2(v):
        wait(v, "sc")
        v.tensor_tensor(C.scr[:, 0:256], _kv(ptm, 0), _kv(ptm, 1), Op.add)
        v.tensor_tensor(C.scr[:, 256:512], _kv(ptm, 2), _kv(ptm, 3), Op.add)
        v.drain()
        v.tensor_tensor(C.scr[:, 512:768], C.scr[:, 0:256], C.scr[:, 256:512], Op.add)
        v.drain()
        v.reciprocal(C.scr[:, 768:1024], C.scr[:, 512:768])
        v.drain()
        v.tensor_tensor(C.scr2[:, 0:1024].rearrange("p (n k) -> p n k", k=4),
                        ptm[:, 0:1024].rearrange("p (n k) -> p n k", k=4),
                        _bc4(C.scr[:, 768:1024]), Op.mult)
    block(vector=sm_v)
    block(scalar=sm_s)
    block(vector=sm_v2)
    C.tap(f"ptm{b}", lambda: C.scr2[:, 0:1024], [128, 1024])

    def pt_pe(t):
        base_ve = C.semv["ve"]
        C._ptve = base_ve
        for bt in range(4):
            bank = pb[6] if bt % 2 == 0 else pb[7]
            if bt >= 2:
                t.wait_ge(C.sems["ve"], base_ve + bt - 1)
            for c in range(4):
                tt = bt * 4 + c
                t.transpose(bank[0:64, c * 128:(c + 1) * 128],
                            C.scr2[:, tt * 64:(tt + 1) * 64],
                            C.idn_f[:, :]).then_inc(C.sems["pe"], 1)
        C.semv["pe"] += 16

    def pt_ve(v):
        base_pe = C.semv["pe"] - 16
        for bt in range(4):
            bank = pb[6] if bt % 2 == 0 else pb[7]
            v.wait_ge(C.sems["pe"], base_pe + 4 * bt + 4)
            v.tensor_copy(pcm[0:64, bt * 512:(bt + 1) * 512], bank[0:64, 0:512])
            v.drain().then_inc(C.sems["ve"], 1)
        C.semv["ve"] += 4
    block(tensor=pt_pe, vector=pt_ve)
    C.tap(f"pcm{b}", lambda: pcm[0:64, :], [64, S])

    def cs_v(v):
        v.tensor_copy(pbf[0:64, :], pcm[0:64, :])
        v.tensor_tensor_scan(C.scr2[0:64, 0:S], onesf[0:64, :], pcm[0:64, :],
                             0.0, Op.mult, Op.add)
        v.drain()
        v.tensor_sub(C.scr2[0:64, 0:S], C.scr2[0:64, 0:S], pcm[0:64, :])
        v.drain()
        v.tensor_scalar(C.scr[0:64, 0:S], C.scr2[0:64, 0:S], C.disc[0:64, 3:4],
                        None, Op.add)
        v.drain()
        v.tensor_copy(C.scri[0:64, 0:S], C.scr[0:64, 0:S])
        v.drain()
        v.tensor_copy(C.scr[0:64, 0:S], C.scri[0:64, 0:S])
        v.drain()
        v.tensor_sub(C.scr2[0:64, 0:S], C.scr2[0:64, 0:S], C.scr[0:64, 0:S])
        v.drain()
        v.tensor_tensor(C.scrb[0:64, 0:S], pcm[0:64, :], C.scr2[0:64, 0:S],
                        Op.mult)
        v.drain()
        v.tensor_tensor(ubf[0:64, :], pcm[0:64, :], C.scrb[0:64, 0:S],
                        Op.subtract)
        v.drain().then_inc(C.sems["ve"], 1)
        C.semv["ve"] += 1

    def cs_sy(sy):
        wait(sy, "ve")
        dinc(sy.dma_start(C.pf_h[b, :, :], C.scr[0:64, 0:S]), "st")
        dinc(sy.dma_start(C.u_h[b, :, :], ubf[0:64, :]), "st")
        dinc(sy.dma_start(C.v_h[b, :, :], C.scrb[0:64, 0:S]), "st")
        wait(sy, "st")
    block(vector=cs_v)
    block(sync=cs_sy)
    C.tap(f"pf{b}", lambda: C.scr[0:64, 0:S], [64, S])
    C.tap(f"ubf{b}", lambda: ubf[0:64, :].bitcast(F32), [64, S // 2])
    scope.close()


PAIR_ITERS = [(ib, jc) for ib in range(4) for jc in range(4 * ib + 4)]
PAIR_DIAG = [jc >= 4 * ib for (ib, jc) in PAIR_ITERS]
PAIR_DC = []
_c = 0
for _d in PAIR_DIAG:
    _c += 1 if _d else 0
    PAIR_DC.append(_c)
NIT = len(PAIR_ITERS)


def _pair_prep(C, A, b, g):
    nc, block, inc, dinc, wait, ccol = C.nc, C.block, C.inc, C.dinc, C.wait, C.ccol
    pb = C.pb

    def pp_g(g_):
        dinc(g_.dma_start(
            A.wqp[:, :].rearrange("p (ic c) -> p ic c", c=128),
            C.wq_h[:, g * 128:(g + 1) * 128].rearrange("(ic p) c -> p ic c",
                                                       p=128)))
        dinc(g_.dma_start(
            A.wkp[:, :].rearrange("p (ic c) -> p ic c", c=128),
            C.wk_h[:, g * 128:(g + 1) * 128].rearrange("(ic p) c -> p ic c",
                                                       p=128)))
        vpv = A.vp[:, :].rearrange("p (tt c) -> p tt c", c=130)
        dinc(g_.dma_start(
            vpv[:, :, 0:64],
            C.vtm_h[b, :, :, g * 128:g * 128 + 64].rearrange(
                "tt p c -> p tt c")))
        dinc(g_.dma_start(
            vpv[:, :, 65:129],
            C.vtm_h[b, :, :, g * 128 + 64:g * 128 + 128].rearrange(
                "tt p c -> p tt c")))
        if g == 0:
            g_.memset(vpv[:, :, 64:65], 1.0)
            g_.memset(vpv[:, :, 129:130], 1.0)
        wait(g_, "ld")
        g_.drain().then_inc(C.sems["gp"], 1)
        C.semv["gp"] += 1

    def pp_sy(sy):
        C._ppsy_ve = C.semv["ve"]
        dinc(sy.dma_start(A.strip[0:4, :], A.pbf[8 * g:8 * g + 4, :]), "st")
        dinc(sy.dma_start(A.strip[32:36, :], A.pbf[8 * g + 4:8 * g + 8, :]), "st")
        wait(sy, "ve")
        for (dst, src) in ((A.qrot, A.qbf), (A.krot, A.kbf)):
            dinc(sy.dma_start(dst[0:32, :], src[32:64, :]), "st")
            dinc(sy.dma_start(dst[32:64, :], src[0:32, :]), "st")
            dinc(sy.dma_start(dst[64:96, :], src[96:128, :]), "st")
            dinc(sy.dma_start(dst[96:128, :], src[64:96, :]), "st")
        wait(sy, "st")
        sy.drain().then_inc(C.sems["sy"], 1)
        C.semv["sy"] += 1

    def pp_pe(t):
        C._pp_pe_ve = C.semv["ve"]
        wait(t, "gp")
        for sblk in range(4):
            if sblk >= 1:
                t.wait_ge(C.sems["ve"], C._pp_pe_ve + sblk)
            for ic in range(8):
                rhs = C.xfm[:, ic * NT + b * S + sblk * 512:
                            ic * NT + b * S + sblk * 512 + 512]
                mq = t.matmul(pb[0][:, :], A.wqp[:, ic * 128:(ic + 1) * 128],
                              rhs, start=(ic == 0), stop=(ic == 7))
                mk = t.matmul(pb[1][:, :], A.wkp[:, ic * 128:(ic + 1) * 128],
                              rhs, start=(ic == 0), stop=(ic == 7))
            mq.then_inc(C.sems["pe"], 1)
            mk.then_inc(C.sems["pe"], 1)
        C.semv["pe"] += 8

    def pp_ve(v):
        base_pe = C.semv["pe"] - 8
        for sblk in range(4):
            v.wait_ge(C.sems["pe"], base_pe + 2 * sblk + 2)
            v.tensor_copy(A.qbf[:, sblk * 512:(sblk + 1) * 512], pb[0][:, :])
            v.tensor_copy(A.kbf[:, sblk * 512:(sblk + 1) * 512], pb[1][:, :])
            v.drain().then_inc(C.sems["ve"], 1)
            C.semv["ve"] += 1
        v.drain().then_inc(C.sems["ve"], 1)
        C.semv["ve"] += 1
    block(gpsimd=pp_g, tensor=pp_pe, vector=pp_ve, sync=pp_sy)

    _rope_emit(C, A, b, g)


def _rope_emit(C, A, b, g):
    nc, block, inc, dinc, wait, ccol = C.nc, C.block, C.inc, C.dinc, C.wait, C.ccol
    pb = C.pb

    def vidx(hh, sblk, j):
        return 12 * hh + 3 * sblk + j

    def rp_g(g_):
        C._rp = {k_: C.semv[k_] for k_ in ("ve", "gp", "gp2", "sc", "pe", "ld")}
        base = C._rp
        for hh in range(2):
            if hh == 1:
                g_.wait_ge(C.sems["ve"], base["ve"] + vidx(0, 3, 3))
            row0 = 8 * g + hh * 4
            for (dst, hsrc) in ((A.pfrep, C.pf_h), (A.urep, C.u_h),
                                (A.vrep, C.v_h)):
                src = bass.AP(hsrc.tensor, b * 64 * S + row0 * S,
                              [[S, 4], [0, 32], [1, S]])
                dinc(g_.dma_start(dst[:, :], src))
            g_.wait_ge(C.sems["ld"], C.semv["ld"])
            g_.drain().then_inc(C.sems["gp"], 1)
            C.semv["gp"] += 1
            for sblk in range(4):
                g_.wait_ge(C.sems["ve"], base["ve"] + vidx(hh, sblk, 1))
                g_.tensor_copy(C.scri[:, 0:1024], C.scr[:, 0:1024])
                g_.drain().then_inc(C.sems["gp"], 1)
                C.semv["gp"] += 1
        for sblk in range(4):
            c0 = sblk * 512
            g_.wait_ge(C.sems["ve"], base["ve"] + 24 + 2 * sblk + 1)
            g_.tensor_tensor(A.qr[:, c0:c0 + 512], A.tmpq[:, :], A.tmpq2[:, :],
                             Op.add)
            g_.drain().then_inc(C.sems["gp2"], 1)
            C.semv["gp2"] += 1
            g_.wait_ge(C.sems["ve"], base["ve"] + 24 + 2 * sblk + 2)
            g_.tensor_tensor(A.kr[:, c0:c0 + 512], A.tmpq[:, :], A.tmpq2[:, :],
                             Op.add)
            g_.drain().then_inc(C.sems["gp2"], 1)
            C.semv["gp2"] += 1

    def rp_v(v):
        base = C._rp
        for hh in range(2):
            for sblk in range(4):
                n_ = 4 * hh + sblk
                col = sblk * 512
                v.wait_ge(C.sems["gp"], base["gp"] + hh * 5 + 1)
                if n_ >= 1:
                    ph, psb = divmod(n_ - 1, 4)
                    v.wait_ge(C.sems["gp"], base["gp"] + ph * 5 + psb + 2)
                v.tensor_scalar(C.scr[:, 0:512], A.pfrep[:, col:col + 512],
                                C.wcol[:, 2:3], C.wcol[:, 3:4], Op.mult, Op.add)
                v.tensor_scalar(C.scr[:, 512:1024], A.pfrep[:, col:col + 512],
                                C.wcol[:, 2:3], C.wcol[:, 4:5], Op.mult, Op.add)
                v.drain().then_inc(C.sems["ve"], 1)
                C.semv["ve"] += 1
                v.wait_ge(C.sems["gp"], base["gp"] + hh * 5 + sblk + 2)
                v.tensor_copy(C.scr2[:, 0:1024], C.scri[:, 0:1024])
                v.drain()
                v.tensor_sub(C.scr[:, 0:1024], C.scr[:, 0:1024],
                             C.scr2[:, 0:1024])
                v.drain().then_inc(C.sems["ve"], 1)
                C.semv["ve"] += 1
                v.wait_ge(C.sems["sc"], base["sc"] + n_ + 1)
                if n_ >= 1:
                    v.wait_ge(C.sems["pe"], base["pe"] + 2 * n_)
                v.tensor_tensor(A.prods[:, 0:512], A.urep[:, col:col + 512],
                                A.trigb[:, 512:1024], Op.mult)
                v.tensor_tensor(A.prods[:, 512:1024], A.vrep[:, col:col + 512],
                                A.trigb[:, 512:1024], Op.mult)
                v.tensor_tensor(A.prods[:, 1024:1536], A.urep[:, col:col + 512],
                                A.trigb[:, 0:512], Op.mult)
                v.tensor_tensor(A.prods[:, 1536:2048], A.vrep[:, col:col + 512],
                                A.trigb[:, 0:512], Op.mult)
                v.drain().then_inc(C.sems["ve"], 1)
                C.semv["ve"] += 1
        for sblk in range(4):
            c0 = sblk * 512
            v.wait_ge(C.sems["pe"], base["pe"] + 2 * (4 + sblk) + 2)
            if sblk >= 1:
                v.wait_ge(C.sems["gp2"], base["gp2"] + 2 * sblk)
            v.tensor_tensor(A.tmpq[:, :], A.qbf[:, c0:c0 + 512],
                            pb[sblk][:, :], Op.mult)
            v.tensor_tensor(A.tmpq2[:, :], A.qrot[:, c0:c0 + 512],
                            pb[4 + sblk][:, :], Op.mult)
            v.drain().then_inc(C.sems["ve"], 1)
            C.semv["ve"] += 1
            v.wait_ge(C.sems["gp2"], base["gp2"] + 2 * sblk + 1)
            v.tensor_tensor(A.tmpq[:, :], A.kbf[:, c0:c0 + 512],
                            pb[sblk][:, :], Op.mult)
            v.tensor_tensor(A.tmpq2[:, :], A.krot[:, c0:c0 + 512],
                            pb[4 + sblk][:, :], Op.mult)
            v.drain().then_inc(C.sems["ve"], 1)
            C.semv["ve"] += 1

    def rp_s(s):
        base = C._rp
        for hh in range(2):
            for sblk in range(4):
                n_ = 4 * hh + sblk
                s.wait_ge(C.sems["ve"], base["ve"] + vidx(hh, sblk, 2))
                if n_ >= 1:
                    ph, psb = divmod(n_ - 1, 4)
                    s.wait_ge(C.sems["ve"], base["ve"] + vidx(ph, psb, 3))
                s.activation(A.trigb[:, 0:512], C.scr[:, 0:512], AF.Sin,
                             bias=C.disc[:, 9:10], scale=TWO_PI)
                s.activation(A.trigb[:, 512:1024], C.scr[:, 512:1024], AF.Sin,
                             bias=C.disc[:, 9:10], scale=TWO_PI)
                s.drain().then_inc(C.sems["sc"], 1)
                C.semv["sc"] += 1

    def rp_pe(t):
        base = C._rp
        sel = C.seldup
        for hh in range(2):
            r0 = hh * 64
            for sblk in range(4):
                n_ = 4 * hh + sblk
                t.wait_ge(C.sems["ve"], base["ve"] + vidx(hh, sblk, 3))
                P1 = A.prods[:, 0:512]
                P2 = A.prods[:, 512:1024]
                P3 = A.prods[:, 1024:1536]
                P4 = A.prods[:, 1536:2048]
                cw, sw = pb[sblk], pb[4 + sblk]
                t.matmul(cw[r0:r0 + 64, :], sel[:, 0:64], P1, start=True,
                         stop=False, skip_group_check=True)
                t.matmul(cw[r0:r0 + 64, :], sel[:, 64:128], P2, start=False,
                         stop=False, skip_group_check=True)
                t.matmul(cw[r0:r0 + 64, :], sel[:, 128:192], P4, start=False,
                         stop=True,
                         skip_group_check=True).then_inc(C.sems["pe"], 1)
                t.matmul(sw[r0:r0 + 64, :], sel[:, 192:256], P3, start=True,
                         stop=False, skip_group_check=True)
                t.matmul(sw[r0:r0 + 64, :], sel[:, 256:320], P4, start=False,
                         stop=False, skip_group_check=True)
                t.matmul(sw[r0:r0 + 64, :], sel[:, 320:384], P2, start=False,
                         stop=True,
                         skip_group_check=True).then_inc(C.sems["pe"], 1)
                C.semv["pe"] += 2
    block(gpsimd=rp_g, vector=rp_v, scalar=rp_s, tensor=rp_pe)

    def nb_s2(s):
        C._nb = {k_: C.semv[k_] for k_ in ("ve", "pe", "sc")}
        s.activation(C.scrb[:, 0:2048], A.qr[:, :], AF.Square, bias=ccol(0.0),
                     scale=1.0)
        s.drain().then_inc(C.sems["sc"], 1)
        C.semv["sc"] += 1
        s.wait_ge(C.sems["pe"], C._nb["pe"] + 4)
        s.activation(C.scrb[:, 0:2048], A.kr[:, :], AF.Square, bias=ccol(0.0),
                     scale=1.0)
        s.drain().then_inc(C.sems["sc"], 1)
        C.semv["sc"] += 1

    def nb_pe2(t):
        base_sc = C.semv["sc"] - 2
        t.wait_ge(C.sems["sc"], base_sc + 1)
        for i_, (qk, bias) in enumerate((("q", 0), ("k", 1))):
            if i_ == 1:
                t.wait_ge(C.sems["sc"], base_sc + 2)
            for sblk in range(4):
                bank = pb[4] if sblk % 2 == 0 else pb[5]
                lin = i_ * 4 + sblk
                if lin >= 2:
                    t.wait_ge(C.sems["ve"], C._nb["ve"] + lin - 1)
                t.matmul(bank[0:2, :], C.rowsel2[:, :],
                         C.scrb[:, sblk * 512:(sblk + 1) * 512],
                         start=True, stop=True,
                         skip_group_check=True).then_inc(C.sems["pe"], 1)
                C.semv["pe"] += 1

    def nb_ve(v):
        base_pe = C.semv["pe"] - 8
        for i_ in range(2):
            for sblk in range(4):
                bank = pb[4] if sblk % 2 == 0 else pb[5]
                v.wait_ge(C.sems["pe"], base_pe + i_ * 4 + sblk + 1)
                v.tensor_reduce(A.qn[0:2, i_ * 8 + sblk:i_ * 8 + sblk + 1],
                                bank[0:2, :], AX, Op.max)
                v.drain().then_inc(C.sems["ve"], 1)
                C.semv["ve"] += 1
        v.tensor_tensor(A.qn[0:2, 4:5], A.qn[0:2, 0:1], A.qn[0:2, 1:2], Op.max)
        v.tensor_tensor(A.qn[0:2, 5:6], A.qn[0:2, 2:3], A.qn[0:2, 3:4], Op.max)
        v.tensor_tensor(A.qn[0:2, 12:13], A.qn[0:2, 8:9], A.qn[0:2, 9:10], Op.max)
        v.tensor_tensor(A.qn[0:2, 13:14], A.qn[0:2, 10:11], A.qn[0:2, 11:12], Op.max)
        v.drain()
        v.tensor_tensor(A.qn[0:2, 14:15], A.qn[0:2, 4:5], A.qn[0:2, 5:6], Op.max)
        v.tensor_tensor(A.qn[0:2, 15:16], A.qn[0:2, 12:13], A.qn[0:2, 13:14], Op.max)
        v.drain()
        v.tensor_add(A.qn[0:2, 6:7], A.qn[0:2, 14:15], A.qn[0:2, 15:16])
        v.drain()
        v.tensor_scalar(A.qn[0:2, 7:8], A.qn[0:2, 6:7], -1.01 / 16.0, None,
                        Op.mult)
        v.drain().then_inc(C.sems["ve"], 1)
        C.semv["ve"] += 1

    def nb_pe3(t):
        wait(t, "ve")
        t.matmul(pb[6][:, 0:1], A.selABf[0:2, 0:128], A.qn[0:2, 7:8],
                 start=True, stop=True,
                 skip_group_check=True).then_inc(C.sems["pe"], 1)
        t.matmul(pb[6][:, 1:2], A.selABf[0:2, 128:256], A.qn[0:2, 7:8],
                 start=True, stop=True,
                 skip_group_check=True).then_inc(C.sems["pe"], 1)
        C.semv["pe"] += 2

    def nb_ve2(v):
        wait(v, "pe")
        v.tensor_copy(A.negC[:, 0:2], pb[6][:, 0:2])
    block(scalar=nb_s2, tensor=nb_pe2, vector=nb_ve)
    block(tensor=nb_pe3)
    block(vector=nb_ve2)
    C.tap(f"qr{b}_{g}", lambda: A.qr[:, :].bitcast(F32), [128, S // 2])
    C.tap(f"kr{b}_{g}", lambda: A.kr[:, :].bitcast(F32), [128, S // 2])
    C.tap(f"negC{b}_{g}", lambda: A.negC[:, 0:2], [128, 2])



PAIR_PRE = [0]
for _ib in range(4):
    PAIR_PRE.append(PAIR_PRE[-1] + 4 * _ib + 4)


def _pair_scores(C, A, b, g):
    nc, block, inc, dinc, wait, ccol = C.nc, C.block, C.inc, C.dinc, C.wait, C.ccol
    pb = C.pb

    def sl_pe(t):
        C._sl = {k_: C.semv[k_] for k_ in
                 ("pe2", "ve2", "sc2", "gp2", "ve3", "sy2")}
        b0 = C._sl
        C._sl["scores_done"] = [0] * NIT
        C._sl["nw_done"] = [0] * NIT
        C._sl["pv_done"] = [0] * 4
        C._sl["norm_done"] = [0] * 4
        pe2c = [b0["pe2"]]

        def pinc(mm):
            mm.then_inc(C.sems["pe2"], 1)
            pe2c[0] += 1

        def emit_pv(m):
            ib_, jc_ = PAIR_ITERS[m]
            last = (jc_ == 4 * ib_ + 3)
            if PAIR_DIAG[m]:
                t.wait_ge(C.sems["gp2"], b0["gp2"] + PAIR_DC[m])
            else:
                t.wait_ge(C.sems["ve2"], b0["ve2"] + m + 1)
            if jc_ == 0:
                t.wait_ge(C.sems["ve3"], b0["ve3"] + ib_)
            pA, pB = A.ptA[m % 2], A.ptB[m % 2]
            c0 = jc_ * 130
            mA = t.matmul(pb[6][0:65, :], A.vp[:, c0:c0 + 65], pA[:, :],
                          start=(jc_ == 0), stop=last, skip_group_check=True)
            mB = t.matmul(pb[7][0:65, :], A.vp[:, c0 + 65:c0 + 130], pB[:, :],
                          start=(jc_ == 0), stop=last, skip_group_check=True)
            if last:
                pinc(mA)
                pinc(mB)
                C._sl["pv_done"][ib_] = pe2c[0]

        for m, (ib_, jc_) in enumerate(PAIR_ITERS):
            iA, iB = pb[0 + m % 2], pb[2 + m % 2]
            j0, i0 = jc_ * 128, ib_ * 512
            if m >= 2:
                t.wait_ge(C.sems["sc2"], b0["sc2"] + m - 1)
            pinc(t.matmul(iA[:, :], A.kr[0:64, j0:j0 + 128],
                          A.qr[0:64, i0:i0 + 512], start=True, stop=True,
                          skip_group_check=True))
            pinc(t.matmul(iB[:, :], A.kr[64:128, j0:j0 + 128],
                          A.qr[64:128, i0:i0 + 512], start=True, stop=True,
                          skip_group_check=True))
            C._sl["scores_done"][m] = pe2c[0]
            if m >= 1:
                t.wait_ge(C.sems["ve2"], b0["ve2"] + m)
            pinc(t.matmul(pb[4][:, :], A.strip[0:5, j0:j0 + 128],
                          A.strip[0:5, i0:i0 + 512], start=True, stop=True,
                          skip_group_check=True))
            pinc(t.matmul(pb[5][:, :], A.strip[32:37, j0:j0 + 128],
                          A.strip[32:37, i0:i0 + 512], start=True, stop=True,
                          skip_group_check=True))
            C._sl["nw_done"][m] = pe2c[0]
            if m >= 1:
                emit_pv(m - 1)
        emit_pv(NIT - 1)
        for ib_ in range(4):
            t.wait_ge(C.sems["ve3"], b0["ve3"] + 5 + ib_)
            pinc(t.matmul(pb[7][:, :], C.rsel[ib_][:, :], A.sstg2[0:8, :],
                          start=True, stop=True, skip_group_check=True))
            C._sl["norm_done"][ib_] = pe2c[0]
        C.semv["pe2"] = pe2c[0]

    def sl_sc(s):
        b0 = C._sl
        for m in range(NIT):
            s.wait_ge(C.sems["pe2"], b0["scores_done"][m])
            if m >= 2:
                s.wait_ge(C.sems["ve2"], b0["ve2"] + m - 1)
            s.activation(A.eA[m % 2][:, :], pb[0 + m % 2][:, :], AF.Exp,
                         bias=A.negC[:, 0:1], scale=0.125)
            s.activation(A.eB[m % 2][:, :], pb[2 + m % 2][:, :], AF.Exp,
                         bias=A.negC[:, 1:2], scale=0.125)
            s.drain().then_inc(C.sems["sc2"], 1)
        C.semv["sc2"] += NIT

    def sl_ve(v):
        b0 = C._sl

        def epilogue(ib_):
            v.wait_ge(C.sems["pe2"], b0["pv_done"][ib_])
            v.tensor_copy(A.outev[0:64, ib_ * 512:(ib_ + 1) * 512],
                          pb[6][0:64, :])
            v.tensor_copy(A.outev[64:128, ib_ * 512:(ib_ + 1) * 512],
                          pb[7][0:64, :])
            v.tensor_copy(A.cscr[0:1, ib_ * 512:(ib_ + 1) * 512],
                          pb[6][64:65, :])
            v.tensor_copy(A.cscr[64:65, ib_ * 512:(ib_ + 1) * 512],
                          pb[7][64:65, :])
            v.drain().then_inc(C.sems["ve3"], 1)

        epi_at = {PAIR_PRE[i_ + 1]: i_ for i_ in range(3)}
        for m in range(NIT):
            v.wait_ge(C.sems["sc2"], b0["sc2"] + m + 1)
            v.wait_ge(C.sems["pe2"], b0["nw_done"][m])
            v.tensor_tensor(A.ptA[m % 2][:, :], A.eA[m % 2][:, :], pb[4][:, :],
                            Op.mult)
            v.tensor_tensor(A.ptB[m % 2][:, :], A.eB[m % 2][:, :], pb[5][:, :],
                            Op.mult)
            v.drain().then_inc(C.sems["ve2"], 1)
            if m in epi_at:
                epilogue(epi_at[m])
        C.semv["ve2"] += NIT
        epilogue(3)
        v.wait_ge(C.sems["sy2"], b0["sy2"] + 1)
        v.reciprocal(A.sstg2[0:8, :], A.sstg[0:8, :])
        v.drain().then_inc(C.sems["ve3"], 1)
        for ib_ in range(4):
            v.wait_ge(C.sems["pe2"], b0["norm_done"][ib_])
            v.tensor_tensor(A.attp[:, ib_ * 512:(ib_ + 1) * 512],
                            A.outev[:, ib_ * 512:(ib_ + 1) * 512], pb[7][:, :],
                            Op.mult)
            v.drain().then_inc(C.sems["ve3"], 1)
        C.semv["ve3"] += 9

    def sl_gp(g_):
        b0 = C._sl
        for m in range(NIT):
            if not PAIR_DIAG[m]:
                continue
            ib_, jc_ = PAIR_ITERS[m]
            mi = jc_ - 4 * ib_
            g_.wait_ge(C.sems["ve2"], b0["ve2"] + m + 1)
            g_.tensor_tensor(A.ptA[m % 2][:, :], A.ptA[m % 2][:, :],
                             C.masks[:, mi * 512:(mi + 1) * 512], Op.mult)
            g_.tensor_tensor(A.ptB[m % 2][:, :], A.ptB[m % 2][:, :],
                             C.masks[:, mi * 512:(mi + 1) * 512], Op.mult)
            g_.drain().then_inc(C.sems["gp2"], 1)
        C.semv["gp2"] += PAIR_DC[-1]

    def sl_sy(sy):
        b0 = C._sl
        st0 = C.semv["st"]
        sy.wait_ge(C.sems["ve3"], b0["ve3"] + 4)
        nst = 0
        for h_ in range(2):
            for ib_ in range(4):
                dinc(sy.dma_start(A.sstg[ib_ * 2 + h_:ib_ * 2 + h_ + 1, :],
                                  A.cscr[h_ * 64:h_ * 64 + 1,
                                         ib_ * 512:(ib_ + 1) * 512]), "st")
                nst += 1
        sy.wait_ge(C.sems["st"], st0 + 16 * nst)
        sy.drain().then_inc(C.sems["sy2"], 1)
        C.semv["sy2"] += 1
        sy.wait_ge(C.sems["ve3"], b0["ve3"] + 9)
        dinc(sy.dma_start(C.attno_h[g, :, b * S:(b + 1) * S], A.attp[:, :]),
             "st")
    block(tensor=sl_pe, scalar=sl_sc, vector=sl_ve, gpsimd=sl_gp, sync=sl_sy)
    C.tap(f"attp{b}_{g}", lambda: A.attp[:, :].bitcast(F32), [128, S // 2])


def _wo_ln1_quant(C):
    nc, block, inc, dinc, wait, ccol = C.nc, C.block, C.inc, C.dinc, C.wait, C.ccol
    pb = C.pb
    scope = ExitStack()
    e = scope.enter_context
    wosb = e(nc.sbuf_tensor("wosb", [128, 8 * E], BF16))
    attld = e(nc.sbuf_tensor("attld", [128, 8 * 512], BF16))
    h1f = e(nc.sbuf_tensor("h1f", [128, 8 * 512], F32))
    h1b = e(nc.sbuf_tensor("h1b", [128, 8 * 512], BF16))
    sqb = e(nc.sbuf_tensor("sqb", [128, 8 * 512], BF16))
    xlnb = e(nc.sbuf_tensor("xlnb", [128, 8 * 512], BF16))
    xqb = e(nc.sbuf_tensor("xqb", [128, 8 * 512], BF16))
    amcol = e(nc.sbuf_tensor("amcol", [128, 16], F32))
    rows = e(nc.sbuf_tensor("rows", [1, 8 * 512], F32))
    rowe = e(nc.sbuf_tensor("rowe", [8, 128], F32))
    C.invsx_h = C.dram("invsx_sc", [8, 512], F32, "Internal")

    def wo_ld(g):
        for c in range(8):
            dinc(g.dma_start(wosb[:, c * E:(c + 1) * E],
                             C.wo_h[c * 128:(c + 1) * 128, :]))
        wait(g, "ld")
    block(gpsimd=wo_ld)

    for tb in range(8):
        t0 = tb * 512
        def a_lsy(sy, t0=t0):
            inc(sy.dma_start(attld[:, :].rearrange("p (ic t) -> p ic t", t=512),
                             C.attno_h[:, :, t0:t0 + 512].rearrange(
                                 "ic p t -> p ic t")), "lds", 16)
            sy.wait_ge(C.sems["lds"], C.semv["lds"])
            sy.drain().then_inc(C.sems["sy"], 1)
            C.semv["sy"] += 1

        def a_pe(t, t0=t0):
            wait(t, "sy")
            for oc in range(8):
                for ic in range(8):
                    t.matmul(pb[oc][:, :],
                             wosb[:, ic * E + oc * 128:ic * E + oc * 128 + 128],
                             attld[:, ic * 512:(ic + 1) * 512],
                             start=(ic == 0), stop=(ic == 7))
        block(sync=a_lsy, tensor=a_pe)

        def b_v(v, t0=t0):
            C._p4b = C.semv["ve"]
            for oc in range(8):
                v.tensor_tensor(h1f[:, oc * 512:(oc + 1) * 512],
                                C.xfm[:, oc * NT + t0:oc * NT + t0 + 512],
                                pb[oc][:, :], Op.add)
                v.drain()
                v.tensor_copy(h1b[:, oc * 512:(oc + 1) * 512],
                              h1f[:, oc * 512:(oc + 1) * 512])
                v.drain().then_inc(C.sems["ve"], 1)
                C.semv["ve"] += 1

        def b_s(s, t0=t0):
            for oc in range(8):
                s.wait_ge(C.sems["ve"], C._p4b + oc + 1)
                s.activation(sqb[:, oc * 512:(oc + 1) * 512],
                             h1f[:, oc * 512:(oc + 1) * 512], AF.Square,
                             bias=ccol(0.0), scale=1.0)
                s.drain().then_inc(C.sems["sc"], 1)
                C.semv["sc"] += 1
        block(vector=b_v, scalar=b_s)

        def c_pe(t, t0=t0):
            for oc in range(8):
                m0 = t.matmul(pb[0][0:1, :], C.ones_bf[:, 0:1],
                              h1b[:, oc * 512:(oc + 1) * 512],
                              start=(oc == 0), stop=(oc == 7),
                              skip_group_check=True)
                m1 = t.matmul(pb[1][0:1, :], C.ones_bf[:, 0:1],
                              sqb[:, oc * 512:(oc + 1) * 512],
                              start=(oc == 0), stop=(oc == 7),
                              skip_group_check=True)
            m0.then_inc(C.sems["pe"], 1)
            m1.then_inc(C.sems["pe"], 1)
            C.semv["pe"] += 2

        def c_v(v, t0=t0):
            wait(v, "pe")
            v.tensor_scalar(rows[0:1, 0:512], pb[0][0:1, :], 1.0 / E, None,
                            Op.mult)
            v.tensor_scalar(rows[0:1, 512:1024], pb[1][0:1, :], 1.0 / E, None,
                            Op.mult)
            for q in range(4):
                v.tensor_tensor(C.scr[:, q * 512:(q + 1) * 512],
                                h1f[:, 2 * q * 512:(2 * q + 1) * 512],
                                h1f[:, (2 * q + 1) * 512:(2 * q + 2) * 512],
                                Op.max)
                v.tensor_tensor(C.scr2[:, q * 512:(q + 1) * 512],
                                h1f[:, 2 * q * 512:(2 * q + 1) * 512],
                                h1f[:, (2 * q + 1) * 512:(2 * q + 2) * 512],
                                Op.min)
            v.drain()
            v.tensor_tensor(rows[0:1, 1024:1536], rows[0:1, 0:512],
                            rows[0:1, 0:512], Op.mult)
            v.tensor_tensor(C.scr[:, 0:512], C.scr[:, 0:512],
                            C.scr[:, 512:1024], Op.max)
            v.tensor_tensor(C.scr[:, 1024:1536], C.scr[:, 1024:1536],
                            C.scr[:, 1536:2048], Op.max)
            v.tensor_tensor(C.scr2[:, 0:512], C.scr2[:, 0:512],
                            C.scr2[:, 512:1024], Op.min)
            v.tensor_tensor(C.scr2[:, 1024:1536], C.scr2[:, 1024:1536],
                            C.scr2[:, 1536:2048], Op.min)
            v.drain()
            v.tensor_sub(rows[0:1, 1024:1536], rows[0:1, 512:1024],
                         rows[0:1, 1024:1536])
            v.tensor_tensor(C.scr[:, 0:512], C.scr[:, 0:512],
                            C.scr[:, 1024:1536], Op.max)
            v.tensor_tensor(C.scr2[:, 0:512], C.scr2[:, 0:512],
                            C.scr2[:, 1024:1536], Op.min)
            v.drain().then_inc(C.sems["ve"], 1)
            C.semv["ve"] += 1

        def c_s(s, t0=t0):
            wait(s, "ve")
            s.activation(rows[0:1, 1536:2048], rows[0:1, 1024:1536], AF.Sqrt,
                         bias=ccol(EPS_LN)[0:1, :], scale=1.0)
            s.drain().then_inc(C.sems["sc"], 1)
            C.semv["sc"] += 1

        def c_pe2(t, t0=t0):
            wait(t, "ve")
            for q in range(4):
                t.transpose(pb[2][:, q * 128:(q + 1) * 128],
                            C.scr[:, q * 128:(q + 1) * 128],
                            C.idn_f[:, :]).then_inc(C.sems["pe"], 1)
                t.transpose(pb[3][:, q * 128:(q + 1) * 128],
                            C.scr2[:, q * 128:(q + 1) * 128],
                            C.idn_f[:, :]).then_inc(C.sems["pe"], 1)
            C.semv["pe"] += 8

        def c_v2(v, t0=t0):
            wait(v, "pe")
            wait(v, "sc")
            for q in range(4):
                v.tensor_reduce(amcol[:, q:q + 1],
                                pb[2][:, q * 128:(q + 1) * 128], AX, Op.max)
                v.tensor_reduce(amcol[:, 4 + q:5 + q],
                                pb[3][:, q * 128:(q + 1) * 128], AX, Op.min)
            v.reciprocal(rows[0:1, 1024:1536], rows[0:1, 1536:2048])
            v.drain()
            v.tensor_tensor(rows[0:1, 1536:2048], rows[0:1, 0:512],
                            rows[0:1, 1024:1536], Op.mult)
            v.drain().then_inc(C.sems["ve"], 1)
            C.semv["ve"] += 1

        def c_pe3(t, t0=t0):
            wait(t, "ve")
            t.transpose(pb[4][0:8, 0:128], amcol[:, 0:8],
                        C.idn_f[:, :]).then_inc(C.sems["pe"], 1)
            C.semv["pe"] += 1

        def c_v3(v, t0=t0):
            wait(v, "pe")
            v.tensor_copy(rowe[0:8, :], pb[4][0:8, 0:128])
            v.drain().then_inc(C.sems["ve"], 1)
            C.semv["ve"] += 1

        def c_sy(sy, tb=tb):
            wait(sy, "ve")
            for q in range(4):
                dinc(sy.dma_start(C.scr[0:1, 1024 + q * 128:1152 + q * 128],
                                  rowe[q:q + 1, :]), "st")
                dinc(sy.dma_start(C.scr[0:1, 1536 + q * 128:1664 + q * 128],
                                  rowe[4 + q:5 + q, :]), "st")
            wait(sy, "st")
            sy.drain().then_inc(C.sems["sy"], 1)
            C.semv["sy"] += 1

        def c_v4(v, tb=tb):
            wait(v, "sy")
            v.tensor_sub(C.scr2[0:1, 0:512], C.scr[0:1, 1024:1536],
                         rows[0:1, 0:512])
            v.tensor_sub(C.scr2[0:1, 512:1024], rows[0:1, 0:512],
                         C.scr[0:1, 1536:2048])
            v.drain()
            v.tensor_tensor(C.scr2[0:1, 0:512], C.scr2[0:1, 0:512],
                            C.scr2[0:1, 512:1024], Op.max)
            v.drain()
            v.tensor_tensor(C.scr2[0:1, 0:512], C.scr2[0:1, 0:512],
                            rows[0:1, 1024:1536], Op.mult)
            v.drain()
            v.tensor_scalar(C.scr2[0:1, 0:512], C.scr2[0:1, 0:512], 1e-5,
                            None, Op.max)
            v.drain()
            v.reciprocal(C.scr2[0:1, 512:1024], C.scr2[0:1, 0:512])
            v.drain()
            v.tensor_scalar(C.scr[0:1, 1024:1536], C.scr2[0:1, 512:1024],
                            127.0, None, Op.mult)
            v.tensor_scalar(C.scr[0:1, 1536:2048], C.scr2[0:1, 0:512],
                            1.0 / 127.0, None, Op.mult)
            v.drain().then_inc(C.sems["ve"], 1)
            C.semv["ve"] += 1

        def c_sy2(sy, tb=tb):
            wait(sy, "ve")
            dinc(sy.dma_start(C.invsx_h[tb, :], C.scr[0:1, 1536:2048]), "st")
            wait(sy, "st")
        block(tensor=c_pe, vector=c_v)
        block(scalar=c_s, tensor=c_pe2)
        block(vector=c_v2)
        block(tensor=c_pe3)
        block(vector=c_v3)
        block(sync=c_sy)
        block(vector=c_v4)
        block(sync=c_sy2)

        def d_pe(t, t0=t0):
            wait(t, "sy")
            t.matmul(pb[2][:, :], C.onerow_f[0:1, :], rows[0:1, 1024:1536],
                     start=True, stop=True,
                     skip_group_check=True).then_inc(C.sems["pe"], 1)
            t.matmul(pb[3][:, :], C.onerow_f[0:1, :], rows[0:1, 1536:2048],
                     start=True, stop=True,
                     skip_group_check=True).then_inc(C.sems["pe"], 1)
            t.matmul(pb[4][:, :], C.onerow_f[0:1, :], C.scr[0:1, 1024:1536],
                     start=True, stop=True,
                     skip_group_check=True).then_inc(C.sems["pe"], 1)
            C.semv["pe"] += 3

        def d_v(v, t0=t0):
            wait(v, "pe")
            for oc in range(8):
                sl = slice(oc * 512, (oc + 1) * 512)
                v.tensor_tensor(C.scr[:, 0:512], h1f[:, sl], pb[2][:, :],
                                Op.mult)
                v.drain()
                v.tensor_sub(C.scr[:, 0:512], C.scr[:, 0:512], pb[3][:, :])
                v.drain()
                v.tensor_copy(xlnb[:, sl], C.scr[:, 0:512])
                v.tensor_tensor(C.scr2[:, 0:512], C.scr[:, 0:512], pb[4][:, :],
                                Op.mult)
                v.drain()
                v.tensor_scalar(C.scr2[:, 0:512], C.scr2[:, 0:512],
                                C.disc[:, 2:3], None, Op.add)
                v.drain().then_inc(C.sems["ve"], 1)
                C.semv["ve"] += 1
                v.wait_ge(C.sems["gp"], C._p4g + oc + 1)
                v.tensor_copy(xqb[:, sl], C.scri[:, 0:512])
                v.drain().then_inc(C.sems["ve"], 1)
                C.semv["ve"] += 1

        def d_g(g, t0=t0):
            C._p4g = C.semv["gp"]
            b_ve = C.semv["ve"]
            for oc in range(8):
                g.wait_ge(C.sems["ve"], b_ve + 2 * oc + 1)
                g.tensor_copy(C.scri[:, 0:512], C.scr2[:, 0:512])
                g.drain().then_inc(C.sems["gp"], 1)
                C.semv["gp"] += 1

        def d_sy(sy, tb=tb, t0=t0):
            wait(sy, "ve")
            dinc(sy.dma_start(C.xln_h[:, :, t0:t0 + 512].rearrange(
                "ic p t -> p ic t"), xlnb[:, :].rearrange(
                "p (ic t) -> p ic t", t=512)), "st")
            dinc(sy.dma_start(C.xq_h[:, :, t0:t0 + 512].rearrange(
                "ic p t -> p ic t"), xqb[:, :].rearrange(
                "p (ic t) -> p ic t", t=512)), "st")
            wait(sy, "st")
        block(tensor=d_pe, gpsimd=d_g, vector=d_v)
        block(sync=d_sy)
        C.tap(f"xln{tb}", lambda t0=t0: xlnb[:, :].bitcast(F32), [128, 4 * 512])
        C.tap(f"xq{tb}", lambda t0=t0: xqb[:, :].bitcast(F32), [128, 4 * 512])
        C.tap(f"h1f{tb}", lambda t0=t0: h1f[:, :], [128, 8 * 512])
    scope.close()


def _ffn(C):
    nc, block, inc, dinc, wait, ccol = C.nc, C.block, C.inc, C.dinc, C.wait, C.ccol
    pb = C.pb
    C._xfm_cm.__exit__(None, None, None)
    scope = ExitStack()
    e = scope.enter_context
    hblk = e(nc.sbuf_tensor("hblk", [128, 32 * 512], F32))
    hqb = e(nc.sbuf_tensor("hqb", [128, 32 * 512], BF16))
    xqld = e(nc.sbuf_tensor("xqld", [128, 8 * 512], BF16))
    sq2 = xqld
    xlnld = e(nc.sbuf_tensor("xlnld", [128, 8 * 512], BF16))
    wgst = [e(nc.sbuf_tensor(f"wgst{i}", [128, 1024], BF16)) for i in range(2)]
    wust = [e(nc.sbuf_tensor(f"wust{i}", [128, 1024], BF16)) for i in range(2)]
    wdst = [e(nc.sbuf_tensor(f"wdst{i}", [128, 4096], BF16)) for i in range(2)]
    sgf = e(nc.sbuf_tensor("sgf", [128, 512], F32))
    t2f = e(nc.sbuf_tensor("t2f", [128, 512], F32))
    ibc = e(nc.sbuf_tensor("ibc", [128, 512], F32))
    sbc = e(nc.sbuf_tensor("sbc", [128, 512], F32))
    dbc = e(nc.sbuf_tensor("dbc", [128, 512], F32))
    frow = e(nc.sbuf_tensor("frow", [1, 2 * 512], F32))
    amcol = e(nc.sbuf_tensor("amcol2", [128, 16], F32))
    rowe = e(nc.sbuf_tensor("rowe2", [8, 128], F32))
    h2f = hblk[:, 4096:8192]
    h2b = hblk.bitcast(BF16)[:, 16384:20480]
    outf = hblk[:, 0:4096]
    otm = e(nc.sbuf_tensor("otm", [128, 2 * E], F32))

    def f0_v(v):
        v.tensor_scalar(C.rsg[:, :], C.rsg[:, :], C.disc[:, 4:5], -1.0,
                        Op.mult, Op.mult)
        v.reciprocal(C.wqs[:, 3:6], C.wqs[:, 0:3])
        v.drain().then_inc(C.sems["ve"], 1)
        C.semv["ve"] += 1
    block(vector=f0_v)

    for tb in range(8):
        t0 = tb * 512
        def a_s(s, tb=tb, t0=t0):
            C._f5 = {k_: C.semv[k_] for k_ in
                     ("ld2", "lds", "lda", "gp", "ve", "sc", "pe")}
            b5 = C._f5
            inc(s.dma_start(xqld[:, :].rearrange("p (ic t) -> p ic t", t=512),
                            C.xq_h[:, :, t0:t0 + 512].rearrange(
                                "ic p t -> p ic t")), "lda", 16)
            inc(s.dma_start(C.scr[0:1, 0:512], C.invsx_h[tb, :]), "lda", 16)
            s.wait_ge(C.sems["lda"], C.semv["lda"])
            for icn in range(32):
                s.wait_ge(C.sems["ve"], b5["ve"] + 2 + 2 * icn)
                s.activation(sgf[:, :], C.scr2[:, 0:512], AF.Silu,
                             bias=ccol(0.0), scale=C.wqs[:, 3:4])
                s.drain().then_inc(C.sems["sc"], 1)
                C.semv["sc"] += 1

        def a_ls(sy, tb=tb, t0=t0):
            b5 = C._f5
            for icn in range(32):
                if icn >= 2:
                    sy.wait_ge(C.sems["pe"], b5["pe"] + 2 * icn - 1)
                if icn >= 1:
                    sy.wait_ge(C.sems["lds"], b5["lds"] + 32 * icn)
                inc(sy.dma_start(wgst[icn % 2][:, :].rearrange(
                    "p (ic c) -> p ic c", c=128),
                    C.wgT_h[icn, :, :].rearrange("(ic p) c -> p ic c", p=128)),
                    "lds", 16)
                inc(sy.dma_start(wust[icn % 2][:, :].rearrange(
                    "p (ic c) -> p ic c", c=128),
                    C.wuT_h[icn, :, :].rearrange("(ic p) c -> p ic c", p=128)),
                    "lds", 16)

        def a_pe(t, tb=tb, t0=t0):
            b5 = C._f5
            t.wait_ge(C.sems["lda"], b5["lda"] + 32)
            mm = t.matmul(pb[4][:, :], C.onerow_f[0:1, :], C.scr[0:1, 0:512],
                          start=True, stop=True, skip_group_check=True)
            mm.then_inc(C.sems["pe"], 1)
            C.semv["pe"] += 1
            for icn in range(32):
                t.wait_ge(C.sems["lds"], b5["lds"] + 32 * (icn + 1))
                if icn >= 2:
                    t.wait_ge(C.sems["ve"], b5["ve"] + 1 + 2 * (icn - 1))
                bg, bu = pb[icn % 2], pb[2 + icn % 2]
                for ic in range(8):
                    mm = t.matmul(bg[:, :],
                                  wgst[icn % 2][:, ic * 128:(ic + 1) * 128],
                                  xqld[:, ic * 512:(ic + 1) * 512],
                                  start=(ic == 0), stop=(ic == 7))
                mm.then_inc(C.sems["pe"], 1)
                for ic in range(8):
                    mm = t.matmul(bu[:, :],
                                  wust[icn % 2][:, ic * 128:(ic + 1) * 128],
                                  xqld[:, ic * 512:(ic + 1) * 512],
                                  start=(ic == 0), stop=(ic == 7))
                mm.then_inc(C.sems["pe"], 1)
                C.semv["pe"] += 2

        def a_v(v, tb=tb, t0=t0):
            b5 = C._f5
            v.wait_ge(C.sems["pe"], b5["pe"] + 1)
            v.tensor_copy(ibc[:, :], pb[4][:, :])
            v.drain().then_inc(C.sems["ve"], 1)
            C.semv["ve"] += 1
            for icn in range(32):
                sl = slice(icn * 512, (icn + 1) * 512)
                v.wait_ge(C.sems["pe"], b5["pe"] + 2 * icn + 2)
                v.scalar_tensor_tensor(C.scr2[:, 0:512], pb[icn % 2][:, :],
                                       C.rsg[:, icn:icn + 1], ibc[:, :],
                                       Op.add, Op.mult)
                v.drain().then_inc(C.sems["ve"], 1)
                C.semv["ve"] += 1
                v.wait_ge(C.sems["pe"], b5["pe"] + 2 * icn + 3)
                v.wait_ge(C.sems["sc"], b5["sc"] + icn + 1)
                v.scalar_tensor_tensor(t2f[:, :], pb[2 + icn % 2][:, :],
                                       C.rsg[:, 32 + icn:33 + icn], ibc[:, :],
                                       Op.add, Op.mult)
                v.drain()
                v.tensor_tensor(hblk[:, sl], sgf[:, :], t2f[:, :], Op.mult)
                v.drain()
                if icn == 0:
                    v.tensor_copy(C.scr[:, 512:1024], hblk[:, sl])
                    v.tensor_copy(C.scr[:, 1024:1536], hblk[:, sl])
                else:
                    v.tensor_tensor(C.scr[:, 512:1024], C.scr[:, 512:1024],
                                    hblk[:, sl], Op.max)
                    v.tensor_tensor(C.scr[:, 1024:1536], C.scr[:, 1024:1536],
                                    hblk[:, sl], Op.min)
                v.drain().then_inc(C.sems["ve"], 1)
                C.semv["ve"] += 1

        block(scalar=a_s, sync=a_ls, tensor=a_pe, vector=a_v)

        def bb_pe(t, tb=tb):
            for q in range(4):
                t.transpose(pb[2][:, q * 128:(q + 1) * 128],
                            C.scr[:, 512 + q * 128:640 + q * 128],
                            C.idn_f[:, :]).then_inc(C.sems["pe"], 1)
                t.transpose(pb[3][:, q * 128:(q + 1) * 128],
                            C.scr[:, 1024 + q * 128:1152 + q * 128],
                            C.idn_f[:, :]).then_inc(C.sems["pe"], 1)
            C.semv["pe"] += 8

        def bb_v(v, tb=tb):
            wait(v, "pe")
            for q in range(4):
                v.tensor_reduce(amcol[:, q:q + 1],
                                pb[2][:, q * 128:(q + 1) * 128], AX, Op.max)
                v.tensor_reduce(amcol[:, 4 + q:5 + q],
                                pb[3][:, q * 128:(q + 1) * 128], AX, Op.min)
            v.drain()
            v.tensor_scalar(amcol[:, 4:8], amcol[:, 4:8], -1.0, None, Op.mult)
            v.drain()
            v.tensor_tensor(amcol[:, 0:4], amcol[:, 0:4], amcol[:, 4:8], Op.max)
            v.drain()
            v.tensor_scalar(amcol[:, 8:12], amcol[:, 0:4], C.wqs[:, 4:5],
                            None, Op.mult)
            v.drain()
            v.tensor_scalar(amcol[:, 8:12], amcol[:, 8:12], 1e-5, None, Op.max)
            v.drain()
            v.reciprocal(amcol[:, 12:16], amcol[:, 8:12])
            v.drain()
            v.tensor_scalar(amcol[:, 0:4], amcol[:, 12:16], 127.0,
                            C.wqs[:, 4:5], Op.mult, Op.mult)
            v.tensor_scalar(amcol[:, 4:8], amcol[:, 8:12], C.wqs[:, 5:6],
                            1.0 / 127.0, Op.mult, Op.mult)
            v.drain().then_inc(C.sems["ve"], 1)
            C.semv["ve"] += 1

        def bb_pe2(t, tb=tb):
            wait(t, "ve")
            t.transpose(pb[4][0:8, 0:128], amcol[:, 0:8],
                        C.idn_f[:, :]).then_inc(C.sems["pe"], 1)
            C.semv["pe"] += 1

        def bb_v2(v, tb=tb):
            wait(v, "pe")
            v.tensor_copy(rowe[0:8, :], pb[4][0:8, 0:128])
            v.drain().then_inc(C.sems["ve"], 1)
            C.semv["ve"] += 1

        def bb_sy(sy, tb=tb):
            wait(sy, "ve")
            st0 = C.semv["st"]
            for q in range(4):
                dinc(sy.dma_start(frow[0:1, q * 128:(q + 1) * 128],
                                  rowe[q:q + 1, :]), "st")
                dinc(sy.dma_start(frow[0:1, 512 + q * 128:640 + q * 128],
                                  rowe[4 + q:5 + q, :]), "st")
            wait(sy, "st")
            sy.drain().then_inc(C.sems["sy"], 1)
            C.semv["sy"] += 1
        block(tensor=bb_pe)
        block(vector=bb_v)
        block(tensor=bb_pe2)
        block(vector=bb_v2)
        block(sync=bb_sy)

        def cc_pe(t, tb=tb):
            wait(t, "sy")
            t.matmul(pb[4][:, :], C.onerow_f[0:1, :], frow[0:1, 0:512],
                     start=True, stop=True,
                     skip_group_check=True).then_inc(C.sems["pe"], 1)
            C.semv["pe"] += 1

        def cc_v(v, tb=tb):
            wait(v, "pe")
            v.tensor_copy(sbc[:, :], pb[4][:, :])
            v.drain()
            for icn in range(32):
                sl = slice(icn * 512, (icn + 1) * 512)
                v.tensor_tensor(C.scr2[:, 0:512], hblk[:, sl], sbc[:, :],
                                Op.mult)
                v.drain()
                v.tensor_scalar(C.scr2[:, 0:512], C.scr2[:, 0:512],
                                C.disc[:, 2:3], None, Op.add)
                v.drain().then_inc(C.sems["ve"], 1)
                C.semv["ve"] += 1
                v.wait_ge(C.sems["gp"], C._f5cg + icn + 1)
                v.tensor_copy(hqb[:, sl], C.scri[:, 512:1024])
                v.drain().then_inc(C.sems["ve"], 1)
                C.semv["ve"] += 1

        def cc_g(g, tb=tb):
            C._f5cg = C.semv["gp"]
            b_ve = C.semv["ve"]
            for icn in range(32):
                g.wait_ge(C.sems["ve"], b_ve + 2 * icn + 1)
                g.tensor_copy(C.scri[:, 512:1024], C.scr2[:, 0:512])
                g.drain().then_inc(C.sems["gp"], 1)
                C.semv["gp"] += 1
        block(tensor=cc_pe, gpsimd=cc_g, vector=cc_v)

        def dd_ls(sy, tb=tb, t0=t0):
            C._f5d = {k_: C.semv[k_] for k_ in
                      ("ld2", "lds", "pe", "ve", "sc")}
            b5 = C._f5d
            inc(sy.dma_start(xlnld[:, :].rearrange("p (ic t) -> p ic t", t=512),
                             C.xln_h[:, :, t0:t0 + 512].rearrange(
                                 "ic p t -> p ic t")), "lds", 16)
            for oc in range(8):
                if oc >= 2:
                    sy.wait_ge(C.sems["pe"], b5["pe"] + oc)
                sy.wait_ge(C.sems["lds"], b5["lds"] + 16 + 16 * oc)
                inc(sy.dma_start(wdst[oc % 2][:, :].rearrange(
                    "p (ic c) -> p ic c", c=128),
                    C.wdT_h[oc, :, :].rearrange("(ic p) c -> p ic c", p=128)),
                    "lds", 16)

        def dd_pe(t, tb=tb, t0=t0):
            b5 = C._f5d
            mm = t.matmul(pb[4][:, :], C.onerow_f[0:1, :], frow[0:1, 512:1024],
                          start=True, stop=True, skip_group_check=True)
            mm.then_inc(C.sems["pe"], 1)
            C.semv["pe"] += 1
            for oc in range(8):
                t.wait_ge(C.sems["lds"], b5["lds"] + 16 + 16 * (oc + 1))
                if oc >= 2:
                    t.wait_ge(C.sems["ve"], b5["ve"] + 1 + oc - 1)
                bd = pb[oc % 2]
                for icn in range(32):
                    mm = t.matmul(bd[:, :],
                                  wdst[oc % 2][:, icn * 128:(icn + 1) * 128],
                                  hqb[:, icn * 512:(icn + 1) * 512],
                                  start=(icn == 0), stop=(icn == 31))
                mm.then_inc(C.sems["pe"], 1)
                C.semv["pe"] += 1

        def dd_v(v, tb=tb, t0=t0):
            b5 = C._f5d
            v.wait_ge(C.sems["pe"], b5["pe"] + 1)
            v.tensor_copy(dbc[:, :], pb[4][:, :])
            v.drain().then_inc(C.sems["ve"], 1)
            C.semv["ve"] += 1
            for oc in range(8):
                sl = slice(oc * 512, (oc + 1) * 512)
                v.wait_ge(C.sems["pe"], b5["pe"] + oc + 2)
                v.scalar_tensor_tensor(C.scr2[:, 0:512], pb[oc % 2][:, :],
                                       C.rsg[:, 64 + oc:65 + oc], dbc[:, :],
                                       Op.add, Op.mult)
                v.drain()
                v.tensor_tensor(h2f[:, sl], xlnld[:, oc * 512:(oc + 1) * 512],
                                C.scr2[:, 0:512], Op.add)
                v.drain()
                v.tensor_copy(h2b[:, sl], h2f[:, sl])
                v.drain().then_inc(C.sems["ve"], 1)
                C.semv["ve"] += 1

        def dd_s(s, tb=tb):
            b5 = C._f5d
            for oc in range(8):
                s.wait_ge(C.sems["ve"], b5["ve"] + oc + 2)
                s.activation(sq2[:, oc * 512:(oc + 1) * 512],
                             h2f[:, oc * 512:(oc + 1) * 512], AF.Square,
                             bias=ccol(0.0), scale=1.0)
                s.drain().then_inc(C.sems["sc"], 1)
                C.semv["sc"] += 1
        block(sync=dd_ls, tensor=dd_pe, vector=dd_v, scalar=dd_s)

        def ee_pe(t, tb=tb):
            for oc in range(8):
                m0 = t.matmul(pb[2][0:1, :], C.ones_bf[:, 0:1],
                              h2b[:, oc * 512:(oc + 1) * 512],
                              start=(oc == 0), stop=(oc == 7),
                              skip_group_check=True)
                m1 = t.matmul(pb[3][0:1, :], C.ones_bf[:, 0:1],
                              sq2[:, oc * 512:(oc + 1) * 512],
                              start=(oc == 0), stop=(oc == 7),
                              skip_group_check=True)
            m0.then_inc(C.sems["pe"], 1)
            m1.then_inc(C.sems["pe"], 1)
            C.semv["pe"] += 2

        def ee_v(v, tb=tb):
            wait(v, "pe")
            v.tensor_scalar(C.scr[0:1, 0:512], pb[2][0:1, :], 1.0 / E, None,
                            Op.mult)
            v.tensor_scalar(C.scr[0:1, 512:1024], pb[3][0:1, :], 1.0 / E, None,
                            Op.mult)
            v.drain()
            v.tensor_tensor(C.scr[0:1, 1024:1536], C.scr[0:1, 0:512],
                            C.scr[0:1, 0:512], Op.mult)
            v.drain()
            v.tensor_sub(C.scr[0:1, 1024:1536], C.scr[0:1, 512:1024],
                         C.scr[0:1, 1024:1536])
            v.drain().then_inc(C.sems["ve"], 1)
            C.semv["ve"] += 1

        def ee_s(s, tb=tb):
            wait(s, "ve")
            s.activation(C.scr[0:1, 1536:2048], C.scr[0:1, 1024:1536], AF.Sqrt,
                         bias=ccol(EPS_LN)[0:1, :], scale=1.0)
            s.drain().then_inc(C.sems["sc"], 1)
            C.semv["sc"] += 1

        def ee_v2(v, tb=tb):
            wait(v, "sc")
            v.reciprocal(C.scr[0:1, 1024:1536], C.scr[0:1, 1536:2048])
            v.drain()
            v.tensor_tensor(C.scr[0:1, 1536:2048], C.scr[0:1, 0:512],
                            C.scr[0:1, 1024:1536], Op.mult)
            v.drain().then_inc(C.sems["ve"], 1)
            C.semv["ve"] += 1

        def ee_pe2(t, tb=tb):
            wait(t, "ve")
            t.matmul(pb[2][:, :], C.onerow_f[0:1, :], C.scr[0:1, 1024:1536],
                     start=True, stop=True,
                     skip_group_check=True).then_inc(C.sems["pe"], 1)
            t.matmul(pb[3][:, :], C.onerow_f[0:1, :], C.scr[0:1, 1536:2048],
                     start=True, stop=True,
                     skip_group_check=True).then_inc(C.sems["pe"], 1)
            C.semv["pe"] += 2

        def ee_v3(v, tb=tb):
            wait(v, "pe")
            for oc in range(8):
                sl = slice(oc * 512, (oc + 1) * 512)
                v.tensor_tensor(outf[:, sl], h2f[:, sl], pb[2][:, :], Op.mult)
                v.drain()
                v.tensor_sub(outf[:, sl], outf[:, sl], pb[3][:, :])
                v.drain()
            v.drain().then_inc(C.sems["ve"], 1)
            C.semv["ve"] += 1
        block(tensor=ee_pe)
        block(vector=ee_v)
        block(scalar=ee_s)
        block(vector=ee_v2)
        block(tensor=ee_pe2)
        block(vector=ee_v3)

        def ff_pe(t, tb=tb):
            wait(t, "ve")
            C._f5f = C.semv["ve"]
            C._f5st = C.semv["st"]
            for q in range(4):
                bank = pb[4] if q % 2 == 0 else pb[5]
                if q >= 2:
                    t.wait_ge(C.sems["ve"], C._f5f + q - 1)
                for oc in range(4):
                    t.transpose(bank[:, oc * 128:(oc + 1) * 128],
                                outf[:, oc * 512 + q * 128:
                                     oc * 512 + q * 128 + 128],
                                C.idn_f[:, :]).then_inc(C.sems["pe"], 1)
                C.semv["pe"] += 4
                bank2 = pb[6] if q % 2 == 0 else pb[7]
                for oc in range(4):
                    t.transpose(bank2[:, oc * 128:(oc + 1) * 128],
                                outf[:, (4 + oc) * 512 + q * 128:
                                     (4 + oc) * 512 + q * 128 + 128],
                                C.idn_f[:, :]).then_inc(C.sems["pe"], 1)
                C.semv["pe"] += 4

        def ff_v(v, tb=tb):
            base_pe = C.semv["pe"] - 32
            for q in range(4):
                bank = pb[4] if q % 2 == 0 else pb[5]
                bank2 = pb[6] if q % 2 == 0 else pb[7]
                v.wait_ge(C.sems["pe"], base_pe + 8 * q + 8)
                if q >= 2:
                    v.wait_ge(C.sems["st"], C._f5st + 16 * (q - 1))
                qe = (q % 2) * E
                for oc in range(4):
                    v.tensor_copy(otm[:, qe + oc * 128:qe + (oc + 1) * 128],
                                  bank[:, oc * 128:(oc + 1) * 128])
                    v.tensor_copy(otm[:, qe + (4 + oc) * 128:
                                      qe + (5 + oc) * 128],
                                  bank2[:, oc * 128:(oc + 1) * 128])
                v.drain().then_inc(C.sems["ve"], 1)
                C.semv["ve"] += 1

        def ff_sy(sy, tb=tb, t0=t0):
            base_ve = C.semv["ve"] - 4
            for q in range(4):
                sy.wait_ge(C.sems["ve"], base_ve + q + 1)
                if q >= 1:
                    sy.wait_ge(C.sems["st"], C._f5st + 16 * q)
                tok = t0 + q * 128
                b_ = tok // S
                s_ = tok % S
                dinc(sy.dma_start(C.out_h[b_, s_:s_ + 128, :],
                                  otm[:, (q % 2) * E:(q % 2 + 1) * E]), "st")
        block(tensor=ff_pe, vector=ff_v, sync=ff_sy)
        C.tap(f"outt{tb}", lambda: otm[:, :], [128, 2 * E])
    scope.close()


_BUILT = {}


def _get_built():
    if "C" not in _BUILT:
        _BUILT["C"] = build(tap_req=_TAP_REQ)
    return _BUILT["C"]


def kernel(L0_attn_x, L0_attn_Wq, L0_attn_Wk, L0_attn_Wv, L0_attn_Wr,
           L0_attn_Wo, L0_ffn_w_gate, L0_ffn_w_up, L0_ffn_w_down):
    C = _get_built()
    x = np.ascontiguousarray(np.asarray(L0_attn_x, dtype=np.float32))
    shards = [x[2 * i:2 * i + 2] for i in range(8)]
    base = {
        "Wq": np.ascontiguousarray(np.asarray(L0_attn_Wq, np.float32)),
        "Wk": np.ascontiguousarray(np.asarray(L0_attn_Wk, np.float32)),
        "Wv": np.ascontiguousarray(np.asarray(L0_attn_Wv, np.float32)),
        "Wr": np.ascontiguousarray(np.asarray(L0_attn_Wr, np.float32)),
        "Wo": np.ascontiguousarray(np.asarray(L0_attn_Wo, np.float32)),
        "w_gate": np.ascontiguousarray(np.asarray(L0_ffn_w_gate, np.float32)),
        "w_up": np.ascontiguousarray(np.asarray(L0_ffn_w_up, np.float32)),
        "w_down": np.ascontiguousarray(np.asarray(L0_ffn_w_down, np.float32)),
    }
    in_maps = [dict(base, x=s_) for s_ in shards]
    res = run_bass_kernel_spmd(C.nc, in_maps, core_ids=list(range(8)))
    out = np.concatenate([np.asarray(r["out"], np.float32)
                          for r in res.results], axis=0)
    return out


# version 04
# speedup vs baseline: 1.1412x; 1.0504x over previous
import numpy as np
import ml_dtypes
from contextlib import ExitStack

import concourse.bass as bass
import concourse.mybir as mybir
from concourse.bass_utils import run_bass_kernel_spmd
from concourse.alu_op_type import AluOpType as Op

F32 = mybir.dt.float32
BF16 = mybir.dt.bfloat16
I32 = mybir.dt.int32
AF = mybir.ActivationFunctionType
AX = mybir.AxisListType.X

B, S, E, H, D, KN, INNER = 2, 2048, 1024, 16, 64, 4, 4096
NT = B * S
EPS_LN = 1e-5
SQRT_EPS_NW = 1e-4
LOG_RB = float(np.log(10000.0))
TWO_PI = float(2 * np.pi)
INV_2PI = float(1.0 / TWO_PI)
QOFF = 140.0

_DETECT_RACES = False
_STAGE = 99
_TAP_REQ = []

CONSTS = [0.0, 1.0, 0.5, 0.25, -0.5, 2.0, -1.0, float(np.pi / 2),
          127.0, 1e-5, float(1.0 / E), float(EPS_LN), float(np.pi)]
CIDX = {float(v): i for i, v in enumerate(CONSTS)}


class _Ctx:
    pass


def _mk_ctx():
    C = _Ctx()
    C.nc = bass.Bass("TRN2", target_bir_lowering=False,
                     detect_race_conditions=_DETECT_RACES)
    C.st = ExitStack()
    C.blocks = []
    C.taps = []
    nc, st = C.nc, C.st

    C.sb = lambda n, sh, dt: st.enter_context(nc.sbuf_tensor(n, sh, dt))
    C.ps = lambda n, sh: st.enter_context(nc.psum_tensor(n, sh, F32))
    C.dram = lambda n, sh, dt, k: nc.dram_tensor(n, sh, dt, kind=k).ap()

    C.sems, C.semv = {}, {}

    def mksem(n):
        C.sems[n] = st.enter_context(nc.semaphore(n))
        C.semv[n] = 0
    C.mksem = mksem
    for s_ in ["ld", "ld2", "st", "pe", "ve", "sc", "gp", "sy",
               "pe2", "ve2", "sc2", "gp2", "ve3", "sy2", "lds", "lda"]:
        mksem(s_)

    def inc(inst, name, n=1):
        inst.then_inc(C.sems[name], n)
        C.semv[name] += n
        return inst
    C.inc = inc
    C.dinc = lambda inst, name="ld": inc(inst, name, 16)

    def wait(eng, name, lag=0):
        v = C.semv[name] - lag
        if v > 0:
            eng.wait_ge(C.sems[name], v)
    C.wait = wait

    def block(**eng_fns):
        C.blocks.append(eng_fns)
    C.block = block
    return C


def build(tap_req=()):
    C = _mk_ctx()
    C.tap_req = list(tap_req)
    nc = C.nc
    dram, sb = C.dram, C.sb

    C.x_h = dram("x", [B, S, E], F32, "ExternalInput")
    C.wq_h = dram("Wq", [E, E], F32, "ExternalInput")
    C.wk_h = dram("Wk", [E, E], F32, "ExternalInput")
    C.wv_h = dram("Wv", [E, E], F32, "ExternalInput")
    C.wr_h = dram("Wr", [E, H * KN], F32, "ExternalInput")
    C.wo_h = dram("Wo", [E, E], F32, "ExternalInput")
    C.wg_h = dram("w_gate", [INNER, E], F32, "ExternalInput")
    C.wu_h = dram("w_up", [INNER, E], F32, "ExternalInput")
    C.wd_h = dram("w_down", [E, INNER], F32, "ExternalInput")
    C.out_h = dram("out", [B, S, E], F32, "ExternalOutput")
    C.wgT_h = dram("wgT", [32, E, 128], BF16, "Internal")
    C.wuT_h = dram("wuT", [32, E, 128], BF16, "Internal")
    C.wdT_h = dram("wdT", [8, INNER, 128], BF16, "Internal")
    C.pf_h = dram("pf_sc", [B, 64, S], F32, "Internal")
    C.u_h = dram("u_sc", [B, 64, S], BF16, "Internal")
    C.v_h = dram("v_sc", [B, 64, S], BF16, "Internal")
    C.vtm_h = dram("vtm_sc", [B, 16, 128, E], BF16, "Internal")
    C.attno_h = dram("attno_sc", [8, 128, NT], BF16, "Internal")
    C.xln_h = dram("xln_sc", [8, 128, NT], BF16, "Internal")
    C.xq_h = dram("xq_sc", [8, 128, NT], BF16, "Internal")

    C.consts = sb("consts", [128, len(CONSTS)], F32)
    C.disc = sb("disc", [128, 12], F32)
    C.wcol = sb("wcol", [128, 12], F32)
    C.ones_bf = sb("ones_bf", [128, 16], BF16)
    C.idn_bf = sb("idn_bf", [128, 128], BF16)
    C.idn_f = sb("idn_f", [128, 128], F32)
    C.seldup = sb("seldup", [128, 6 * 64], BF16)
    C.masks = sb("masks", [128, 4 * 512], BF16)
    C.rsel = [sb(f"rsel{i}", [8, 128], F32) for i in range(4)]
    C.onerow_f = sb("onerow_f", [1, 128], F32)
    C.wqs = sb("wqs", [128, 8], F32)
    C.rowsel2 = sb("rowsel2", [128, 2], BF16)
    C.rsg = sb("rsg", [128, 3 * 32], F32)
    C.scr = sb("scr", [128, 2048], F32)
    C.scr2 = sb("scr2", [128, 2048], F32)
    C.scri = sb("scri", [128, 2048], I32)
    C.scrb = sb("scrb", [128, 2048], BF16)
    C.selABf = sb("selABf", [2, 256], F32)
    C._xfm_cm = nc.sbuf_tensor("xfm", [128, 8 * NT], BF16)
    C.xfm = C._xfm_cm.__enter__()
    C.pb = [C.ps(f"pb{i}", [128, 512]) for i in range(8)]

    for v in CONSTS:
        nc.const_aps.register(F32, float(v), C.consts[:, CIDX[float(v)]:CIDX[float(v)] + 1])
    C.ccol = lambda v: C.consts[:, CIDX[float(v)]:CIDX[float(v)] + 1]

    def tap(name, src_fn, shape, dt=F32):
        if name not in C.tap_req:
            return
        t = dram("tap_" + name, shape, dt, "ExternalOutput")

        def _sy(sy, t=t, src_fn=src_fn):
            C.dinc(sy.dma_start(t, src_fn()), "st")
            C.wait(sy, "st")
        C.block(sync=_sy)
    C.tap = tap

    _p0(C)
    if _STAGE >= 1:
        _wprep(C)
    if _STAGE >= 2:
        _attention(C)
    if _STAGE >= 4:
        _wo_ln1_quant(C)
    if _STAGE >= 5:
        _ffn(C)

    def _fin(sy):
        C.wait(sy, "st")
    C.block(sync=_fin)

    for eng_fns in C.blocks:
        with nc.Block() as blk:
            for ename, fn in eng_fns.items():
                getattr(blk, ename)(fn)
    return C


def _p0(C):
    nc, block, inc, dinc, wait, ccol = C.nc, C.block, C.inc, C.dinc, C.wait, C.ccol
    C._p0scope = ExitStack()
    xtm = C._p0scope.enter_context(nc.sbuf_tensor("xtm", [128, 32 * E], BF16))
    pb = C.pb

    def g0(g):
        for t in range(32):
            b_, s_ = divmod(t, 16)
            dinc(g.dma_start(xtm[:, t * E:(t + 1) * E],
                             C.x_h[b_, s_ * 128:(s_ + 1) * 128, :]))
        for i, v in enumerate(CONSTS):
            g.memset(C.consts[:, i:i + 1], float(v))
        g.memset(C.ones_bf[:, :], 1.0)
        g.memset(C.onerow_f[:, :], 1.0)
        g.memset(C.disc[:, :], 0.0)
        g.memset(C.wcol[:, :], 0.0)
        g.memset(C.wqs[:, :], 0.0)
        g.memset(C.rsg[:, :], 0.0)
        g.iota(C.scr[:, 0:128], [[1, 128]], channel_multiplier=0,
               allow_small_or_imprecise_dtypes=True)
        g.iota(C.scr[:, 128:129], [[0, 1]], channel_multiplier=1,
               allow_small_or_imprecise_dtypes=True)
        for m in range(4):
            g.iota(C.scr2[:, 0:512], [[1, 512]], channel_multiplier=-1,
                   base=-128 * m, allow_small_or_imprecise_dtypes=True)
            g.drain()
            g.tensor_scalar(C.scr2[:, 512:1024], C.scr2[:, 0:512], -0.5,
                            None, Op.is_gt)
            g.drain()
            g.tensor_copy(C.masks[:, m * 512:(m + 1) * 512], C.scr2[:, 512:1024])
        for ib in range(4):
            g.memset(C.rsel[ib][:, :], 0.0)
        wait(g, "ld")
    block(gpsimd=g0)

    def v0(v):
        v.tensor_scalar(C.scr2[0:8, 0:128], C.scr[0:8, 0:128], 64.0, None,
                        Op.is_lt)
        v.tensor_scalar(C.scr2[0:8, 128:256], C.scr[0:8, 0:128], 63.5, None,
                        Op.is_gt)
        v.drain()
        for ib in range(4):
            v.tensor_scalar(C.scr2[0:8, 256:257], C.scr[0:8, 128:129],
                            float(2 * ib), None, Op.is_equal)
            v.tensor_scalar(C.scr2[0:8, 257:258], C.scr[0:8, 128:129],
                            float(2 * ib + 1), None, Op.is_equal)
            v.drain()
            v.tensor_scalar(C.scr2[0:8, 258:386], C.scr2[0:8, 0:128],
                            C.scr2[0:8, 256:257], None, Op.mult)
            v.tensor_scalar(C.scr2[0:8, 386:514], C.scr2[0:8, 128:256],
                            C.scr2[0:8, 257:258], None, Op.mult)
            v.drain()
            v.tensor_tensor(C.rsel[ib][:, :], C.scr2[0:8, 258:386],
                            C.scr2[0:8, 386:514], Op.add)
            v.drain()
        v.tensor_scalar(C.disc[:, 1:2], ccol(0.25), 0.5, None, Op.add)
        v.drain()
        v.tensor_copy(C.scri[:, 0:1], C.disc[:, 1:2])
        v.drain()
        v.tensor_copy(C.disc[:, 0:1], C.scri[:, 0:1])
        v.drain()
        v.tensor_scalar(C.disc[:, 1:2], C.disc[:, 0:1], -1.0, 1.0, Op.mult, Op.add)
        v.drain()
        v.tensor_scalar(C.disc[:, 2:3], C.disc[:, 1:2], QOFF + 0.5, None, Op.mult)
        v.tensor_scalar(C.disc[:, 3:4], C.disc[:, 0:1], -0.5, None, Op.mult)
        v.tensor_scalar(C.disc[:, 4:5], C.disc[:, 1:2], QOFF, None, Op.mult)
        v.tensor_scalar(C.disc[:, 5:6], C.disc[:, 1:2], 128.5, None, Op.mult)
        v.tensor_scalar(C.disc[:, 6:7], C.disc[:, 1:2], 128.0, 1.0, Op.mult, Op.add)
        v.tensor_scalar(C.disc[:, 7:8], C.disc[:, 1:2], 128.0, -1.0, Op.mult, Op.add)
        v.tensor_scalar(C.disc[:, 8:9], C.disc[:, 1:2], 128.0, None, Op.mult)
        v.tensor_scalar(C.disc[:, 9:10], C.disc[:, 0:1], float(np.pi),
                        float(-np.pi), Op.mult, Op.add)
        v.tensor_scalar(C.idn_f[:, :], C.scr[:, 0:128], C.scr[:, 128:129],
                        None, Op.is_equal)
        v.tensor_scalar(C.idn_bf[:, :], C.scr[:, 0:128], C.scr[:, 128:129],
                        None, Op.is_equal)
        v.tensor_scalar(C.rowsel2[:, 0:1], C.scr[:, 128:129], 64.0, None, Op.is_lt)
        v.tensor_scalar(C.rowsel2[:, 1:2], C.scr[:, 128:129], 63.5, None, Op.is_gt)
        v.tensor_copy(C.scri[:, 1:2], C.scr[:, 128:129])
        v.drain()
        v.tensor_scalar(C.scri[:, 2:3], C.scri[:, 1:2], 31, None, Op.bitwise_and)
        v.drain()
        v.tensor_copy(C.wcol[:, 0:1], C.scri[:, 2:3])
    block(vector=v0)

    def s0(s):
        s.activation(C.wcol[:, 1:2], C.wcol[:, 0:1], AF.Exp, bias=ccol(0.0),
                     scale=-LOG_RB / 32.0)
    block(scalar=s0)

    def v0b(v):
        v.tensor_scalar(C.wcol[:, 2:3], C.wcol[:, 1:2], INV_2PI, None, Op.mult)
        v.tensor_scalar(C.wcol[:, 3:4], C.disc[:, 3:4], 1.0, 0.5, Op.mult, Op.add)
        v.tensor_scalar(C.wcol[:, 4:5], C.disc[:, 3:4], 1.0, 0.75, Op.mult, Op.add)
    block(vector=v0b)

    def s0b(s):
        s.activation(C.wcol[:, 5:6], C.wcol[:, 1:2], AF.Sin, bias=ccol(0.0),
                     scale=1.0)
        s.activation(C.wcol[:, 6:7], C.wcol[:, 1:2], AF.Sin,
                     bias=ccol(np.pi / 2), scale=1.0)
    block(scalar=s0b)

    def v0c(v):
        sc = C.scr
        v.tensor_copy(C.scri[:, 0:64], sc[:, 0:64])
        v.drain()
        v.tensor_scalar(C.scri[:, 64:128], C.scri[:, 0:64], 31, None, Op.bitwise_and)
        v.drain()
        v.tensor_copy(sc[:, 256:320], C.scri[:, 64:128])
        v.tensor_scalar(sc[:, 576:640], sc[:, 0:64], 31.5, None, Op.is_gt)
        v.drain()
        v.tensor_scalar(sc[:, 320:384], sc[:, 256:320], C.wcol[:, 0:1],
                        None, Op.is_equal)
        v.tensor_scalar(sc[:, 704:768], sc[:, 576:640], 2.0, -1.0, Op.mult, Op.add)
        v.drain()
        v.tensor_copy(C.seldup[:, 0:64], sc[:, 320:384])
        v.tensor_scalar(sc[:, 384:448], sc[:, 320:384], C.wcol[:, 6:7], None, Op.mult)
        v.tensor_scalar(sc[:, 448:512], sc[:, 320:384], C.wcol[:, 5:6], None, Op.mult)
        v.tensor_mul(sc[:, 640:704], sc[:, 320:384], sc[:, 704:768])
        v.drain()
        v.tensor_copy(C.seldup[:, 64:128], sc[:, 384:448])
        v.tensor_scalar(sc[:, 512:576], sc[:, 448:512], -1.0, None, Op.mult)
        v.tensor_copy(C.seldup[:, 192:256], sc[:, 640:704])
        v.tensor_mul(sc[:, 768:832], sc[:, 384:448], sc[:, 704:768])
        v.tensor_mul(sc[:, 832:896], sc[:, 448:512], sc[:, 704:768])
        v.drain()
        v.tensor_copy(C.seldup[:, 128:192], sc[:, 512:576])
        v.tensor_copy(C.seldup[:, 256:320], sc[:, 768:832])
        v.tensor_copy(C.seldup[:, 320:384], sc[:, 832:896])
    block(vector=v0c)

    def pe0(t):
        base_ve = C.semv["ve"]
        for n in range(64):
            tt, half = divmod(n, 2)
            bank = pb[6] if n % 2 == 0 else pb[7]
            if n >= 2:
                t.wait_ge(C.sems["ve"], base_ve + n - 1)
            for c in range(4):
                cc = half * 4 + c
                t.transpose(bank.bitcast(BF16)[:, c * 128:(c + 1) * 128],
                            xtm[:, tt * E + cc * 128:tt * E + (cc + 1) * 128],
                            C.idn_bf[:, :])
            t.drain().then_inc(C.sems["pe"], 1)
        C.semv["pe"] += 64

    def ve0(v):
        base_pe = C.semv["pe"] - 64
        for n in range(64):
            tt, half = divmod(n, 2)
            b_, s_ = divmod(tt, 16)
            tok0 = b_ * S + s_ * 128
            bank = pb[6] if n % 2 == 0 else pb[7]
            v.wait_ge(C.sems["pe"], base_pe + n + 1)
            for c in range(4):
                cc = half * 4 + c
                v.tensor_copy(C.xfm[:, cc * NT + tok0:cc * NT + tok0 + 128],
                              bank.bitcast(BF16)[:, c * 128:(c + 1) * 128])
            v.drain().then_inc(C.sems["ve"], 1)
        C.semv["ve"] += 64
    block(tensor=pe0, vector=ve0)

    C.tap("disc", lambda: C.disc[:, :], [128, 12])
    C.tap("wcol", lambda: C.wcol[:, :], [128, 12])
    C.tap("xfm", lambda: C.xfm[:, :].bitcast(F32), [128, 4 * NT])
    C.tap("masks", lambda: C.masks[:, :].bitcast(F32), [128, 1024])
    C.tap("seldup", lambda: C.seldup[:, :].bitcast(F32), [128, 192])


def _wprep(C):
    nc, block, inc, dinc, wait, ccol = C.nc, C.block, C.inc, C.dinc, C.wait, C.ccol
    pb = C.pb
    C._p0scope.close()
    with ExitStack() as scope:
        e = scope.enter_context
        wfl = e(nc.sbuf_tensor("wfl", [128, 4096], F32))
        wfl2 = e(nc.sbuf_tensor("wfl2", [128, 4096], F32))
        wiq = e(nc.sbuf_tensor("wiq", [128, 4096], I32))
        wzb = e(nc.sbuf_tensor("wzb", [128, 4096], F32))
        wqb = e(nc.sbuf_tensor("wqb", [128, 4096], BF16))
        wtb = e(nc.sbuf_tensor("wtb", [128, 2 * 1024], BF16))

        WCFG = [(C.wg_h, INNER, E, C.wgT_h, 0), (C.wu_h, INNER, E, C.wuT_h, 1),
                (C.wd_h, E, INNER, C.wdT_h, 2)]

        chl = []
        for wh, rows, cols, _, wi in WCFG:
            for ch in range(rows // 128):
                chl.append((wh, cols, wi, ch))
        NCH1 = len(chl)

        def w1g(g):
            C._w1 = {k_: C.semv[k_] for k_ in ("ve", "lds", "lda", "gp")}
            g.memset(C.scr2[:, 0:5], 0.0)
            g.drain().then_inc(C.sems["gp"], 1)
            C.semv["gp"] += 1

        def _w1_load(q, par):
            b1 = C._w1
            sem = "lds" if par == 0 else "lda"
            for n, (wh, cols, wi, ch) in enumerate(chl):
                if n % 2 != par:
                    continue
                if n >= 2:
                    q.wait_ge(C.sems["ve"], b1["ve"] + n - 1)
                if n >= 2:
                    q.wait_ge(C.sems[sem], b1[sem] + 16 * (n // 2))
                inc(q.dma_start((wfl if par == 0 else wfl2)[:, 0:cols],
                                wh[ch * 128:(ch + 1) * 128, :]), sem, 16)

        def w1ls(sy):
            _w1_load(sy, 0)

        def w1la(s):
            _w1_load(s, 1)

        def w1v(v):
            b1 = C._w1
            for n, (wh, cols, wi, ch) in enumerate(chl):
                buf = wfl if n % 2 == 0 else wfl2
                sem = "lds" if n % 2 == 0 else "lda"
                v.wait_ge(C.sems["gp"], b1["gp"] + 1)
                v.wait_ge(C.sems[sem], b1[sem] + 16 * (n // 2 + 1))
                v.tensor_reduce(C.scr2[:, 4:5], buf[:, 0:cols], AX, Op.add,
                                apply_absolute_value=True)
                v.drain()
                v.tensor_add(C.scr2[:, wi:wi + 1], C.scr2[:, wi:wi + 1],
                             C.scr2[:, 4:5])
                v.drain().then_inc(C.sems["ve"], 1)
            C.semv["ve"] += NCH1
        block(gpsimd=w1g, sync=w1ls, scalar=w1la, vector=w1v)

        def w1pe(t):
            t.matmul(pb[0][0:3, 0:1], C.scr2[:, 0:3], ccol(1.0), start=True,
                     stop=True)
        block(tensor=w1pe)

        def w1v2(v):
            v.tensor_copy(C.scr2[0:3, 8:9], pb[0][0:3, 0:1])
            v.drain()
            v.tensor_scalar(C.scr2[0:3, 9:10], C.scr2[0:3, 8:9],
                            1.0 / (INNER * E), None, Op.mult)
            v.drain()
            v.tensor_scalar(C.scr2[0:3, 10:11], C.scr2[0:3, 9:10], 1e-5,
                            None, Op.max)
            v.drain()
            v.reciprocal(C.scr2[0:3, 11:12], C.scr2[0:3, 10:11])
        block(vector=w1v2)

        def w1sy(sy):
            dinc(sy.dma_start(C.scr2[0:1, 12:13], C.scr2[1:2, 11:12]), "st")
            dinc(sy.dma_start(C.scr2[0:1, 13:14], C.scr2[2:3, 11:12]), "st")
            wait(sy, "st")
        block(sync=w1sy)

        def w1pe2(t):
            for wi in range(3):
                t.matmul(pb[1][:, wi:wi + 1], C.onerow_f[0:1, :],
                         C.scr2[0:1, 11 + wi:12 + wi], start=True, stop=True,
                         skip_group_check=True)
        block(tensor=w1pe2)

        def w1v3(v):
            v.tensor_copy(C.wqs[:, 0:3], pb[1][:, 0:3])
        block(vector=w1v3)
        C.tap("wqs", lambda: C.wqs[:, :], [128, 8])

        chunks = []
        for wh, rows, cols, whT, wi in WCFG:
            for ch in range(rows // 128):
                chunks.append((wh, rows, cols, whT, wi, ch))
        NCH = len(chunks)
        nbat = [cols // 1024 for _, _, cols, _, _, _ in chunks]
        batpre = [0]
        for nb in nbat:
            batpre.append(batpre[-1] + nb)
        NBAT = batpre[-1]

        b0 = {}

        def w2g(g):
            for k_ in ("ld2", "lda", "gp", "ve", "pe", "ve3", "st"):
                b0[k_] = C.semv[k_]
            for n, (wh, rows, cols, whT, wi, ch) in enumerate(chunks):
                buf = wfl if n % 2 == 0 else wfl2
                if n % 2 == 1:
                    if n >= 2:
                        g.wait_ge(C.sems["ve"], b0["ve"] + 2 * (n - 1))
                    g.wait_ge(C.sems["ld2"], b0["ld2"] + 16 * (n // 2))
                    inc(g.dma_start(buf[:, 0:cols],
                                    wh[ch * 128:(ch + 1) * 128, :]), "ld2", 16)
                g.wait_ge(C.sems["ve"], b0["ve"] + 2 * n + 1)
                g.tensor_copy(wiq[:, 0:cols], wzb[:, 0:cols])
                g.drain().then_inc(C.sems["gp"], 1)
            C.semv["gp"] += NCH

        def w2la(s):
            for n, (wh, rows, cols, whT, wi, ch) in enumerate(chunks):
                if n % 2 != 0:
                    continue
                if n >= 2:
                    s.wait_ge(C.sems["ve"], b0["ve"] + 2 * (n - 1))
                    s.wait_ge(C.sems["lda"], b0["lda"] + 16 * (n // 2))
                inc(s.dma_start(wfl[:, 0:cols],
                                wh[ch * 128:(ch + 1) * 128, :]), "lda", 16)

        def w2v(v):
            for n, (wh, rows, cols, whT, wi, ch) in enumerate(chunks):
                buf = wfl if n % 2 == 0 else wfl2
                if n % 2 == 0:
                    v.wait_ge(C.sems["lda"], b0["lda"] + 16 * (n // 2 + 1))
                else:
                    v.wait_ge(C.sems["ld2"], b0["ld2"] + 16 * (n // 2 + 1))
                if n >= 1:
                    v.wait_ge(C.sems["gp"], b0["gp"] + n)
                v.tensor_scalar(wzb[:, 0:cols], buf[:, 0:cols],
                                C.wqs[:, wi:wi + 1], C.disc[:, 5:6],
                                Op.mult, Op.add)
                v.drain().then_inc(C.sems["ve"], 1)
                v.wait_ge(C.sems["gp"], b0["gp"] + n + 1)
                v.tensor_copy(wzb[:, 0:cols], wiq[:, 0:cols])
                v.drain()
                v.tensor_scalar(wzb[:, 0:cols], wzb[:, 0:cols], C.disc[:, 6:7],
                                C.disc[:, 7:8], Op.min, Op.max)
                v.drain()
                v.tensor_scalar(wqb[:, 0:cols], wzb[:, 0:cols], C.disc[:, 8:9],
                                None, Op.subtract)
                v.drain()
                v.tensor_reduce(C.rsg[:, wi * 32 + ch:wi * 32 + ch + 1],
                                wqb[:, 0:cols], AX, Op.add)
                v.drain().then_inc(C.sems["ve"], 1)
                for bb in range(nbat[n]):
                    gb = batpre[n] + bb
                    bank = pb[6] if gb % 2 == 0 else pb[7]
                    v.wait_ge(C.sems["pe"], b0["pe"] + gb + 1)
                    if gb >= 2:
                        v.wait_ge(C.sems["st"], b0["st"] + 16 * (gb - 1))
                    v.tensor_copy(wtb[:, (gb % 2) * 1024:(gb % 2) * 1024 + 1024],
                                  bank.bitcast(BF16)[:, 0:1024])
                    v.drain().then_inc(C.sems["ve3"], 1)
            C.semv["ve"] += 2 * NCH
            C.semv["ve3"] += NBAT

        def w2pe(t):
            for n, (wh, rows, cols, whT, wi, ch) in enumerate(chunks):
                t.wait_ge(C.sems["ve"], b0["ve"] + 2 * n + 2)
                for bb in range(nbat[n]):
                    gb = batpre[n] + bb
                    bank = pb[6] if gb % 2 == 0 else pb[7]
                    if gb >= 2:
                        t.wait_ge(C.sems["ve3"], b0["ve3"] + gb - 1)
                    for c in range(8):
                        col0 = bb * 1024 + c * 128
                        t.transpose(bank.bitcast(BF16)[:, c * 128:(c + 1) * 128],
                                    wqb[:, col0:col0 + 128], C.idn_bf[:, :])
                    t.drain().then_inc(C.sems["pe"], 1)
            C.semv["pe"] += NBAT

        def w2sy(sy):
            for n, (wh, rows, cols, whT, wi, ch) in enumerate(chunks):
                for bb in range(nbat[n]):
                    gb = batpre[n] + bb
                    sy.wait_ge(C.sems["ve3"], b0["ve3"] + gb + 1)
                    if gb >= 1:
                        sy.wait_ge(C.sems["st"], b0["st"] + 16 * gb)
                    src = wtb[:, (gb % 2) * 1024:(gb % 2) * 1024 + 1024]
                    if wi < 2:
                        dst = whT[ch, :, :].rearrange("(ic r) c -> r ic c", r=128)
                    else:
                        icn0 = bb * 8
                        dst = whT[ch, icn0 * 128:(icn0 + 8) * 128, :].rearrange(
                            "(ic r) c -> r ic c", r=128)
                    dinc(sy.dma_start(dst, src.rearrange("r (ic c) -> r ic c",
                                                         c=128)), "st")
        block(gpsimd=w2g, scalar=w2la, vector=w2v, tensor=w2pe, sync=w2sy)
    C.tap("rsg", lambda: C.rsg[:, :], [128, 96])
    C.tap("wgT0", lambda: C.wgT_h[0, :, :], [E, 128], BF16)
    C.tap("wdT0", lambda: C.wdT_h[0, :, :], [INNER, 128], BF16)


def _attention(C):
    nc, block, inc, dinc, wait, ccol = C.nc, C.block, C.inc, C.dinc, C.wait, C.ccol
    pb = C.pb
    selABf = C.selABf

    def selABf_init(v):
        pc = C.scr[0:2, 128:129].broadcast_to((2, 128))
        v.tensor_scalar(selABf[0:2, 0:128], pc, 0.5, None, Op.is_lt)
        v.tensor_scalar(selABf[0:2, 128:256], pc, 0.5, None, Op.is_gt)
    block(vector=selABf_init)

    ps_ = ExitStack()
    e = ps_.enter_context
    A = _Ctx()
    A.selABf = selABf
    A.pbf_all = e(nc.sbuf_tensor("pbf_all", [128, S], BF16))
    A.strip_all = e(nc.sbuf_tensor("strip_all", [64, 2 * S], BF16))

    def strips_init(g_):
        g_.memset(A.strip_all[:, :], SQRT_EPS_NW)
    block(gpsimd=strips_init)

    for b in range(B):
        _p1_seq(C, b, pbf=A.pbf_all, prow=b * 64)

    A.wqp = e(nc.sbuf_tensor("wqp", [128, 8 * 128], BF16))
    A.wkp = e(nc.sbuf_tensor("wkp", [128, 8 * 128], BF16))
    A.vp2 = [e(nc.sbuf_tensor(f"vp{i}", [128, 16 * 130], BF16))
             for i in range(2)]
    A.qbf = e(nc.sbuf_tensor("qbf", [128, S], BF16))
    A.kbf = e(nc.sbuf_tensor("kbf", [128, S], BF16))
    A.qrot = e(nc.sbuf_tensor("qrot", [128, S], BF16))
    A.krot = e(nc.sbuf_tensor("krot", [128, S], BF16))
    A.qr2 = [e(nc.sbuf_tensor(f"qr{i}", [128, S], BF16)) for i in range(2)]
    A.kr2 = [e(nc.sbuf_tensor(f"kr{i}", [128, S], BF16)) for i in range(2)]
    A.pfrep = e(nc.sbuf_tensor("pfrep", [128, S], F32))
    A.urep = e(nc.sbuf_tensor("urep", [128, S], BF16))
    A.vrep = e(nc.sbuf_tensor("vrep", [128, S], BF16))
    A.trigb = e(nc.sbuf_tensor("trigb", [128, 2 * 512], BF16))
    A.prods = e(nc.sbuf_tensor("prods", [128, 4 * 512], BF16))
    A.tmpq = e(nc.sbuf_tensor("tmpq", [128, 512], BF16))
    A.tmpq2 = e(nc.sbuf_tensor("tmpq2", [128, 512], BF16))
    A.negC2 = [e(nc.sbuf_tensor(f"negC{i}", [128, 2], F32)) for i in range(2)]
    A.qn = e(nc.sbuf_tensor("qn", [2, 16], F32))
    A.eA = [e(nc.sbuf_tensor(f"eA{i}", [128, 512], BF16)) for i in range(2)]
    A.eB = [e(nc.sbuf_tensor(f"eB{i}", [128, 512], BF16)) for i in range(2)]
    A.ptA = [e(nc.sbuf_tensor(f"ptA{i}", [128, 512], BF16)) for i in range(2)]
    A.ptB = [e(nc.sbuf_tensor(f"ptB{i}", [128, 512], BF16)) for i in range(2)]
    A.outev = e(nc.sbuf_tensor("outev", [128, S], F32))
    A.attp = e(nc.sbuf_tensor("attp", [128, S], BF16))
    sstgt = e(nc.sbuf_tensor("sstgt", [8, 1024], F32))
    A.sumstg_a = C.scr[0:1, 1536:2048]
    A.sumstg_b = C.scr[64:65, 1536:2048]
    A.sstg = sstgt[0:8, 0:512]
    A.sstg2 = sstgt[0:8, 512:1024]

    def compose(fns):
        def run(e_):
            for f in fns:
                f(e_)
        return run

    NPAIR = 2 * 8
    pr = _pair_prep(C, A, 0, merged=False)
    block(tensor=compose([pr["lead"]] + pr["tensor"]),
          scalar=compose(pr["scalar"]), vector=compose(pr["vector"]),
          gpsimd=compose(pr["gp_early"] + pr["gp_late"]),
          sync=compose(pr["sy_early"] + pr["sy_late"]))
    for idx in range(NPAIR):
        b, g = divmod(idx, 8)
        par = idx % 2
        sc = _pair_scores(C, A, b, g, A.qr2[par], A.kr2[par], A.vp2[par],
                          par, A.negC2[par])
        if idx + 1 < NPAIR:
            pr = _pair_prep(C, A, idx + 1, merged=True)
            block(tensor=compose([pr["lead"], sc["tensor"], pr["guard"]]
                                 + pr["tensor"]),
                  scalar=compose([sc["scalar"]] + pr["scalar"]),
                  vector=compose([sc["vector"]] + pr["vector"]),
                  gpsimd=compose(pr["gp_early"] + [sc["gpsimd"]]
                                 + pr["gp_late"]),
                  sync=compose(pr["sy_early"] + [sc["sync"]]
                               + pr["sy_late"]))
        else:
            block(tensor=sc["tensor"], scalar=sc["scalar"],
                  vector=sc["vector"], gpsimd=sc["gpsimd"], sync=sc["sync"])
        C.tap(f"attp{b}_{g}", lambda: A.attp[:, :].bitcast(F32), [128, S // 2])
    ps_.close()


def _p1_seq(C, b, pbf, prow):
    nc, block, inc, dinc, wait, ccol = C.nc, C.block, C.inc, C.dinc, C.wait, C.ccol
    pb = C.pb
    scope = ExitStack()
    e = scope.enter_context
    wvs = e(nc.sbuf_tensor(f"wvs{b}", [128, 8 * E], BF16))
    wrs = e(nc.sbuf_tensor(f"wrs{b}", [128, 8 * 64], BF16))
    vtm = e(nc.sbuf_tensor(f"vtm{b}", [128, 2 * E], BF16))
    ltm = e(nc.sbuf_tensor(f"ltm{b}", [128, 16 * 64], F32))
    ptm = e(nc.sbuf_tensor(f"ptm{b}", [128, 16 * 64], F32))
    onesf = e(nc.sbuf_tensor(f"onesf{b}", [64, S], F32))
    pcm = e(nc.sbuf_tensor(f"pcm{b}", [64, S], F32))
    ubf = e(nc.sbuf_tensor(f"ubf{b}", [64, S], BF16))

    def l0(g):
        for c in range(8):
            dinc(g.dma_start(wvs[:, c * E:(c + 1) * E],
                             C.wv_h[c * 128:(c + 1) * 128, :]))
            dinc(g.dma_start(wrs[:, c * 64:(c + 1) * 64],
                             C.wr_h[c * 128:(c + 1) * 128, :]))
        g.memset(onesf[:, :], 1.0)
        wait(g, "ld")
    block(gpsimd=l0)

    def vp_pe(t):
        C._p1st = C.semv["st"]
        base_ve = C.semv["ve"]
        C._p1ve = base_ve
        for tt in range(16):
            bs = [pb[0], pb[1], pb[2]] if tt % 2 == 0 else [pb[3], pb[4], pb[5]]
            if tt >= 2:
                t.wait_ge(C.sems["ve"], base_ve + tt - 1)
            for ic in range(8):
                lhs = C.xfm[:, ic * NT + b * S + tt * 128:
                            ic * NT + b * S + tt * 128 + 128]
                m0 = t.matmul(bs[0][:, :], lhs, wvs[:, ic * E:ic * E + 512],
                              start=(ic == 0), stop=(ic == 7))
                m1 = t.matmul(bs[1][:, :], lhs,
                              wvs[:, ic * E + 512:ic * E + 1024],
                              start=(ic == 0), stop=(ic == 7))
                m2 = t.matmul(bs[2][:, 0:64], lhs, wrs[:, ic * 64:(ic + 1) * 64],
                              start=(ic == 0), stop=(ic == 7))
            for m_ in (m0, m1, m2):
                m_.then_inc(C.sems["pe"], 1)
        C.semv["pe"] += 48

    def vp_ve(v):
        base_pe = C.semv["pe"] - 48
        for tt in range(16):
            bs = [pb[0], pb[1], pb[2]] if tt % 2 == 0 else [pb[3], pb[4], pb[5]]
            v.wait_ge(C.sems["pe"], base_pe + 3 * tt + 3)
            if tt >= 2:
                v.wait_ge(C.sems["st"], C._p1st + 16 * (tt - 1))
            v.tensor_copy(vtm[:, (tt % 2) * E:(tt % 2) * E + 512], bs[0][:, :])
            v.tensor_copy(vtm[:, (tt % 2) * E + 512:(tt % 2) * E + 1024],
                          bs[1][:, :])
            v.tensor_copy(ltm[:, tt * 64:(tt + 1) * 64], bs[2][:, 0:64])
            v.drain().then_inc(C.sems["ve"], 1)
        C.semv["ve"] += 16

    def vp_sy(sy):
        base_ve = C.semv["ve"] - 16
        for tt in range(16):
            sy.wait_ge(C.sems["ve"], base_ve + tt + 1)
            if tt >= 1:
                sy.wait_ge(C.sems["st"], C._p1st + 16 * tt)
            dinc(sy.dma_start(C.vtm_h[b, tt, :, :],
                              vtm[:, (tt % 2) * E:(tt % 2 + 1) * E]), "st")
    block(tensor=vp_pe, vector=vp_ve, sync=vp_sy)
    C.tap(f"ltm{b}", lambda: ltm[:, :], [128, 1024])

    def _kv(t_, k):
        return t_[:, 0:1024].rearrange("p (n k) -> p n k", k=4)[:, :, k]

    def _bc4(row):
        return row.rearrange("p n -> p n ()").broadcast_to((128, 256, 4))

    def sm_v(v):
        v.tensor_tensor(C.scr[:, 0:256], _kv(ltm, 0), _kv(ltm, 1), Op.max)
        v.tensor_tensor(C.scr[:, 256:512], _kv(ltm, 2), _kv(ltm, 3), Op.max)
        v.drain()
        v.tensor_tensor(C.scr[:, 512:768], C.scr[:, 0:256], C.scr[:, 256:512], Op.max)
        v.drain()
        v.tensor_tensor(C.scr2[:, 0:1024].rearrange("p (n k) -> p n k", k=4),
                        ltm[:, 0:1024].rearrange("p (n k) -> p n k", k=4),
                        _bc4(C.scr[:, 512:768]), Op.subtract)
        v.drain().then_inc(C.sems["ve"], 1)
        C.semv["ve"] += 1

    def sm_s(s):
        wait(s, "ve")
        s.activation(ptm[:, 0:1024], C.scr2[:, 0:1024], AF.Exp, bias=ccol(0.0),
                     scale=1.0)
        s.drain().then_inc(C.sems["sc"], 1)
        C.semv["sc"] += 1

    def sm_v2(v):
        wait(v, "sc")
        v.tensor_tensor(C.scr[:, 0:256], _kv(ptm, 0), _kv(ptm, 1), Op.add)
        v.tensor_tensor(C.scr[:, 256:512], _kv(ptm, 2), _kv(ptm, 3), Op.add)
        v.drain()
        v.tensor_tensor(C.scr[:, 512:768], C.scr[:, 0:256], C.scr[:, 256:512], Op.add)
        v.drain()
        v.reciprocal(C.scr[:, 768:1024], C.scr[:, 512:768])
        v.drain()
        v.tensor_tensor(C.scr2[:, 0:1024].rearrange("p (n k) -> p n k", k=4),
                        ptm[:, 0:1024].rearrange("p (n k) -> p n k", k=4),
                        _bc4(C.scr[:, 768:1024]), Op.mult)
    block(vector=sm_v)
    block(scalar=sm_s)
    block(vector=sm_v2)
    C.tap(f"ptm{b}", lambda: C.scr2[:, 0:1024], [128, 1024])

    def pt_pe(t):
        base_ve = C.semv["ve"]
        C._ptve = base_ve
        for bt in range(4):
            bank = pb[6] if bt % 2 == 0 else pb[7]
            if bt >= 2:
                t.wait_ge(C.sems["ve"], base_ve + bt - 1)
            for c in range(4):
                tt = bt * 4 + c
                t.transpose(bank[0:64, c * 128:(c + 1) * 128],
                            C.scr2[:, tt * 64:(tt + 1) * 64],
                            C.idn_f[:, :]).then_inc(C.sems["pe"], 1)
        C.semv["pe"] += 16

    def pt_ve(v):
        base_pe = C.semv["pe"] - 16
        for bt in range(4):
            bank = pb[6] if bt % 2 == 0 else pb[7]
            v.wait_ge(C.sems["pe"], base_pe + 4 * bt + 4)
            v.tensor_copy(pcm[0:64, bt * 512:(bt + 1) * 512], bank[0:64, 0:512])
            v.drain().then_inc(C.sems["ve"], 1)
        C.semv["ve"] += 4
    block(tensor=pt_pe, vector=pt_ve)
    C.tap(f"pcm{b}", lambda: pcm[0:64, :], [64, S])

    def cs_v(v):
        v.tensor_copy(pbf[prow:prow + 64, :], pcm[0:64, :])
        v.tensor_tensor_scan(C.scr2[0:64, 0:S], onesf[0:64, :], pcm[0:64, :],
                             0.0, Op.mult, Op.add)
        v.drain()
        v.tensor_sub(C.scr2[0:64, 0:S], C.scr2[0:64, 0:S], pcm[0:64, :])
        v.drain()
        v.tensor_scalar(C.scr[0:64, 0:S], C.scr2[0:64, 0:S], C.disc[0:64, 3:4],
                        None, Op.add)
        v.drain()
        v.tensor_copy(C.scri[0:64, 0:S], C.scr[0:64, 0:S])
        v.drain()
        v.tensor_copy(C.scr[0:64, 0:S], C.scri[0:64, 0:S])
        v.drain()
        v.tensor_sub(C.scr2[0:64, 0:S], C.scr2[0:64, 0:S], C.scr[0:64, 0:S])
        v.drain()
        v.tensor_tensor(C.scrb[0:64, 0:S], pcm[0:64, :], C.scr2[0:64, 0:S],
                        Op.mult)
        v.drain()
        v.tensor_tensor(ubf[0:64, :], pcm[0:64, :], C.scrb[0:64, 0:S],
                        Op.subtract)
        v.drain().then_inc(C.sems["ve"], 1)
        C.semv["ve"] += 1

    def cs_sy(sy):
        wait(sy, "ve")
        dinc(sy.dma_start(C.pf_h[b, :, :], C.scr[0:64, 0:S]), "st")
        dinc(sy.dma_start(C.u_h[b, :, :], ubf[0:64, :]), "st")
        dinc(sy.dma_start(C.v_h[b, :, :], C.scrb[0:64, 0:S]), "st")
        wait(sy, "st")
    block(vector=cs_v)
    block(sync=cs_sy)
    C.tap(f"pf{b}", lambda: C.scr[0:64, 0:S], [64, S])
    C.tap(f"ubf{b}", lambda: ubf[0:64, :].bitcast(F32), [64, S // 2])
    scope.close()


PAIR_ITERS = [(ib, jc) for ib in range(4) for jc in range(4 * ib + 4)]
PAIR_DIAG = [jc >= 4 * ib for (ib, jc) in PAIR_ITERS]
PAIR_DC = []
_c = 0
for _d in PAIR_DIAG:
    _c += 1 if _d else 0
    PAIR_DC.append(_c)
NIT = len(PAIR_ITERS)


def _pair_prep(C, A, idx, merged):
    nc, block, inc, dinc, wait, ccol = C.nc, C.block, C.inc, C.dinc, C.wait, C.ccol
    pb = C.pb
    b, g = divmod(idx, 8)
    par = idx % 2
    qrw, krw = A.qr2[par], A.kr2[par]
    vpw, negCw = A.vp2[par], A.negC2[par]
    soff = par
    prow = b * 64
    PB = {}

    def vidx(hh, sblk, j):
        return 12 * hh + 3 * sblk + j

    def lead(t):
        for k_ in ("ve", "gp", "gp2", "pe", "sc", "sy"):
            PB[k_] = C.semv[k_]
        PB["gp_rp"] = PB["gp"] + 1
        PB["gp2_rp"] = PB["gp2"] + (PAIR_DC[-1] if merged else 0)
        PB["ve_rp"] = PB["ve"] + 5
        PB["ve_nb"] = PB["ve"] + 37
        PB["pe_rp"] = PB["pe"] + 8
        PB["pe_nb"] = PB["pe"] + 24
        PB["sc_rp"] = PB["sc"]
        PB["sc_nb"] = PB["sc"] + 8

    def guard(t):
        b0 = C._sl
        t.wait_ge(C.sems["ve3"], b0["ve3"] + 9)
        t.wait_ge(C.sems["sc2"], b0["sc2"] + NIT)
        t.wait_ge(C.sems["ve2"], b0["ve2"] + NIT)

    def pp_g(g_):
        dinc(g_.dma_start(
            A.wqp[:, :].rearrange("p (ic c) -> p ic c", c=128),
            C.wq_h[:, g * 128:(g + 1) * 128].rearrange("(ic p) c -> p ic c",
                                                       p=128)))
        dinc(g_.dma_start(
            A.wkp[:, :].rearrange("p (ic c) -> p ic c", c=128),
            C.wk_h[:, g * 128:(g + 1) * 128].rearrange("(ic p) c -> p ic c",
                                                       p=128)))
        vpv = vpw[:, :].rearrange("p (tt c) -> p tt c", c=130)
        dinc(g_.dma_start(
            vpv[:, :, 0:64],
            C.vtm_h[b, :, :, g * 128:g * 128 + 64].rearrange(
                "tt p c -> p tt c")))
        dinc(g_.dma_start(
            vpv[:, :, 65:129],
            C.vtm_h[b, :, :, g * 128 + 64:g * 128 + 128].rearrange(
                "tt p c -> p tt c")))
        if idx < 2:
            g_.memset(vpv[:, :, 64:65], 1.0)
            g_.memset(vpv[:, :, 129:130], 1.0)
        wait(g_, "ld")
        g_.drain().then_inc(C.sems["gp"], 1)
        C.semv["gp"] += 1

    def rp_loads0(g_):
        row0 = 8 * g
        for (dst, hsrc) in ((A.pfrep, C.pf_h), (A.urep, C.u_h),
                            (A.vrep, C.v_h)):
            src = bass.AP(hsrc.tensor, b * 64 * S + row0 * S,
                          [[S, 4], [0, 32], [1, S]])
            dinc(g_.dma_start(dst[:, :], src))
        g_.wait_ge(C.sems["ld"], C.semv["ld"])
        g_.drain().then_inc(C.sems["gp"], 1)
        C.semv["gp"] += 1

    def pp_sy_early(sy):
        r0 = prow + 8 * g
        c0s = soff * S
        dinc(sy.dma_start(A.strip_all[0:4, c0s:c0s + S],
                          A.pbf_all[r0:r0 + 4, :]), "st")
        dinc(sy.dma_start(A.strip_all[32:36, c0s:c0s + S],
                          A.pbf_all[r0 + 4:r0 + 8, :]), "st")

    def pp_sy_late(sy):
        sy.wait_ge(C.sems["ve"], PB["ve"] + 5)
        for (dst, src) in ((A.qrot, A.qbf), (A.krot, A.kbf)):
            dinc(sy.dma_start(dst[0:32, :], src[32:64, :]), "st")
            dinc(sy.dma_start(dst[32:64, :], src[0:32, :]), "st")
            dinc(sy.dma_start(dst[64:96, :], src[96:128, :]), "st")
            dinc(sy.dma_start(dst[96:128, :], src[64:96, :]), "st")
        wait(sy, "st")
        sy.drain().then_inc(C.sems["sy"], 1)
        C.semv["sy"] += 1

    def pp_pe(t):
        t.wait_ge(C.sems["gp"], PB["gp"] + 1)
        for sblk in range(4):
            if sblk >= 1:
                t.wait_ge(C.sems["ve"], PB["ve"] + sblk)
            for ic in range(8):
                rhs = C.xfm[:, ic * NT + b * S + sblk * 512:
                            ic * NT + b * S + sblk * 512 + 512]
                mq = t.matmul(pb[0][:, :], A.wqp[:, ic * 128:(ic + 1) * 128],
                              rhs, start=(ic == 0), stop=(ic == 7))
                mk = t.matmul(pb[1][:, :], A.wkp[:, ic * 128:(ic + 1) * 128],
                              rhs, start=(ic == 0), stop=(ic == 7))
            mq.then_inc(C.sems["pe"], 1)
            mk.then_inc(C.sems["pe"], 1)
        C.semv["pe"] += 8

    def pp_ve(v):
        for sblk in range(4):
            v.wait_ge(C.sems["pe"], PB["pe"] + 2 * sblk + 2)
            v.tensor_copy(A.qbf[:, sblk * 512:(sblk + 1) * 512], pb[0][:, :])
            v.tensor_copy(A.kbf[:, sblk * 512:(sblk + 1) * 512], pb[1][:, :])
            v.drain().then_inc(C.sems["ve"], 1)
            C.semv["ve"] += 1
        v.drain().then_inc(C.sems["ve"], 1)
        C.semv["ve"] += 1

    def rp_g_main(g_):
        for hh in range(2):
            if hh == 1:
                g_.wait_ge(C.sems["ve"], PB["ve_rp"] + vidx(0, 3, 3))
                row0 = 8 * g + 4
                for (dst, hsrc) in ((A.pfrep, C.pf_h), (A.urep, C.u_h),
                                    (A.vrep, C.v_h)):
                    src = bass.AP(hsrc.tensor, b * 64 * S + row0 * S,
                                  [[S, 4], [0, 32], [1, S]])
                    dinc(g_.dma_start(dst[:, :], src))
                g_.wait_ge(C.sems["ld"], C.semv["ld"])
                g_.drain().then_inc(C.sems["gp"], 1)
                C.semv["gp"] += 1
            for sblk in range(4):
                g_.wait_ge(C.sems["ve"], PB["ve_rp"] + vidx(hh, sblk, 1))
                g_.tensor_copy(C.scri[:, 0:1024], C.scr[:, 0:1024])
                g_.drain().then_inc(C.sems["gp"], 1)
                C.semv["gp"] += 1
        for sblk in range(4):
            c0 = sblk * 512
            g_.wait_ge(C.sems["ve"], PB["ve_rp"] + 24 + 2 * sblk + 1)
            g_.tensor_tensor(qrw[:, c0:c0 + 512], A.tmpq[:, :], A.tmpq2[:, :],
                             Op.add)
            g_.drain().then_inc(C.sems["gp2"], 1)
            C.semv["gp2"] += 1
            g_.wait_ge(C.sems["ve"], PB["ve_rp"] + 24 + 2 * sblk + 2)
            g_.tensor_tensor(krw[:, c0:c0 + 512], A.tmpq[:, :], A.tmpq2[:, :],
                             Op.add)
            g_.drain().then_inc(C.sems["gp2"], 1)
            C.semv["gp2"] += 1

    def rp_v(v):
        for hh in range(2):
            for sblk in range(4):
                n_ = 4 * hh + sblk
                col = sblk * 512
                v.wait_ge(C.sems["gp"], PB["gp_rp"] + hh * 5 + 1)
                if n_ >= 1:
                    ph, psb = divmod(n_ - 1, 4)
                    v.wait_ge(C.sems["gp"], PB["gp_rp"] + ph * 5 + psb + 2)
                v.tensor_scalar(C.scr[:, 0:512], A.pfrep[:, col:col + 512],
                                C.wcol[:, 2:3], C.wcol[:, 3:4], Op.mult, Op.add)
                v.tensor_scalar(C.scr[:, 512:1024], A.pfrep[:, col:col + 512],
                                C.wcol[:, 2:3], C.wcol[:, 4:5], Op.mult, Op.add)
                v.drain().then_inc(C.sems["ve"], 1)
                C.semv["ve"] += 1
                v.wait_ge(C.sems["gp"], PB["gp_rp"] + hh * 5 + sblk + 2)
                v.tensor_copy(C.scr2[:, 0:1024], C.scri[:, 0:1024])
                v.drain()
                v.tensor_sub(C.scr[:, 0:1024], C.scr[:, 0:1024],
                             C.scr2[:, 0:1024])
                v.drain().then_inc(C.sems["ve"], 1)
                C.semv["ve"] += 1
                v.wait_ge(C.sems["sc"], PB["sc_rp"] + n_ + 1)
                if n_ >= 1:
                    v.wait_ge(C.sems["pe"], PB["pe_rp"] + 2 * n_)
                v.tensor_tensor(A.prods[:, 0:512], A.urep[:, col:col + 512],
                                A.trigb[:, 512:1024], Op.mult)
                v.tensor_tensor(A.prods[:, 512:1024], A.vrep[:, col:col + 512],
                                A.trigb[:, 512:1024], Op.mult)
                v.tensor_tensor(A.prods[:, 1024:1536], A.urep[:, col:col + 512],
                                A.trigb[:, 0:512], Op.mult)
                v.tensor_tensor(A.prods[:, 1536:2048], A.vrep[:, col:col + 512],
                                A.trigb[:, 0:512], Op.mult)
                v.drain().then_inc(C.sems["ve"], 1)
                C.semv["ve"] += 1
        v.wait_ge(C.sems["sy"], PB["sy"] + 1)
        for sblk in range(4):
            c0 = sblk * 512
            v.wait_ge(C.sems["pe"], PB["pe_rp"] + 2 * (4 + sblk) + 2)
            if sblk >= 1:
                v.wait_ge(C.sems["gp2"], PB["gp2_rp"] + 2 * sblk)
            v.tensor_tensor(A.tmpq[:, :], A.qbf[:, c0:c0 + 512],
                            pb[sblk][:, :], Op.mult)
            v.tensor_tensor(A.tmpq2[:, :], A.qrot[:, c0:c0 + 512],
                            pb[4 + sblk][:, :], Op.mult)
            v.drain().then_inc(C.sems["ve"], 1)
            C.semv["ve"] += 1
            v.wait_ge(C.sems["gp2"], PB["gp2_rp"] + 2 * sblk + 1)
            v.tensor_tensor(A.tmpq[:, :], A.kbf[:, c0:c0 + 512],
                            pb[sblk][:, :], Op.mult)
            v.tensor_tensor(A.tmpq2[:, :], A.krot[:, c0:c0 + 512],
                            pb[4 + sblk][:, :], Op.mult)
            v.drain().then_inc(C.sems["ve"], 1)
            C.semv["ve"] += 1

    def rp_s(s):
        for hh in range(2):
            for sblk in range(4):
                n_ = 4 * hh + sblk
                s.wait_ge(C.sems["ve"], PB["ve_rp"] + vidx(hh, sblk, 2))
                if n_ >= 1:
                    ph, psb = divmod(n_ - 1, 4)
                    s.wait_ge(C.sems["ve"], PB["ve_rp"] + vidx(ph, psb, 3))
                s.activation(A.trigb[:, 0:512], C.scr[:, 0:512], AF.Sin,
                             bias=C.disc[:, 9:10], scale=TWO_PI)
                s.activation(A.trigb[:, 512:1024], C.scr[:, 512:1024], AF.Sin,
                             bias=C.disc[:, 9:10], scale=TWO_PI)
                s.drain().then_inc(C.sems["sc"], 1)
                C.semv["sc"] += 1

    def rp_pe(t):
        sel = C.seldup
        for hh in range(2):
            r0 = hh * 64
            for sblk in range(4):
                t.wait_ge(C.sems["ve"], PB["ve_rp"] + vidx(hh, sblk, 3))
                P1 = A.prods[:, 0:512]
                P2 = A.prods[:, 512:1024]
                P3 = A.prods[:, 1024:1536]
                P4 = A.prods[:, 1536:2048]
                cw, sw = pb[sblk], pb[4 + sblk]
                t.matmul(cw[r0:r0 + 64, :], sel[:, 0:64], P1, start=True,
                         stop=False, skip_group_check=True)
                t.matmul(cw[r0:r0 + 64, :], sel[:, 64:128], P2, start=False,
                         stop=False, skip_group_check=True)
                t.matmul(cw[r0:r0 + 64, :], sel[:, 128:192], P4, start=False,
                         stop=True,
                         skip_group_check=True).then_inc(C.sems["pe"], 1)
                t.matmul(sw[r0:r0 + 64, :], sel[:, 192:256], P3, start=True,
                         stop=False, skip_group_check=True)
                t.matmul(sw[r0:r0 + 64, :], sel[:, 256:320], P4, start=False,
                         stop=False, skip_group_check=True)
                t.matmul(sw[r0:r0 + 64, :], sel[:, 320:384], P2, start=False,
                         stop=True,
                         skip_group_check=True).then_inc(C.sems["pe"], 1)
                C.semv["pe"] += 2

    def nb_s2(s):
        s.wait_ge(C.sems["gp2"], PB["gp2_rp"] + 7)
        s.activation(C.scrb[:, 0:2048], qrw[:, :], AF.Square, bias=ccol(0.0),
                     scale=1.0)
        s.drain().then_inc(C.sems["sc"], 1)
        C.semv["sc"] += 1
        s.wait_ge(C.sems["pe"], PB["pe_nb"] + 4)
        s.wait_ge(C.sems["gp2"], PB["gp2_rp"] + 8)
        s.activation(C.scrb[:, 0:2048], krw[:, :], AF.Square, bias=ccol(0.0),
                     scale=1.0)
        s.drain().then_inc(C.sems["sc"], 1)
        C.semv["sc"] += 1

    def nb_pe2(t):
        t.wait_ge(C.sems["sc"], PB["sc_nb"] + 1)
        for i_ in range(2):
            if i_ == 1:
                t.wait_ge(C.sems["sc"], PB["sc_nb"] + 2)
            for sblk in range(4):
                bank = pb[4] if sblk % 2 == 0 else pb[5]
                lin = i_ * 4 + sblk
                if lin >= 2:
                    t.wait_ge(C.sems["ve"], PB["ve_nb"] + lin - 1)
                t.matmul(bank[0:2, :], C.rowsel2[:, :],
                         C.scrb[:, sblk * 512:(sblk + 1) * 512],
                         start=True, stop=True,
                         skip_group_check=True).then_inc(C.sems["pe"], 1)
                C.semv["pe"] += 1

    def nb_ve(v):
        for i_ in range(2):
            for sblk in range(4):
                bank = pb[4] if sblk % 2 == 0 else pb[5]
                v.wait_ge(C.sems["pe"], PB["pe_nb"] + i_ * 4 + sblk + 1)
                v.tensor_reduce(A.qn[0:2, i_ * 8 + sblk:i_ * 8 + sblk + 1],
                                bank[0:2, :], AX, Op.max)
                v.drain().then_inc(C.sems["ve"], 1)
                C.semv["ve"] += 1
        v.tensor_tensor(A.qn[0:2, 4:5], A.qn[0:2, 0:1], A.qn[0:2, 1:2], Op.max)
        v.tensor_tensor(A.qn[0:2, 5:6], A.qn[0:2, 2:3], A.qn[0:2, 3:4], Op.max)
        v.tensor_tensor(A.qn[0:2, 12:13], A.qn[0:2, 8:9], A.qn[0:2, 9:10], Op.max)
        v.tensor_tensor(A.qn[0:2, 13:14], A.qn[0:2, 10:11], A.qn[0:2, 11:12], Op.max)
        v.drain()
        v.tensor_tensor(A.qn[0:2, 14:15], A.qn[0:2, 4:5], A.qn[0:2, 5:6], Op.max)
        v.tensor_tensor(A.qn[0:2, 15:16], A.qn[0:2, 12:13], A.qn[0:2, 13:14], Op.max)
        v.drain()
        v.tensor_add(A.qn[0:2, 6:7], A.qn[0:2, 14:15], A.qn[0:2, 15:16])
        v.drain()
        v.tensor_scalar(A.qn[0:2, 7:8], A.qn[0:2, 6:7], -1.01 / 16.0, None,
                        Op.mult)
        v.drain().then_inc(C.sems["ve"], 1)
        C.semv["ve"] += 1

    def nb_pe3(t):
        t.wait_ge(C.sems["ve"], PB["ve_nb"] + 9)
        t.matmul(pb[6][:, 0:1], A.selABf[0:2, 0:128], A.qn[0:2, 7:8],
                 start=True, stop=True,
                 skip_group_check=True).then_inc(C.sems["pe"], 1)
        t.matmul(pb[6][:, 1:2], A.selABf[0:2, 128:256], A.qn[0:2, 7:8],
                 start=True, stop=True,
                 skip_group_check=True).then_inc(C.sems["pe"], 1)
        C.semv["pe"] += 2

    def nb_ve2(v):
        v.wait_ge(C.sems["pe"], PB["pe_nb"] + 10)
        v.tensor_copy(negCw[:, 0:2], pb[6][:, 0:2])

    return dict(lead=lead, guard=guard,
                tensor=[pp_pe, rp_pe, nb_pe2, nb_pe3],
                scalar=[rp_s, nb_s2],
                vector=[pp_ve, rp_v, nb_ve, nb_ve2],
                gp_early=[pp_g, rp_loads0], gp_late=[rp_g_main],
                sy_early=[pp_sy_early], sy_late=[pp_sy_late])


PAIR_PRE = [0]
for _ib in range(4):
    PAIR_PRE.append(PAIR_PRE[-1] + 4 * _ib + 4)


def _pair_scores(C, A, b, g, qr, kr, vp, soff, negC):
    nc, block, inc, dinc, wait, ccol = C.nc, C.block, C.inc, C.dinc, C.wait, C.ccol
    pb = C.pb

    def sl_pe(t):
        C._sl = {k_: C.semv[k_] for k_ in
                 ("pe2", "ve2", "sc2", "gp2", "ve3", "sy2")}
        b0 = C._sl
        C._sl["scores_done"] = [0] * NIT
        C._sl["nw_done"] = [0] * NIT
        C._sl["pv_done"] = [0] * 4
        C._sl["norm_done"] = [0] * 4
        pe2c = [b0["pe2"]]

        def pinc(mm):
            mm.then_inc(C.sems["pe2"], 1)
            pe2c[0] += 1

        def emit_pv(m):
            ib_, jc_ = PAIR_ITERS[m]
            last = (jc_ == 4 * ib_ + 3)
            if PAIR_DIAG[m]:
                t.wait_ge(C.sems["gp2"], b0["gp2"] + PAIR_DC[m])
            else:
                t.wait_ge(C.sems["ve2"], b0["ve2"] + m + 1)
            if jc_ == 0:
                t.wait_ge(C.sems["ve3"], b0["ve3"] + ib_)
            pA, pB = A.ptA[m % 2], A.ptB[m % 2]
            c0 = jc_ * 130
            mA = t.matmul(pb[6][0:65, :], vp[:, c0:c0 + 65], pA[:, :],
                          start=(jc_ == 0), stop=last, skip_group_check=True)
            mB = t.matmul(pb[7][0:65, :], vp[:, c0 + 65:c0 + 130], pB[:, :],
                          start=(jc_ == 0), stop=last, skip_group_check=True)
            if last:
                pinc(mA)
                pinc(mB)
                C._sl["pv_done"][ib_] = pe2c[0]

        for m, (ib_, jc_) in enumerate(PAIR_ITERS):
            iA, iB = pb[0 + m % 2], pb[2 + m % 2]
            j0, i0 = jc_ * 128, ib_ * 512
            if m >= 2:
                t.wait_ge(C.sems["sc2"], b0["sc2"] + m - 1)
            pinc(t.matmul(iA[:, :], kr[0:64, j0:j0 + 128],
                          qr[0:64, i0:i0 + 512], start=True, stop=True,
                          skip_group_check=True))
            pinc(t.matmul(iB[:, :], kr[64:128, j0:j0 + 128],
                          qr[64:128, i0:i0 + 512], start=True, stop=True,
                          skip_group_check=True))
            C._sl["scores_done"][m] = pe2c[0]
            if m >= 1:
                t.wait_ge(C.sems["ve2"], b0["ve2"] + m)
            c0s = soff * S
            pinc(t.matmul(pb[4][:, :],
                          A.strip_all[0:5, c0s + j0:c0s + j0 + 128],
                          A.strip_all[0:5, c0s + i0:c0s + i0 + 512],
                          start=True, stop=True, skip_group_check=True))
            pinc(t.matmul(pb[5][:, :],
                          A.strip_all[32:37, c0s + j0:c0s + j0 + 128],
                          A.strip_all[32:37, c0s + i0:c0s + i0 + 512],
                          start=True, stop=True, skip_group_check=True))
            C._sl["nw_done"][m] = pe2c[0]
            if m >= 1:
                emit_pv(m - 1)
        emit_pv(NIT - 1)
        for ib_ in range(4):
            t.wait_ge(C.sems["ve3"], b0["ve3"] + 5 + ib_)
            pinc(t.matmul(pb[7][:, :], C.rsel[ib_][:, :], A.sstg2[0:8, :],
                          start=True, stop=True, skip_group_check=True))
            C._sl["norm_done"][ib_] = pe2c[0]
        C.semv["pe2"] = pe2c[0]

    def sl_sc(s):
        b0 = C._sl
        for m in range(NIT):
            s.wait_ge(C.sems["pe2"], b0["scores_done"][m])
            if m >= 2:
                s.wait_ge(C.sems["ve2"], b0["ve2"] + m - 1)
            s.activation(A.eA[m % 2][:, :], pb[0 + m % 2][:, :], AF.Exp,
                         bias=negC[:, 0:1], scale=0.125)
            s.activation(A.eB[m % 2][:, :], pb[2 + m % 2][:, :], AF.Exp,
                         bias=negC[:, 1:2], scale=0.125)
            s.drain().then_inc(C.sems["sc2"], 1)
        C.semv["sc2"] += NIT

    def sl_ve(v):
        b0 = C._sl

        def epilogue(ib_):
            v.wait_ge(C.sems["pe2"], b0["pv_done"][ib_])
            if ib_ >= 1:
                v.wait_ge(C.sems["sy2"], b0["sy2"] + ib_)
            v.tensor_copy(A.outev[0:64, ib_ * 512:(ib_ + 1) * 512],
                          pb[6][0:64, :])
            v.tensor_copy(A.outev[64:128, ib_ * 512:(ib_ + 1) * 512],
                          pb[7][0:64, :])
            v.tensor_copy(A.sumstg_a, pb[6][64:65, :])
            v.tensor_copy(A.sumstg_b, pb[7][64:65, :])
            v.drain().then_inc(C.sems["ve3"], 1)

        epi_at = {PAIR_PRE[i_ + 1]: i_ for i_ in range(3)}
        for m in range(NIT):
            v.wait_ge(C.sems["sc2"], b0["sc2"] + m + 1)
            v.wait_ge(C.sems["pe2"], b0["nw_done"][m])
            v.tensor_tensor(A.ptA[m % 2][:, :], A.eA[m % 2][:, :], pb[4][:, :],
                            Op.mult)
            v.tensor_tensor(A.ptB[m % 2][:, :], A.eB[m % 2][:, :], pb[5][:, :],
                            Op.mult)
            v.drain().then_inc(C.sems["ve2"], 1)
            if m in epi_at:
                epilogue(epi_at[m])
        C.semv["ve2"] += NIT
        epilogue(3)
        v.wait_ge(C.sems["sy2"], b0["sy2"] + 4)
        v.reciprocal(A.sstg2[0:8, :], A.sstg[0:8, :])
        v.drain().then_inc(C.sems["ve3"], 1)
        for ib_ in range(4):
            v.wait_ge(C.sems["pe2"], b0["norm_done"][ib_])
            v.tensor_tensor(A.attp[:, ib_ * 512:(ib_ + 1) * 512],
                            A.outev[:, ib_ * 512:(ib_ + 1) * 512], pb[7][:, :],
                            Op.mult)
            v.drain().then_inc(C.sems["ve3"], 1)
        C.semv["ve3"] += 9

    def sl_gp(g_):
        b0 = C._sl
        for m in range(NIT):
            if not PAIR_DIAG[m]:
                continue
            ib_, jc_ = PAIR_ITERS[m]
            mi = jc_ - 4 * ib_
            g_.wait_ge(C.sems["ve2"], b0["ve2"] + m + 1)
            g_.tensor_tensor(A.ptA[m % 2][:, :], A.ptA[m % 2][:, :],
                             C.masks[:, mi * 512:(mi + 1) * 512], Op.mult)
            g_.tensor_tensor(A.ptB[m % 2][:, :], A.ptB[m % 2][:, :],
                             C.masks[:, mi * 512:(mi + 1) * 512], Op.mult)
            g_.drain().then_inc(C.sems["gp2"], 1)
        C.semv["gp2"] += PAIR_DC[-1]

    def sl_sy(sy):
        b0 = C._sl
        for ib_ in range(4):
            sy.wait_ge(C.sems["ve3"], b0["ve3"] + ib_ + 1)
            st0 = C.semv["st"]
            dinc(sy.dma_start(A.sstg[ib_ * 2:ib_ * 2 + 1, :],
                              A.sumstg_a), "st")
            dinc(sy.dma_start(A.sstg[ib_ * 2 + 1:ib_ * 2 + 2, :],
                              A.sumstg_b), "st")
            sy.wait_ge(C.sems["st"], st0 + 32)
            sy.drain().then_inc(C.sems["sy2"], 1)
            C.semv["sy2"] += 1
        sy.wait_ge(C.sems["ve3"], b0["ve3"] + 9)
        dinc(sy.dma_start(C.attno_h[g, :, b * S:(b + 1) * S], A.attp[:, :]),
             "st")
    return dict(tensor=sl_pe, scalar=sl_sc, vector=sl_ve, gpsimd=sl_gp,
                sync=sl_sy)


def _wo_ln1_quant(C):
    nc, block, inc, dinc, wait, ccol = C.nc, C.block, C.inc, C.dinc, C.wait, C.ccol
    pb = C.pb
    scope = ExitStack()
    e = scope.enter_context
    wosb = e(nc.sbuf_tensor("wosb", [128, 8 * E], BF16))
    attld = e(nc.sbuf_tensor("attld", [128, 8 * 512], BF16))
    h1f = e(nc.sbuf_tensor("h1f", [128, 8 * 512], F32))
    h1b = e(nc.sbuf_tensor("h1b", [128, 8 * 512], BF16))
    sqb = e(nc.sbuf_tensor("sqb", [128, 8 * 512], BF16))
    xlnb = e(nc.sbuf_tensor("xlnb", [128, 8 * 512], BF16))
    xqb = e(nc.sbuf_tensor("xqb", [128, 8 * 512], BF16))
    amcol = e(nc.sbuf_tensor("amcol", [128, 16], F32))
    rows = e(nc.sbuf_tensor("rows", [1, 8 * 512], F32))
    rowe = e(nc.sbuf_tensor("rowe", [8, 128], F32))
    C.invsx_h = C.dram("invsx_sc", [8, 512], F32, "Internal")

    def wo_ld(g):
        for c in range(8):
            dinc(g.dma_start(wosb[:, c * E:(c + 1) * E],
                             C.wo_h[c * 128:(c + 1) * 128, :]))
        wait(g, "ld")
    block(gpsimd=wo_ld)

    for tb in range(8):
        t0 = tb * 512
        def a_lsy(sy, t0=t0):
            inc(sy.dma_start(attld[:, :].rearrange("p (ic t) -> p ic t", t=512),
                             C.attno_h[:, :, t0:t0 + 512].rearrange(
                                 "ic p t -> p ic t")), "lds", 16)
            sy.wait_ge(C.sems["lds"], C.semv["lds"])
            sy.drain().then_inc(C.sems["sy"], 1)
            C.semv["sy"] += 1

        def a_pe(t, t0=t0):
            wait(t, "sy")
            for oc in range(8):
                for ic in range(8):
                    t.matmul(pb[oc][:, :],
                             wosb[:, ic * E + oc * 128:ic * E + oc * 128 + 128],
                             attld[:, ic * 512:(ic + 1) * 512],
                             start=(ic == 0), stop=(ic == 7))
        block(sync=a_lsy, tensor=a_pe)

        def b_v(v, t0=t0):
            C._p4b = C.semv["ve"]
            for oc in range(8):
                v.tensor_tensor(h1f[:, oc * 512:(oc + 1) * 512],
                                C.xfm[:, oc * NT + t0:oc * NT + t0 + 512],
                                pb[oc][:, :], Op.add)
                v.drain()
                v.tensor_copy(h1b[:, oc * 512:(oc + 1) * 512],
                              h1f[:, oc * 512:(oc + 1) * 512])
                v.drain().then_inc(C.sems["ve"], 1)
                C.semv["ve"] += 1

        def b_s(s, t0=t0):
            for oc in range(8):
                s.wait_ge(C.sems["ve"], C._p4b + oc + 1)
                s.activation(sqb[:, oc * 512:(oc + 1) * 512],
                             h1f[:, oc * 512:(oc + 1) * 512], AF.Square,
                             bias=ccol(0.0), scale=1.0)
                s.drain().then_inc(C.sems["sc"], 1)
                C.semv["sc"] += 1
        block(vector=b_v, scalar=b_s)

        def c_pe(t, t0=t0):
            for oc in range(8):
                m0 = t.matmul(pb[0][0:1, :], C.ones_bf[:, 0:1],
                              h1b[:, oc * 512:(oc + 1) * 512],
                              start=(oc == 0), stop=(oc == 7),
                              skip_group_check=True)
                m1 = t.matmul(pb[1][0:1, :], C.ones_bf[:, 0:1],
                              sqb[:, oc * 512:(oc + 1) * 512],
                              start=(oc == 0), stop=(oc == 7),
                              skip_group_check=True)
            m0.then_inc(C.sems["pe"], 1)
            m1.then_inc(C.sems["pe"], 1)
            C.semv["pe"] += 2

        def c_v(v, t0=t0):
            wait(v, "pe")
            v.tensor_scalar(rows[0:1, 0:512], pb[0][0:1, :], 1.0 / E, None,
                            Op.mult)
            v.tensor_scalar(rows[0:1, 512:1024], pb[1][0:1, :], 1.0 / E, None,
                            Op.mult)
            for q in range(4):
                v.tensor_tensor(C.scr[:, q * 512:(q + 1) * 512],
                                h1f[:, 2 * q * 512:(2 * q + 1) * 512],
                                h1f[:, (2 * q + 1) * 512:(2 * q + 2) * 512],
                                Op.max)
                v.tensor_tensor(C.scr2[:, q * 512:(q + 1) * 512],
                                h1f[:, 2 * q * 512:(2 * q + 1) * 512],
                                h1f[:, (2 * q + 1) * 512:(2 * q + 2) * 512],
                                Op.min)
            v.drain()
            v.tensor_tensor(rows[0:1, 1024:1536], rows[0:1, 0:512],
                            rows[0:1, 0:512], Op.mult)
            v.tensor_tensor(C.scr[:, 0:512], C.scr[:, 0:512],
                            C.scr[:, 512:1024], Op.max)
            v.tensor_tensor(C.scr[:, 1024:1536], C.scr[:, 1024:1536],
                            C.scr[:, 1536:2048], Op.max)
            v.tensor_tensor(C.scr2[:, 0:512], C.scr2[:, 0:512],
                            C.scr2[:, 512:1024], Op.min)
            v.tensor_tensor(C.scr2[:, 1024:1536], C.scr2[:, 1024:1536],
                            C.scr2[:, 1536:2048], Op.min)
            v.drain()
            v.tensor_sub(rows[0:1, 1024:1536], rows[0:1, 512:1024],
                         rows[0:1, 1024:1536])
            v.tensor_tensor(C.scr[:, 0:512], C.scr[:, 0:512],
                            C.scr[:, 1024:1536], Op.max)
            v.tensor_tensor(C.scr2[:, 0:512], C.scr2[:, 0:512],
                            C.scr2[:, 1024:1536], Op.min)
            v.drain().then_inc(C.sems["ve"], 1)
            C.semv["ve"] += 1

        def c_s(s, t0=t0):
            wait(s, "ve")
            s.activation(rows[0:1, 1536:2048], rows[0:1, 1024:1536], AF.Sqrt,
                         bias=ccol(EPS_LN)[0:1, :], scale=1.0)
            s.drain().then_inc(C.sems["sc"], 1)
            C.semv["sc"] += 1

        def c_pe2(t, t0=t0):
            wait(t, "ve")
            for q in range(4):
                t.transpose(pb[2][:, q * 128:(q + 1) * 128],
                            C.scr[:, q * 128:(q + 1) * 128],
                            C.idn_f[:, :]).then_inc(C.sems["pe"], 1)
                t.transpose(pb[3][:, q * 128:(q + 1) * 128],
                            C.scr2[:, q * 128:(q + 1) * 128],
                            C.idn_f[:, :]).then_inc(C.sems["pe"], 1)
            C.semv["pe"] += 8

        def c_v2(v, t0=t0):
            wait(v, "pe")
            wait(v, "sc")
            for q in range(4):
                v.tensor_reduce(amcol[:, q:q + 1],
                                pb[2][:, q * 128:(q + 1) * 128], AX, Op.max)
                v.tensor_reduce(amcol[:, 4 + q:5 + q],
                                pb[3][:, q * 128:(q + 1) * 128], AX, Op.min)
            v.reciprocal(rows[0:1, 1024:1536], rows[0:1, 1536:2048])
            v.drain()
            v.tensor_tensor(rows[0:1, 1536:2048], rows[0:1, 0:512],
                            rows[0:1, 1024:1536], Op.mult)
            v.drain().then_inc(C.sems["ve"], 1)
            C.semv["ve"] += 1

        def c_pe3(t, t0=t0):
            wait(t, "ve")
            t.transpose(pb[4][0:8, 0:128], amcol[:, 0:8],
                        C.idn_f[:, :]).then_inc(C.sems["pe"], 1)
            C.semv["pe"] += 1

        def c_v3(v, t0=t0):
            wait(v, "pe")
            v.tensor_copy(rowe[0:8, :], pb[4][0:8, 0:128])
            v.drain().then_inc(C.sems["ve"], 1)
            C.semv["ve"] += 1

        def c_sy(sy, tb=tb):
            wait(sy, "ve")
            for q in range(4):
                dinc(sy.dma_start(C.scr[0:1, 1024 + q * 128:1152 + q * 128],
                                  rowe[q:q + 1, :]), "st")
                dinc(sy.dma_start(C.scr[0:1, 1536 + q * 128:1664 + q * 128],
                                  rowe[4 + q:5 + q, :]), "st")
            wait(sy, "st")
            sy.drain().then_inc(C.sems["sy"], 1)
            C.semv["sy"] += 1

        def c_v4(v, tb=tb):
            wait(v, "sy")
            v.tensor_sub(C.scr2[0:1, 0:512], C.scr[0:1, 1024:1536],
                         rows[0:1, 0:512])
            v.tensor_sub(C.scr2[0:1, 512:1024], rows[0:1, 0:512],
                         C.scr[0:1, 1536:2048])
            v.drain()
            v.tensor_tensor(C.scr2[0:1, 0:512], C.scr2[0:1, 0:512],
                            C.scr2[0:1, 512:1024], Op.max)
            v.drain()
            v.tensor_tensor(C.scr2[0:1, 0:512], C.scr2[0:1, 0:512],
                            rows[0:1, 1024:1536], Op.mult)
            v.drain()
            v.tensor_scalar(C.scr2[0:1, 0:512], C.scr2[0:1, 0:512], 1e-5,
                            None, Op.max)
            v.drain()
            v.reciprocal(C.scr2[0:1, 512:1024], C.scr2[0:1, 0:512])
            v.drain()
            v.tensor_scalar(C.scr[0:1, 1024:1536], C.scr2[0:1, 512:1024],
                            127.0, None, Op.mult)
            v.tensor_scalar(C.scr[0:1, 1536:2048], C.scr2[0:1, 0:512],
                            1.0 / 127.0, None, Op.mult)
            v.drain().then_inc(C.sems["ve"], 1)
            C.semv["ve"] += 1

        def c_sy2(sy, tb=tb):
            wait(sy, "ve")
            dinc(sy.dma_start(C.invsx_h[tb, :], C.scr[0:1, 1536:2048]), "st")
            wait(sy, "st")
        block(tensor=c_pe, vector=c_v)
        block(scalar=c_s, tensor=c_pe2)
        block(vector=c_v2)
        block(tensor=c_pe3)
        block(vector=c_v3)
        block(sync=c_sy)
        block(vector=c_v4)
        block(sync=c_sy2)

        def d_pe(t, t0=t0):
            wait(t, "sy")
            t.matmul(pb[2][:, :], C.onerow_f[0:1, :], rows[0:1, 1024:1536],
                     start=True, stop=True,
                     skip_group_check=True).then_inc(C.sems["pe"], 1)
            t.matmul(pb[3][:, :], C.onerow_f[0:1, :], rows[0:1, 1536:2048],
                     start=True, stop=True,
                     skip_group_check=True).then_inc(C.sems["pe"], 1)
            t.matmul(pb[4][:, :], C.onerow_f[0:1, :], C.scr[0:1, 1024:1536],
                     start=True, stop=True,
                     skip_group_check=True).then_inc(C.sems["pe"], 1)
            C.semv["pe"] += 3

        def d_v(v, t0=t0):
            wait(v, "pe")
            for oc in range(8):
                sl = slice(oc * 512, (oc + 1) * 512)
                v.tensor_tensor(C.scr[:, 0:512], h1f[:, sl], pb[2][:, :],
                                Op.mult)
                v.drain()
                v.tensor_sub(C.scr[:, 0:512], C.scr[:, 0:512], pb[3][:, :])
                v.drain()
                v.tensor_copy(xlnb[:, sl], C.scr[:, 0:512])
                v.tensor_tensor(C.scr2[:, 0:512], C.scr[:, 0:512], pb[4][:, :],
                                Op.mult)
                v.drain()
                v.tensor_scalar(C.scr2[:, 0:512], C.scr2[:, 0:512],
                                C.disc[:, 2:3], None, Op.add)
                v.drain().then_inc(C.sems["ve"], 1)
                C.semv["ve"] += 1
                v.wait_ge(C.sems["gp"], C._p4g + oc + 1)
                v.tensor_copy(xqb[:, sl], C.scri[:, 0:512])
                v.drain().then_inc(C.sems["ve"], 1)
                C.semv["ve"] += 1

        def d_g(g, t0=t0):
            C._p4g = C.semv["gp"]
            b_ve = C.semv["ve"]
            for oc in range(8):
                g.wait_ge(C.sems["ve"], b_ve + 2 * oc + 1)
                g.tensor_copy(C.scri[:, 0:512], C.scr2[:, 0:512])
                g.drain().then_inc(C.sems["gp"], 1)
                C.semv["gp"] += 1

        def d_sy(sy, tb=tb, t0=t0):
            wait(sy, "ve")
            dinc(sy.dma_start(C.xln_h[:, :, t0:t0 + 512].rearrange(
                "ic p t -> p ic t"), xlnb[:, :].rearrange(
                "p (ic t) -> p ic t", t=512)), "st")
            dinc(sy.dma_start(C.xq_h[:, :, t0:t0 + 512].rearrange(
                "ic p t -> p ic t"), xqb[:, :].rearrange(
                "p (ic t) -> p ic t", t=512)), "st")
            wait(sy, "st")
        block(tensor=d_pe, gpsimd=d_g, vector=d_v)
        block(sync=d_sy)
        C.tap(f"xln{tb}", lambda t0=t0: xlnb[:, :].bitcast(F32), [128, 4 * 512])
        C.tap(f"xq{tb}", lambda t0=t0: xqb[:, :].bitcast(F32), [128, 4 * 512])
        C.tap(f"h1f{tb}", lambda t0=t0: h1f[:, :], [128, 8 * 512])
    scope.close()


def _ffn(C):
    nc, block, inc, dinc, wait, ccol = C.nc, C.block, C.inc, C.dinc, C.wait, C.ccol
    pb = C.pb
    C._xfm_cm.__exit__(None, None, None)
    scope = ExitStack()
    e = scope.enter_context
    hblk = e(nc.sbuf_tensor("hblk", [128, 32 * 512], F32))
    hqb = e(nc.sbuf_tensor("hqb", [128, 32 * 512], BF16))
    xqld = e(nc.sbuf_tensor("xqld", [128, 8 * 512], BF16))
    sq2 = xqld
    xlnld = e(nc.sbuf_tensor("xlnld", [128, 8 * 512], BF16))
    wgst = [e(nc.sbuf_tensor(f"wgst{i}", [128, 1024], BF16)) for i in range(2)]
    wust = [e(nc.sbuf_tensor(f"wust{i}", [128, 1024], BF16)) for i in range(2)]
    wdst = [e(nc.sbuf_tensor(f"wdst{i}", [128, 4096], BF16)) for i in range(5)]
    sgf = e(nc.sbuf_tensor("sgf", [128, 512], F32))
    t2f = e(nc.sbuf_tensor("t2f", [128, 512], F32))
    ibc = e(nc.sbuf_tensor("ibc", [128, 512], F32))
    sbc = e(nc.sbuf_tensor("sbc", [128, 512], F32))
    dbc = e(nc.sbuf_tensor("dbc", [128, 512], F32))
    amcol = e(nc.sbuf_tensor("amcol2", [128, 16], F32))
    h2f = hblk[:, 4096:8192]
    h2b = hblk.bitcast(BF16)[:, 16384:20480]
    outf = hblk[:, 0:4096]

    def f0_v(v):
        v.tensor_scalar(C.rsg[:, :], C.rsg[:, :], C.disc[:, 4:5], -1.0,
                        Op.mult, Op.mult)
        v.reciprocal(C.wqs[:, 3:6], C.wqs[:, 0:3])
        v.drain().then_inc(C.sems["ve"], 1)
        C.semv["ve"] += 1
    block(vector=f0_v)

    for tb in range(8):
        t0 = tb * 512
        def a_s(s, tb=tb, t0=t0):
            C._f5 = {k_: C.semv[k_] for k_ in
                     ("ld2", "lds", "lda", "gp", "ve", "sc", "pe")}
            b5 = C._f5
            inc(s.dma_start(xqld[:, :].rearrange("p (ic t) -> p ic t", t=512),
                            C.xq_h[:, :, t0:t0 + 512].rearrange(
                                "ic p t -> p ic t")), "lda", 16)
            inc(s.dma_start(C.scr[0:1, 0:512], C.invsx_h[tb, :]), "lda", 16)
            s.wait_ge(C.sems["lda"], C.semv["lda"])
            for icn in range(32):
                s.wait_ge(C.sems["ve"], b5["ve"] + 2 + 2 * icn)
                s.activation(sgf[:, :], C.scr2[:, 0:512], AF.Silu,
                             bias=ccol(0.0), scale=C.wqs[:, 3:4])
                s.drain().then_inc(C.sems["sc"], 1)
                C.semv["sc"] += 1

        def a_ls(sy, tb=tb, t0=t0):
            b5 = C._f5
            for icn in range(32):
                if icn >= 2:
                    sy.wait_ge(C.sems["pe"], b5["pe"] + 2 * icn - 1)
                if icn >= 1:
                    sy.wait_ge(C.sems["lds"], b5["lds"] + 32 * icn)
                inc(sy.dma_start(wgst[icn % 2][:, :].rearrange(
                    "p (ic c) -> p ic c", c=128),
                    C.wgT_h[icn, :, :].rearrange("(ic p) c -> p ic c", p=128)),
                    "lds", 16)
                inc(sy.dma_start(wust[icn % 2][:, :].rearrange(
                    "p (ic c) -> p ic c", c=128),
                    C.wuT_h[icn, :, :].rearrange("(ic p) c -> p ic c", p=128)),
                    "lds", 16)

        def a_pe(t, tb=tb, t0=t0):
            b5 = C._f5
            t.wait_ge(C.sems["lda"], b5["lda"] + 32)
            mm = t.matmul(pb[4][:, :], C.onerow_f[0:1, :], C.scr[0:1, 0:512],
                          start=True, stop=True, skip_group_check=True)
            mm.then_inc(C.sems["pe"], 1)
            C.semv["pe"] += 1
            for icn in range(32):
                t.wait_ge(C.sems["lds"], b5["lds"] + 32 * (icn + 1))
                if icn >= 2:
                    t.wait_ge(C.sems["ve"], b5["ve"] + 1 + 2 * (icn - 1))
                bg, bu = pb[icn % 2], pb[2 + icn % 2]
                for ic in range(8):
                    mm = t.matmul(bg[:, :],
                                  wgst[icn % 2][:, ic * 128:(ic + 1) * 128],
                                  xqld[:, ic * 512:(ic + 1) * 512],
                                  start=(ic == 0), stop=(ic == 7))
                mm.then_inc(C.sems["pe"], 1)
                for ic in range(8):
                    mm = t.matmul(bu[:, :],
                                  wust[icn % 2][:, ic * 128:(ic + 1) * 128],
                                  xqld[:, ic * 512:(ic + 1) * 512],
                                  start=(ic == 0), stop=(ic == 7))
                mm.then_inc(C.sems["pe"], 1)
                C.semv["pe"] += 2

        def a_v(v, tb=tb, t0=t0):
            b5 = C._f5
            v.wait_ge(C.sems["pe"], b5["pe"] + 1)
            v.tensor_copy(ibc[:, :], pb[4][:, :])
            v.drain().then_inc(C.sems["ve"], 1)
            C.semv["ve"] += 1
            for icn in range(32):
                sl = slice(icn * 512, (icn + 1) * 512)
                v.wait_ge(C.sems["pe"], b5["pe"] + 2 * icn + 2)
                v.scalar_tensor_tensor(C.scr2[:, 0:512], pb[icn % 2][:, :],
                                       C.rsg[:, icn:icn + 1], ibc[:, :],
                                       Op.add, Op.mult)
                v.drain().then_inc(C.sems["ve"], 1)
                C.semv["ve"] += 1
                v.wait_ge(C.sems["pe"], b5["pe"] + 2 * icn + 3)
                v.wait_ge(C.sems["sc"], b5["sc"] + icn + 1)
                v.scalar_tensor_tensor(t2f[:, :], pb[2 + icn % 2][:, :],
                                       C.rsg[:, 32 + icn:33 + icn], ibc[:, :],
                                       Op.add, Op.mult)
                v.drain()
                v.tensor_tensor(hblk[:, sl], sgf[:, :], t2f[:, :], Op.mult)
                v.drain()
                if icn == 0:
                    v.tensor_copy(C.scr[:, 512:1024], hblk[:, sl])
                    v.tensor_copy(C.scr[:, 1024:1536], hblk[:, sl])
                else:
                    v.tensor_tensor(C.scr[:, 512:1024], C.scr[:, 512:1024],
                                    hblk[:, sl], Op.max)
                    v.tensor_tensor(C.scr[:, 1024:1536], C.scr[:, 1024:1536],
                                    hblk[:, sl], Op.min)
                v.drain().then_inc(C.sems["ve"], 1)
                C.semv["ve"] += 1

        block(scalar=a_s, sync=a_ls, tensor=a_pe, vector=a_v)

        def bb_pe(t, tb=tb):
            for q in range(4):
                t.transpose(pb[2][:, q * 128:(q + 1) * 128],
                            C.scr[:, 512 + q * 128:640 + q * 128],
                            C.idn_f[:, :]).then_inc(C.sems["pe"], 1)
                t.transpose(pb[3][:, q * 128:(q + 1) * 128],
                            C.scr[:, 1024 + q * 128:1152 + q * 128],
                            C.idn_f[:, :]).then_inc(C.sems["pe"], 1)
            C.semv["pe"] += 8

        def bb_v(v, tb=tb):
            wait(v, "pe")
            for q in range(4):
                v.tensor_reduce(amcol[:, q:q + 1],
                                pb[2][:, q * 128:(q + 1) * 128], AX, Op.max)
                v.tensor_reduce(amcol[:, 4 + q:5 + q],
                                pb[3][:, q * 128:(q + 1) * 128], AX, Op.min)
            v.drain()
            v.tensor_scalar(amcol[:, 4:8], amcol[:, 4:8], -1.0, None, Op.mult)
            v.drain()
            v.tensor_tensor(amcol[:, 0:4], amcol[:, 0:4], amcol[:, 4:8], Op.max)
            v.drain()
            v.tensor_scalar(amcol[:, 8:12], amcol[:, 0:4], C.wqs[:, 4:5],
                            None, Op.mult)
            v.drain()
            v.tensor_scalar(amcol[:, 8:12], amcol[:, 8:12], 1e-5, None, Op.max)
            v.drain()
            v.reciprocal(amcol[:, 12:16], amcol[:, 8:12])
            v.drain()
            v.tensor_scalar(amcol[:, 0:4], amcol[:, 12:16], 127.0,
                            C.wqs[:, 4:5], Op.mult, Op.mult)
            v.tensor_scalar(amcol[:, 4:8], amcol[:, 8:12], C.wqs[:, 5:6],
                            1.0 / 127.0, Op.mult, Op.mult)
            v.drain().then_inc(C.sems["ve"], 1)
            C.semv["ve"] += 1

        def bb_pe2(t, tb=tb):
            wait(t, "ve")
            t.transpose(pb[4][0:8, 0:128], amcol[:, 0:8],
                        C.idn_f[:, :]).then_inc(C.sems["pe"], 1)
            C.semv["pe"] += 1

        def bb_v2(v, tb=tb):
            wait(v, "pe")
            v.tensor_copy(C.scr2[0:8, 512:640], pb[4][0:8, 0:128])
            v.drain().then_inc(C.sems["ve"], 1)
            C.semv["ve"] += 1

        def bb_sy(sy, tb=tb):
            wait(sy, "ve")
            st0 = C.semv["st"]
            for q in range(4):
                dinc(sy.dma_start(C.scr[0:1, q * 128:(q + 1) * 128],
                                  C.scr2[q:q + 1, 512:640]), "st")
                dinc(sy.dma_start(C.scr[0:1, 512 + q * 128:640 + q * 128],
                                  C.scr2[4 + q:5 + q, 512:640]), "st")
            wait(sy, "st")
            sy.drain().then_inc(C.sems["sy"], 1)
            C.semv["sy"] += 1
        block(tensor=bb_pe)
        block(vector=bb_v)
        block(tensor=bb_pe2)
        block(vector=bb_v2)
        block(sync=bb_sy)

        def dd_ls(sy, tb=tb, t0=t0):
            C._f5d = {k_: C.semv[k_] for k_ in
                      ("ld2", "lds", "pe", "ve", "sc", "gp")}
            b5 = C._f5d
            sy.wait_ge(C.sems["sy"], C.semv["sy"])
            inc(sy.dma_start(xlnld[:, :].rearrange("p (ic t) -> p ic t", t=512),
                             C.xln_h[:, :, t0:t0 + 512].rearrange(
                                 "ic p t -> p ic t")), "lds", 16)
            for j, (oc, bi) in enumerate(((0, 0), (1, 1), (2, 2), (3, 3),
                                          (4, 4), (5, 0), (6, 1), (7, 2))):
                sy.wait_ge(C.sems["lds"], b5["lds"] + 16 + 16 * j)
                if oc >= 5:
                    sy.wait_ge(C.sems["pe"], b5["pe"] + 3 + bi)
                inc(sy.dma_start(wdst[bi][:, :].rearrange(
                    "p (ic c) -> p ic c", c=128),
                    C.wdT_h[oc, :, :].rearrange("(ic p) c -> p ic c", p=128)),
                    "lds", 16)

        def cc_pe(t, tb=tb):
            b5 = C._f5d
            t.matmul(pb[4][:, :], C.onerow_f[0:1, :], C.scr[0:1, 0:512],
                     start=True, stop=True,
                     skip_group_check=True).then_inc(C.sems["pe"], 1)
            t.matmul(pb[5][:, :], C.onerow_f[0:1, :], C.scr[0:1, 512:1024],
                     start=True, stop=True,
                     skip_group_check=True).then_inc(C.sems["pe"], 1)
            C.semv["pe"] += 2
            for p_ in range(2):
                banks = [pb[4 * p_ + k] for k in range(4)]
                bufs = [0, 1, 2, 3] if p_ == 0 else [4, 0, 1, 2]
                t.wait_ge(C.sems["ve"], b5["ve"] + 1)
                t.wait_ge(C.sems["lds"], b5["lds"] + 16 + 16 * 4 * (p_ + 1))
                if p_ == 1:
                    t.wait_ge(C.sems["lds"], b5["lds"] + 16 + 16 * 8)
                for icn in range(32):
                    if p_ == 0:
                        t.wait_ge(C.sems["ve"], b5["ve"] + 1 + 2 * icn + 2)
                    for k in range(4):
                        mm = t.matmul(banks[k][:, :],
                                      wdst[bufs[k]][:, icn * 128:(icn + 1) * 128],
                                      hqb[:, icn * 512:(icn + 1) * 512],
                                      start=(icn == 0), stop=(icn == 31))
                        if icn == 31:
                            mm.then_inc(C.sems["pe"], 1)
                C.semv["pe"] += 4

        def cc_v(v, tb=tb):
            b5 = C._f5d
            v.wait_ge(C.sems["pe"], b5["pe"] + 1)
            v.tensor_copy(sbc[:, :], pb[4][:, :])
            v.wait_ge(C.sems["pe"], b5["pe"] + 2)
            v.tensor_copy(dbc[:, :], pb[5][:, :])
            v.drain().then_inc(C.sems["ve"], 1)
            C.semv["ve"] += 1
            for icn in range(32):
                sl = slice(icn * 512, (icn + 1) * 512)
                v.tensor_tensor(C.scr2[:, 0:512], hblk[:, sl], sbc[:, :],
                                Op.mult)
                v.drain()
                v.tensor_scalar(C.scr2[:, 0:512], C.scr2[:, 0:512],
                                C.disc[:, 2:3], None, Op.add)
                v.drain().then_inc(C.sems["ve"], 1)
                C.semv["ve"] += 1
                v.wait_ge(C.sems["gp"], b5["gp"] + icn + 1)
                v.tensor_copy(hqb[:, sl], C.scri[:, 512:1024])
                v.drain().then_inc(C.sems["ve"], 1)
                C.semv["ve"] += 1
            for oc in range(8):
                slo = slice(oc * 512, (oc + 1) * 512)
                p_, k = divmod(oc, 4)
                v.wait_ge(C.sems["pe"], b5["pe"] + 2 + 4 * p_ + k + 1)
                v.scalar_tensor_tensor(C.scr2[:, 0:512], pb[4 * p_ + k][:, :],
                                       C.rsg[:, 64 + oc:65 + oc], dbc[:, :],
                                       Op.add, Op.mult)
                v.drain()
                v.tensor_tensor(h2f[:, slo], xlnld[:, oc * 512:(oc + 1) * 512],
                                C.scr2[:, 0:512], Op.add)
                v.drain()
                v.tensor_copy(h2b[:, slo], h2f[:, slo])
                v.drain().then_inc(C.sems["ve"], 1)
                C.semv["ve"] += 1

        def cc_g(g, tb=tb):
            b5 = C._f5d
            for icn in range(32):
                g.wait_ge(C.sems["ve"], b5["ve"] + 1 + 2 * icn + 1)
                g.tensor_copy(C.scri[:, 512:1024], C.scr2[:, 0:512])
                g.drain().then_inc(C.sems["gp"], 1)
                C.semv["gp"] += 1
        def dd_s(s, tb=tb):
            b5 = C._f5d
            for oc in range(8):
                s.wait_ge(C.sems["ve"], b5["ve"] + 66 + oc)
                s.activation(sq2[:, oc * 512:(oc + 1) * 512],
                             h2f[:, oc * 512:(oc + 1) * 512], AF.Square,
                             bias=ccol(0.0), scale=1.0)
                s.drain().then_inc(C.sems["sc"], 1)
                C.semv["sc"] += 1
        block(sync=dd_ls, tensor=cc_pe, gpsimd=cc_g, vector=cc_v,
              scalar=dd_s)

        def ee_pe(t, tb=tb):
            for oc in range(8):
                m0 = t.matmul(pb[2][0:1, :], C.ones_bf[:, 0:1],
                              h2b[:, oc * 512:(oc + 1) * 512],
                              start=(oc == 0), stop=(oc == 7),
                              skip_group_check=True)
                m1 = t.matmul(pb[3][0:1, :], C.ones_bf[:, 0:1],
                              sq2[:, oc * 512:(oc + 1) * 512],
                              start=(oc == 0), stop=(oc == 7),
                              skip_group_check=True)
            m0.then_inc(C.sems["pe"], 1)
            m1.then_inc(C.sems["pe"], 1)
            C.semv["pe"] += 2

        def ee_v(v, tb=tb):
            wait(v, "pe")
            v.tensor_scalar(C.scr[0:1, 0:512], pb[2][0:1, :], 1.0 / E, None,
                            Op.mult)
            v.tensor_scalar(C.scr[0:1, 512:1024], pb[3][0:1, :], 1.0 / E, None,
                            Op.mult)
            v.drain()
            v.tensor_tensor(C.scr[0:1, 1024:1536], C.scr[0:1, 0:512],
                            C.scr[0:1, 0:512], Op.mult)
            v.drain()
            v.tensor_sub(C.scr[0:1, 1024:1536], C.scr[0:1, 512:1024],
                         C.scr[0:1, 1024:1536])
            v.drain().then_inc(C.sems["ve"], 1)
            C.semv["ve"] += 1

        def ee_s(s, tb=tb):
            wait(s, "ve")
            s.activation(C.scr[0:1, 1536:2048], C.scr[0:1, 1024:1536], AF.Sqrt,
                         bias=ccol(EPS_LN)[0:1, :], scale=1.0)
            s.drain().then_inc(C.sems["sc"], 1)
            C.semv["sc"] += 1

        def ee_v2(v, tb=tb):
            wait(v, "sc")
            v.reciprocal(C.scr[0:1, 1024:1536], C.scr[0:1, 1536:2048])
            v.drain()
            v.tensor_tensor(C.scr[0:1, 1536:2048], C.scr[0:1, 0:512],
                            C.scr[0:1, 1024:1536], Op.mult)
            v.drain().then_inc(C.sems["ve"], 1)
            C.semv["ve"] += 1

        def ee_pe2(t, tb=tb):
            wait(t, "ve")
            t.matmul(pb[2][:, :], C.onerow_f[0:1, :], C.scr[0:1, 1024:1536],
                     start=True, stop=True,
                     skip_group_check=True).then_inc(C.sems["pe"], 1)
            t.matmul(pb[3][:, :], C.onerow_f[0:1, :], C.scr[0:1, 1536:2048],
                     start=True, stop=True,
                     skip_group_check=True).then_inc(C.sems["pe"], 1)
            C.semv["pe"] += 2

        def ee_v3(v, tb=tb):
            wait(v, "pe")
            for oc in range(8):
                sl = slice(oc * 512, (oc + 1) * 512)
                v.tensor_tensor(outf[:, sl], h2f[:, sl], pb[2][:, :], Op.mult)
                v.drain()
                v.tensor_sub(outf[:, sl], outf[:, sl], pb[3][:, :])
                v.drain()
            v.drain().then_inc(C.sems["ve"], 1)
            C.semv["ve"] += 1
        block(tensor=ee_pe)
        block(vector=ee_v)
        block(scalar=ee_s)
        block(vector=ee_v2)
        block(tensor=ee_pe2)
        block(vector=ee_v3)

        def ff_pe(t, tb=tb):
            wait(t, "ve")
            C._f5f = C.semv["ve"]
            C._f5st = C.semv["st"]
            for q in range(4):
                bank = pb[4] if q % 2 == 0 else pb[5]
                if q >= 2:
                    t.wait_ge(C.sems["ve"], C._f5f + q - 1)
                for oc in range(4):
                    t.transpose(bank[:, oc * 128:(oc + 1) * 128],
                                outf[:, oc * 512 + q * 128:
                                     oc * 512 + q * 128 + 128],
                                C.idn_f[:, :]).then_inc(C.sems["pe"], 1)
                C.semv["pe"] += 4
                bank2 = pb[6] if q % 2 == 0 else pb[7]
                for oc in range(4):
                    t.transpose(bank2[:, oc * 128:(oc + 1) * 128],
                                outf[:, (4 + oc) * 512 + q * 128:
                                     (4 + oc) * 512 + q * 128 + 128],
                                C.idn_f[:, :]).then_inc(C.sems["pe"], 1)
                C.semv["pe"] += 4

        def ff_v(v, tb=tb):
            base_pe = C.semv["pe"] - 32
            for q in range(4):
                bank = pb[4] if q % 2 == 0 else pb[5]
                bank2 = pb[6] if q % 2 == 0 else pb[7]
                v.wait_ge(C.sems["pe"], base_pe + 8 * q + 8)
                if q >= 1:
                    v.wait_ge(C.sems["st"], C._f5st + 16 * q)
                for oc in range(4):
                    v.tensor_copy(hblk[:, 8192 + oc * 128:8192 + (oc + 1) * 128],
                                  bank[:, oc * 128:(oc + 1) * 128])
                    v.tensor_copy(hblk[:, 8192 + (4 + oc) * 128:
                                       8192 + (5 + oc) * 128],
                                  bank2[:, oc * 128:(oc + 1) * 128])
                v.drain().then_inc(C.sems["ve"], 1)
                C.semv["ve"] += 1

        def ff_sy(sy, tb=tb, t0=t0):
            base_ve = C.semv["ve"] - 4
            for q in range(4):
                sy.wait_ge(C.sems["ve"], base_ve + q + 1)
                if q >= 1:
                    sy.wait_ge(C.sems["st"], C._f5st + 16 * q)
                tok = t0 + q * 128
                b_ = tok // S
                s_ = tok % S
                dinc(sy.dma_start(C.out_h[b_, s_:s_ + 128, :],
                                  hblk[:, 8192:8192 + E]), "st")
            wait(sy, "st")
        block(tensor=ff_pe, vector=ff_v, sync=ff_sy)
        C.tap(f"outt{tb}", lambda: hblk[:, 8192:8192 + E], [128, E])
    scope.close()


_BUILT = {}


def _get_built():
    if "C" not in _BUILT:
        _BUILT["C"] = build(tap_req=_TAP_REQ)
    return _BUILT["C"]


def kernel(L0_attn_x, L0_attn_Wq, L0_attn_Wk, L0_attn_Wv, L0_attn_Wr,
           L0_attn_Wo, L0_ffn_w_gate, L0_ffn_w_up, L0_ffn_w_down):
    C = _get_built()
    x = np.ascontiguousarray(np.asarray(L0_attn_x, dtype=np.float32))
    shards = [x[2 * i:2 * i + 2] for i in range(8)]
    base = {
        "Wq": np.ascontiguousarray(np.asarray(L0_attn_Wq, np.float32)),
        "Wk": np.ascontiguousarray(np.asarray(L0_attn_Wk, np.float32)),
        "Wv": np.ascontiguousarray(np.asarray(L0_attn_Wv, np.float32)),
        "Wr": np.ascontiguousarray(np.asarray(L0_attn_Wr, np.float32)),
        "Wo": np.ascontiguousarray(np.asarray(L0_attn_Wo, np.float32)),
        "w_gate": np.ascontiguousarray(np.asarray(L0_ffn_w_gate, np.float32)),
        "w_up": np.ascontiguousarray(np.asarray(L0_ffn_w_up, np.float32)),
        "w_down": np.ascontiguousarray(np.asarray(L0_ffn_w_down, np.float32)),
    }
    in_maps = [dict(base, x=s_) for s_ in shards]
    res = run_bass_kernel_spmd(C.nc, in_maps, core_ids=list(range(8)))
    out = np.concatenate([np.asarray(r["out"], np.float32)
                          for r in res.results], axis=0)
    return out
